# version 27
# baseline (speedup 1.0000x reference)
import numpy as np
import ml_dtypes
import concourse.bass as bass
import concourse.mybir as mybir
from concourse.bass_utils import run_bass_kernel_spmd

F32 = mybir.dt.float32
BF16 = mybir.dt.bfloat16
ALU = mybir.AluOpType
ACT = mybir.ActivationFunctionType
AXX = mybir.AxisListType.X
BF = ml_dtypes.bfloat16

D = 2048
SEQ = 2048
BATCH = 4
H = 16
HD = 128
KC = 16
II = 5632
ITN = 44
NTO = 8
STN = 16
GH = 2
NG = H // GH
EPS = 1e-5
CSC = 1.0 / float(np.sqrt(128.0))

_CACHE = {}


class Sub:
    __slots__ = ("eng", "sem", "val", "dma", "name")

    def __init__(self, eng, dma, name):
        self.eng = eng
        self.sem = None
        self.val = 0
        self.dma = dma
        self.name = name


class Ch:

    def __init__(self, eng, dma, nlanes, name):
        self.dma = dma
        self.lanes = [Sub(eng, dma, f"{name}{i}") for i in range(nlanes)]
        self.rr = 0

    def issue(self, amt):
        if self.dma:
            s = self.lanes[self.rr]
            self.rr = (self.rr + 1) % len(self.lanes)
        else:
            s = self.lanes[0]
        s.val += amt
        return s, (s, s.val)


class KB:

    ENGS = ("T", "V", "S", "Y", "G")

    def __init__(self):
        self.ch = {
            "T": Ch("T", False, 1, "cT"), "V": Ch("V", False, 1, "cV"),
            "S": Ch("S", False, 1, "cS"), "Y": Ch("Y", True, 8, "cY"),
            "G": Ch("G", False, 1, "cG"), "GD": Ch("G", True, 4, "cD"),
        }
        self.ops = {e: [] for e in self.ENGS}
        self.waited = {}
        self.pending = {e: [] for e in self.ENGS}

    def all_subs(self):
        for c in self.ch.values():
            yield from c.lanes

    def do(self, eng, fn, waits=(), inc=0, chan=None):
        extra = []
        tok_out = None
        sub = None
        if inc:
            ch = self.ch[chan or eng]
            if ch.dma:
                lane = ch.lanes[ch.rr]
                if lane.val > 0:
                    extra.append((lane, lane.val))
            sub, tok_out = ch.issue(inc)
        wl = []
        for tok in list(waits) + self.pending[eng] + extra:
            if tok is None:
                continue
            c, v = tok
            if c.eng == eng and not c.dma:
                continue
            key = (eng, id(c))
            if self.waited.get(key, 0) >= v:
                continue
            self.waited[key] = v
            wl.append((c, v))
        self.pending[eng] = []
        self.ops[eng].append((wl, fn, sub, inc))
        return tok_out

    def fence(self, engs=None):
        snap = [(s, s.val) for s in self.all_subs() if s.val > 0]
        for e in engs or self.ENGS:
            self.pending[e].extend(snap)

    def replay(self, eng, engobj):
        first = True
        for wl, fn, sub, amt in self.ops[eng]:
            if eng == "V" and not first:
                engobj.drain()
            first = False
            for c, v in wl:
                engobj.wait_ge(c.sem, v)
            ins = fn(engobj)
            if sub is not None:
                ins.then_inc(sub.sem, amt)


def build_program(dbg=False):
    nc = bass.Bass("TRN2", target_bir_lowering=False, debug=False,
               detect_race_conditions=False)
    kb = KB()

    dt = nc.dram_tensor
    x_l = dt("x_l", [STN, 128, D], F32, kind="ExternalInput").ap()
    wq_d = dt("wq_d", [H, 128, KC, 128], F32, kind="ExternalInput").ap()
    wk_d = dt("wk_d", [H, 128, KC, 128], F32, kind="ExternalInput").ap()
    wv_d = dt("wv_d", [NG, 128, KC, GH * 128], F32, kind="ExternalInput").ap()
    wo_d = dt("wo_d", [KC, 128, D], F32, kind="ExternalInput").ap()
    wg_d = dt("wg_d", [ITN, 128, KC, 128], F32, kind="ExternalInput").ap()
    wu_d = dt("wu_d", [ITN, 128, KC, 128], F32, kind="ExternalInput").ap()
    wd_d = dt("wd_d", [4, ITN, 128, 512], F32, kind="ExternalInput").ap()
    gwt_d = dt("gwt_d", [128, 128], F32, kind="ExternalInput").ap()
    gb_d = dt("gb_d", [128, 1], F32, kind="ExternalInput").ap()
    gs_d = dt("gs_d", [128, ITN], F32, kind="ExternalInput").ap()
    us_d = dt("us_d", [128, ITN], F32, kind="ExternalInput").ap()
    ds_d = dt("ds_d", [128, D], F32, kind="ExternalInput").ap()
    tri_d = dt("tri_d", [128, 128], BF16, kind="ExternalInput").ap()
    oth_d = dt("oth_d", [128, 128], BF16, kind="ExternalInput").ap()
    trin_d = dt("trin_d", [128, 128], BF16, kind="ExternalInput").ap()
    othn_d = dt("othn_d", [128, 128], BF16, kind="ExternalInput").ap()
    one_d = dt("one_d", [128, 1], BF16, kind="ExternalInput").ap()
    idn_d = dt("idn_d", [128, 128], BF16, kind="ExternalInput").ap()
    eps_d = dt("eps_d", [128, 1], F32, kind="ExternalInput").ap()
    onr_d = dt("onr_d", [1, 128], F32, kind="ExternalInput").ap()
    onrb_d = dt("onrb_d", [1, 128], BF16, kind="ExternalInput").ap()
    y_d = dt("y_d", [NTO, 128, D], F32, kind="ExternalOutput").ap()
    if dbg:
        dbg_ao = dt("dbg_ao", [128, D], F32, kind="ExternalOutput").ap()
        dbg_ln = dt("dbg_ln", [128, D], F32, kind="ExternalOutput").ap()
        dbg_mv = dt("dbg_mv", [128, 3], F32, kind="ExternalOutput").ap()
        dbg_st = dt("dbg_st", [128, 24], F32, kind="ExternalOutput").ap()
        dbg_xo = dt("dbg_xo", [128, D], F32, kind="ExternalOutput").ap()
        dbg_h1T = dt("dbg_h1T", [128, KC, SEQ], BF16, kind="ExternalOutput").ap()
        dbg_kT = dt("dbg_kT", [128, GH, SEQ], BF16, kind="ExternalOutput").ap()
        dbg_qT = dt("dbg_qT", [128, GH, 1024], BF16, kind="ExternalOutput").ap()
        dbg_vg = dt("dbg_vg", [128, STN, GH * 128], BF16, kind="ExternalOutput").ap()
        dbg_gat = dt("dbg_gat", [128, KC, 1024], BF16, kind="ExternalOutput").ap()
        dbg_Bb = dt("dbg_Bb", [1, 1024], BF16, kind="ExternalOutput").ap()
        dbg_p = dt("dbg_p", [128, 1024], BF16, kind="ExternalOutput").ap()
    x2_d = dt("x2_d", [NTO, 128, D], F32,
              kind="ExternalOutput" if dbg else "Internal").ap()

    PS = [nc.alloc_psum_tensor(f"ps{i}", [128, 1024], F32) for i in range(4)]

    base = 16512

    def sb(name, shape, dty, at):
        return nc.alloc_sbuf_tensor_at(name, shape, dty, offset=base + at)

    S1, S2, S3, S4 = 0, 65536, 155648, 188416
    KS = 204800
    h1T = sb("h1T", [128, KC, SEQ], BF16, S1)
    wo_b = sb("wo_b", [128, KC, D], BF16, S1)
    sg_b = [sb(f"sg{i}", [128, 1024], BF16, S1 + i * 2048) for i in range(2)]
    uu_b = [sb(f"uu{i}", [128, 1024], BF16, S1 + 4096 + i * 2048) for i in range(2)]
    wgb = [sb(f"wgb{i}", [128, KC, 128], BF16, S1 + 8192 + i * 4096) for i in range(2)]
    wub = [sb(f"wub{i}", [128, KC, 128], BF16, S1 + 16384 + i * 4096) for i in range(2)]
    ffo = sb("ffo", [128, NTO, D], F32, S1)
    xst = [sb(f"xst{i}", [128, D], F32,
              (S3 + i * 8192) if i < 2 else (S3 + 24576)) for i in range(3)]
    h1t = [sb(f"h1t{i}", [128, D], BF16, S3 + 16384 + i * 4096) for i in range(2)]
    kT = sb("kT", [128, GH, SEQ], BF16, S2 + 0)
    vg = sb("vg", [128, STN, GH * 128], BF16, S2 + 8192)
    qT = sb("qT", [128, GH, 1024], BF16, S2 + 16384)
    wq_s = [sb(f"wq_s{i}", [128, GH, KC, 128], BF16, S2 + 20480 + 8192 * i)
            for i in range(2)]
    wk_s = [sb(f"wk_s{i}", [128, GH, KC, 128], BF16, S2 + 36864 + 8192 * i)
            for i in range(2)]
    wvb = [sb(f"wvb{i}", [128, KC, GH * 128], BF16, S2 + 53248 + 8192 * i)
           for i in range(2)]
    p_b = [sb(f"p{i}", [128, 1024], BF16, S2 + 69632 + i * 2048) for i in range(4)]
    sigT = sb("sigT", [128, 1024], BF16, S2 + 77824)
    t1 = sb("t1", [128, 1024], F32, S2 + 79872)
    Brow2 = [sb(f"Brow{i}", [1, 1024], BF16, S2 + 83968 + 2048 * i)
             for i in range(2)]
    rcb = sb("rcb", [128, 1024], BF16, S2 + 88064)
    rrow = sb("rrow", [1, 1024], F32, S2 + 79872)
    ln2o = sb("ln2o", [128, D], F32, S2 + 0)
    h2t = [sb(f"h2t{i}", [128, D], BF16, S2 + 8192 + i * 4096) for i in range(2)]
    xow = [sb(f"xow{i}", [128, D], F32, S2 + 16384 + i * 8192) for i in range(2)]
    x2t = [sb(f"x2t{i}", [128, D], F32, S2 + 32768 + i * 8192) for i in range(2)]
    h2T = sb("h2T", [128, KC, 1024], BF16, S2 + 57344)
    hdnb = sb("hdnb", [128, ITN - 16, 1024], BF16, S2 + 0)
    ds_b = sb("ds_b", [128, D], F32, S2 + 57344)
    ln4o = sb("ln4o", [128, D], F32, S2 + 0)
    x2r = [sb(f"x2r{i}", [128, D], F32, S2 + 16384 + i * 8192) for i in range(2)]
    ost = [sb(f"ost{i}", [128, D], F32, S2 + 32768 + i * 8192) for i in range(2)]
    gat = sb("gat", [128, KC, 1024], BF16, S3)
    hdna = sb("hdna", [128, 16, 1024], BF16, S3)
    wst = [sb(f"wst{i}", [128, D], F32, S4 + i * 8192) for i in range(2)]
    dwst = [sb(f"dwst{i}", [128, 512], F32, S4 + i * 2048) for i in range(3)]
    dwb = [sb(f"dwb{i}", [128, 512], BF16, S4 + 6144 + i * 1024) for i in range(2)]
    ko = KS
    tri = sb("tri", [128, 128], BF16, ko); ko += 256
    trin = sb("trin", [128, 128], BF16, ko); ko += 256
    othn = sb("othn", [128, 128], BF16, ko); ko += 256
    oth = sb("oth", [128, 128], BF16, ko); ko += 256
    idn = sb("idn", [128, 128], BF16, ko); ko += 256
    onec = sb("onec", [128, 1], BF16, ko); ko += 32
    gwt = sb("gwt", [128, 128], F32, ko); ko += 512
    gwb = sb("gwb", [128, 128], BF16, ko); ko += 256
    gbv = sb("gbv", [128, 1], F32, ko); ko += 32
    gbh = sb("gbh", [128, 1], F32, ko); ko += 32
    gsv = sb("gsv", [128, ITN], F32, ko); ko += 192
    usv = sb("usv", [128, ITN], F32, ko); ko += 192
    st24 = sb("st24", [128, 48], F32, ko); ko += 192
    mvp = sb("mvp", [128, 2, 2], F32, ko); ko += 32
    sqvp = sb("sqvp", [128, 2], F32, ko); ko += 32
    rstdp = sb("rstdp", [128, 2], F32, ko); ko += 32
    ngmp = sb("ngmp", [128, 2], F32, ko); ko += 32
    mv2 = sb("mv2", [128, 2], F32, ko); ko += 32
    sqv = sb("sqv", [128, 1], F32, ko); ko += 32
    rstd = sb("rstd", [128, 1], F32, ko); ko += 32
    ngm = sb("ngm", [128, 1], F32, ko); ko += 32
    km = sb("km", [128, 1], BF16, ko); ko += 32
    kmb = sb("kmb", [128, 1], BF16, ko); ko += 32
    epsv = sb("epsv", [128, 1], F32, ko); ko += 32
    mv2d = sb("mv2d", [128, 4], F32, ko); ko += 64
    ko = (ko + 31) // 32 * 32
    sgt = sb("sgt", [128, 1024], F32, ko)
    absq = sb("absq", [128, 1024], BF16, ko)
    ko += 4096
    onr = sb("onr", [1, 128], F32, ko); ko += 512
    onrb = sb("onrb", [1, 128], BF16, ko); ko += 256
    assert base + ko <= 229344

    for s in kb.all_subs():
        s.sem = nc.alloc_semaphore(name=s.name)

    def ln_stats(src_aps, tok_ready):
        tread = None
        i = 0
        for ap in src_aps:
            w = ap.shape[-1]
            for j in range(w // 512):
                tread = kb.do(
                    "V",
                    (lambda a=ap, jj=j, ii=i: lambda e: e.bn_stats(
                        st24[:, 6 * ii:6 * ii + 6],
                        a[:, 512 * jj:512 * jj + 512]))(),
                    waits=list(tok_ready), inc=1)
                i += 1
        assert i == 4
        tag = kb.do("V", lambda e: e.bn_aggr(mv2[:, :], st24[:, 0:24]),
                    inc=1)
        ts_ = kb.do("S", lambda e: e.activation(sqv[:, :], mv2[:, 1:2], ACT.Sqrt,
                                                bias=epsv[:, 0:1]),
                    waits=[tag, tok_eps], inc=1)
        tn = kb.do("V", lambda e: e.reciprocal(rstd[:, :], sqv[:, :]),
                   waits=[ts_], inc=1)
        return tread, tn

    def ln_apply(dst, src, tok_n, waits=()):
        return kb.do("V", lambda e: e.tensor_scalar(
            dst, src, mv2[:, 0:1], rstd[:, 0:1], ALU.subtract, ALU.mult),
            waits=[tok_n] + list(waits), inc=1)

    tok_idn = kb.do("Y", lambda e: e.dma_start(idn[:, :], idn_d), inc=16)
    tok_gw = kb.do("Y", lambda e: e.dma_start(gwt[:, :], gwt_d), inc=16)
    tok_gb = kb.do("Y", lambda e: e.dma_start(gbv[:, :], gb_d), inc=16)
    kb.do("Y", lambda e: e.dma_start(tri[:, :], tri_d), inc=16)
    kb.do("Y", lambda e: e.dma_start(oth[:, :], oth_d), inc=16)
    kb.do("Y", lambda e: e.dma_start(trin[:, :], trin_d), inc=16)
    kb.do("Y", lambda e: e.dma_start(othn[:, :], othn_d), inc=16)
    kb.do("Y", lambda e: e.dma_start(onec[:, :], one_d), inc=16)
    tok_eps = kb.do("Y", lambda e: e.dma_start(epsv[:, :], eps_d), inc=16)
    kb.do("Y", lambda e: e.dma_start(onr[:, :], onr_d), inc=16)
    kb.do("Y", lambda e: e.dma_start(onrb[:, :], onrb_d), inc=16)
    tok_gwb = kb.do("V", lambda e: e.tensor_copy(gwb[:, :], gwt[:, :]),
                    waits=[tok_gw], inc=1)
    kb.do("V", lambda e: e.tensor_scalar(gbh[:, :], gbv[:, :], 0.5, None,
                                         ALU.mult), waits=[tok_gb], inc=0)

    wst_free = [None, None]
    wq_free = [None, None]
    wk_free = [None, None]
    wvb_free = [None, None]

    wcast = {}

    def load_group_w(g):
        if g >= NG:
            return
        p2 = g % 2
        toks = []
        for hl in range(GH):
            h = g * GH + hl
            bx = hl % 2
            tq = kb.do("Y", (lambda hh=h, b=bx: lambda e: e.dma_start(
                wst[b][:, :], wq_d[hh]))(), waits=[wst_free[bx]], inc=16)
            cq = kb.do("S", (lambda b=bx, p=p2, hh=hl: lambda e: e.copy(
                wq_s[p][:, hh, :, :], wst[b][:, :]))(),
                waits=[tq, wq_free[p2]], inc=1)
            wst_free[bx] = cq
            tk = kb.do("Y", (lambda hh=h, b=bx: lambda e: e.dma_start(
                wst[b][:, :], wk_d[hh]))(), waits=[wst_free[bx]], inc=16)
            ck = kb.do("S", (lambda b=bx, p=p2, hh=hl: lambda e: e.copy(
                wk_s[p][:, hh, :, :], wst[b][:, :]))(),
                waits=[tk, wk_free[p2]], inc=1)
            wst_free[bx] = ck
            toks.append((cq, ck))
        cv = None
        for half in range(2):
            bx = half
            tv = kb.do("Y", (lambda gg=g, hf=half, b=bx: lambda e: e.dma_start(
                wst[b][:, :], wv_d[gg, :, 8 * hf:8 * hf + 8, :]))(),
                waits=[wst_free[bx]], inc=16)
            cv = kb.do("S", (lambda p=p2, hf=half, b=bx: lambda e: e.copy(
                wvb[p][:, 8 * hf:8 * hf + 8, :], wst[b][:, :]))(),
                waits=[tv, wvb_free[p2]], inc=1)
            wst_free[bx] = cv
        wcast[g] = (toks, cv)

    load_group_w(0)

    tr_evac = [None, None]
    xst_free = [None, None, None]
    h1T_done = []
    h1t_free = [None, None]
    pend_ev = None
    p1_rc = None
    p1_ap = None
    sq_prev = [None, None]

    pend_evs = []

    def flush_ev():
        pp, pi, ptk = pend_evs.pop(0)
        ev = kb.do("V", (lambda p=pp, i=pi: lambda e: e.tensor_copy(
            h1T[:, :, i * 128:(i + 1) * 128], p[:, :].bitcast(BF16)))(),
            waits=[ptk], inc=1)
        tr_evac[pi % 2] = ev
        h1T_done.append(ev)
        return ev

    for pr in range(STN // 2):
        tags = [None, None]
        for k2 in range(2):
            idx = 2 * pr + k2
            b3 = idx % 3
            tok_x = kb.do("Y", (lambda i=idx, b=b3: lambda e: e.dma_start(
                xst[b][:, :], x_l[i]))(), waits=[xst_free[b3]], inc=16)
            for j in range(4):
                kb.do("V", (lambda b=b3, c=k2, jj=j: lambda e: e.bn_stats(
                    st24[:, 24 * c + 6 * jj:24 * c + 6 * jj + 6],
                    xst[b][:, 512 * jj:512 * jj + 512]))(),
                    waits=[tok_x], inc=0)
            tags[k2] = kb.do("V", (lambda c=k2: lambda e: e.bn_aggr(
                mvp[:, c, :], st24[:, 24 * c:24 * c + 24]))(),
                waits=[sq_prev[k2]], inc=1)
            if pend_evs:
                flush_ev()
        tsq = None
        for k2 in range(2):
            tsq = kb.do("S", (lambda c=k2: lambda e: e.activation(
                sqvp[:, c:c + 1], mvp[:, c, 1:2], ACT.Sqrt,
                bias=epsv[:, 0:1]))(),
                waits=[tags[k2], tok_eps, p1_rc], inc=1)
            sq_prev[k2] = tsq
        p1_rc = kb.do("V", lambda e: e.reciprocal(
            rstdp[:, :], sqvp[:, :]), waits=[tsq, p1_ap], inc=1)
        tnm = None
        for k2 in range(2):
            tnm = kb.do("V", (lambda c=k2: lambda e: e.scalar_tensor_tensor(
                ngmp[:, c:c + 1], mvp[:, c, 0:1], -1.0,
                rstdp[:, c:c + 1], ALU.mult, ALU.mult))(),
                waits=[p1_rc, p1_ap], inc=1)
        for k2 in range(2):
            idx = 2 * pr + k2
            b3 = idx % 3
            ta = kb.do("S", (lambda b=b3, h2=k2, c=k2: lambda e: e.activation(
                h1t[h2][:, :], xst[b][:, :], ACT.Identity,
                scale=rstdp[:, c:c + 1], bias=ngmp[:, c:c + 1]))(),
                waits=[tnm, h1t_free[k2]], inc=1)
            p1_ap = ta
            xst_free[b3] = ta
            pt = PS[2 + idx % 2]
            ttok = None
            for kc in range(KC):
                ttok = kb.do("T", (lambda p=pt, b=k2, k=kc: lambda e:
                             e.transpose(
                    p[:, 64 * k:64 * k + 64].bitcast(BF16),
                    h1t[b][:, k * 128:(k + 1) * 128], idn[:, :]))(),
                    waits=[ta, tr_evac[idx % 2], tok_idn],
                    inc=(1 if kc == KC - 1 else 0))
            h1t_free[k2] = ttok
            pend_evs.append((pt, idx, ttok))
    flush_ev()
    h1T_ready = flush_ev()
    if dbg:
        kb.do("Y", lambda e: e.dma_start(dbg_h1T, h1T[:, :, :]),
              waits=[h1T_ready], inc=16)

    kb.fence()

    kT_free = [None] * GH
    qT_free = [None] * GH
    vg_free = None
    gat_done = [None] * H
    absq_free = None
    Brow_free = None
    Bb_free = []
    kmb_free = None
    sigT_free = None
    sgt_free = None
    rrow_free = None
    rcb_free = None
    h1T_last_read = None
    t_last = {0: None, 1: None, 2: None, 3: None}
    tok_avl = None
    pend_epi = None
    pend_b = None
    tBr2 = [None, None]
    eqh = [None, None]
    tkm2 = [None, None]
    tabs2 = [None, None]
    kmh = [km, kmb]
    kmb_free2 = [None, None]

    for g in range(NG):
        p2 = g % 2
        wq_cast = [wcast[g][0][hl][0] for hl in range(GH)]
        wk_cast = [wcast[g][0][hl][1] for hl in range(GH)]
        vcast = wcast[g][1]
        if pend_epi is not None:
            epi_a_prev, epi_b_prev = pend_epi
            pend_bp = (lambda f=epi_b_prev, a=epi_a_prev(): lambda: f(ab=a))()
            pend_epi = None
        else:
            pend_bp = None
        for hl in range(GH):
            h = g * GH + hl
            bx = hl % 2
            ktok = None
            for ch_ in range(4):
                pt = PS[0] if ch_ < 2 else PS[1]
                col = (ch_ % 2) * 512
                for kc in range(KC):
                    ktok = kb.do("T", (lambda p=pt, c=col, k=kc, cc=ch_,
                                 pp=p2, hh=hl: lambda e: e.matmul(
                        p[:, c:c + 512], wk_s[pp][:, hh, k, :],
                        h1T[:, k, cc * 512:cc * 512 + 512],
                        start=(k == 0), stop=(k == KC - 1)))(),
                        waits=[wk_cast[hl],
                               h1T_done[4 * ch_ + 3] if g == 0
                               else h1T_ready,
                               t_last[0 if ch_ < 2 else 1]],
                        inc=(1 if (kc == KC - 1 and ch_ == 3) else 0))
                if ch_ == 0 and hl == 0 and pend_bp is not None:
                    pend_bp()
                    pend_bp = None
            ek = None
            for half in range(2):
                ek = kb.do("V", (lambda hh=hl, hf=half: lambda e: e.tensor_copy(
                    kT[:, hh, hf * 1024:hf * 1024 + 1024], PS[hf][:, :]))(),
                    waits=[ktok, kT_free[hl]], inc=1)
            t_last[0] = ek
            t_last[1] = ek
            tkm2[hl] = kb.do("V", (lambda hh=hl: lambda e: e.reduce_max(
                kmh[hh][:, :], kT[:, hh, :], AXX,
                apply_absolute_value=True))(),
                waits=[kmb_free2[hl], ek], inc=1)
            qp = PS[2 + hl]
            qtok = None
            for ch_ in range(2):
                for kc in range(KC):
                    qtok = kb.do("T", (lambda p=qp, c=ch_ * 512, k=kc,
                                 pp=p2, hh=hl: lambda e: e.matmul(
                        p[:, c:c + 512], wq_s[pp][:, hh, k, :],
                        h1T[:, k, c:c + 512],
                        start=(k == 0), stop=(k == KC - 1)))(),
                        waits=[wq_cast[hl], t_last[2 + hl]],
                        inc=(1 if (kc == KC - 1 and ch_ == 1) else 0))
            eq = kb.do("V", (lambda hh=hl, p=qp: lambda e: e.tensor_copy(
                qT[:, hh, :], p[:, :]))(), waits=[qtok, qT_free[hl]], inc=1)
            t_last[2 + hl] = eq
            eqh[hl] = eq
            if hl == 0:
                tabs2[0] = kb.do("S", lambda e: e.activation(
                    absq[:, :], qT[:, 0, :], ACT.Abs, scale=1.01),
                    waits=[absq_free, sgt_free, eq], inc=1)
            wq_free[p2] = qtok
            wk_free[p2] = ktok
            h1T_last_read = qtok
        if dbg and g == 0:
            kb.do("Y", lambda e: e.dma_start(dbg_kT, kT[:, :, :]),
                  waits=[t_last[1], ek], inc=16)
            kb.do("Y", lambda e: e.dma_start(dbg_qT, qT[:, :, :]),
                  waits=[eq], inc=16)

        tBr2 = [None, None]
        for hl in range(GH):
            if tabs2[hl] is None:
                tabs2[hl] = kb.do("S", (lambda hh=hl: lambda e: e.activation(
                    absq[:, :], qT[:, hh, :], ACT.Abs, scale=1.01))(),
                    waits=[absq_free, sgt_free, eqh[hl]], inc=1)
            btok = None
            for ch_ in range(2):
                btok = kb.do("T", (lambda c=ch_ * 512, hh=hl: lambda e:
                             e.matmul(
                    PS[2 + hh][0:1, c:c + 512], kmh[hh][:, 0:1],
                    absq[:, c:c + 512],
                    start=True, stop=True))(),
                    waits=[tkm2[hl], tabs2[hl], t_last[2 + hl]],
                    inc=(1 if ch_ == 1 else 0))
            kmb_free2[hl] = btok
            absq_free = btok
            tBr2[hl] = kb.do("V", (lambda hh=hl: lambda e: e.tensor_scalar(
                Brow2[hh][:, :], PS[2 + hh][0:1, :], -1.0, None, ALU.mult))(),
                waits=[btok, Brow_free], inc=1)
            if dbg and g == 0 and hl == 0:
                kb.do("Y", (lambda hh=hl: lambda e: e.dma_start(
                    dbg_Bb, Brow2[hh][:, :]))(), waits=[tBr2[hl]], inc=16)
        t_last[2] = tBr2[0]
        t_last[3] = tBr2[1]
        tabs2 = [None, None]
        load_group_w(g + 1)

        vev = None
        for sl in range(STN):
            pt = PS[0]
            col = (sl % 2) * 512
            vtok = None
            for kc in range(KC):
                vtok = kb.do("T", (lambda p=pt, c=col, k=kc, s=sl, pp=p2:
                             lambda e: e.matmul(
                    p[:, c:c + GH * 128], h1T[:, k, s * 128:(s + 1) * 128],
                    wvb[pp][:, k, :], start=(k == 0), stop=(k == KC - 1)))(),
                    waits=[vcast, t_last[0], h1T_ready],
                    inc=(1 if kc == KC - 1 else 0))
            vev = kb.do("V", (lambda p=pt, c=col, s=sl: lambda e: e.tensor_copy(
                vg[:, s, :], p[:, c:c + GH * 128]))(),
                waits=[vtok, vg_free], inc=1)
            if sl % 2 == 1:
                t_last[0] = vev
        t_last[0] = vev
        wvb_free[p2] = vtok
        h1T_last_read = vtok
        if dbg and g == 0:
            kb.do("Y", lambda e: e.dma_start(dbg_vg, vg[:, :, :]),
                  waits=[vev], inc=16)

        for hl in range(GH):
            h = g * GH + hl
            Brow = Brow2[hl]
            tBr = tBr2[hl]
            p_free = [None, None, None, None]
            av_first = {0: True, 1: True}
            l_first = {0: True, 1: True}
            av_rem = {0: 8, 1: 16}
            l_rem = {0: 8, 1: 16}
            sT_free = [t_last[0], t_last[1]]
            tcount = 0
            pendq = []

            def emit_avl(hh):
                nonlocal tok_avl
                bxp, slp, chp, ptokp = pendq.pop(0)
                for (c0, c1) in chp:
                    bank = 0 if c0 < 512 else 1
                    av_rem[bank] -= 1
                    kb.do("T", (lambda b=bxp, s=slp, a=c0, z=c1,
                                st_=av_first[bank],
                                sp=(av_rem[bank] == 0): lambda e: e.matmul(
                        PS[2][:, a:z], vg[:, s, hh * 128:(hh + 1) * 128],
                        p_b[b][:, a:z], start=st_, stop=sp))(),
                        waits=[ptokp, t_last[2]], inc=0)
                    av_first[bank] = False
                for (c0, c1) in chp:
                    bank = 0 if c0 < 512 else 1
                    l_rem[bank] -= 1
                    tok_avl = kb.do("T", (lambda b=bxp, a=c0, z=c1,
                                    st_=l_first[bank],
                                    sp=(l_rem[bank] == 0): lambda e:
                                    e.matmul(
                        PS[3][0:1, a:z], onec[:, :], p_b[b][:, a:z],
                        start=st_, stop=sp))(),
                        waits=[t_last[3], tBr],
                        inc=(1 if (c0, c1) == chp[-1] else 0))
                    l_first[bank] = False
                p_free[bxp] = tok_avl

            for i in range(8):
                ts0 = i * 128
                chunks = ([(ts0, 512), (512, 1024)] if i < 4
                          else [(ts0, 1024)])
                for half in range(2):
                    sl = i + 8 * half
                    bx = tcount % 2
                    pbx = tcount % 4
                    tcount += 1
                    stok = None
                    for (c0, c1) in chunks:
                        kb.do("T", (lambda b=bx, a=c0, z=c1, br=Brow:
                              lambda e: e.matmul(
                            PS[b][:, a:z], onrb[0:1, :], br[0:1, a:z],
                            start=True, stop=False))(),
                            waits=[sT_free[bx], tBr], inc=0)
                    mk = trin if half == 0 else othn
                    kb.do("T", (lambda b=bx, a=ts0, m=mk: lambda e: e.matmul(
                        PS[b][:, a:a + 128], m[:, :], idn[:, :],
                        start=False, stop=False))(), inc=0)
                    for (c0, c1) in chunks:
                        stok = kb.do("T", (lambda b=bx, hh=hl, s=sl, a=c0,
                                     z=c1: lambda e: e.matmul(
                            PS[b][:, a:z], kT[:, hh, s * 128:(s + 1) * 128],
                            qT[:, hh, a:z], start=False, stop=True))(),
                            waits=[],
                            inc=(1 if (c0, c1) == chunks[-1] else 0))
                    if len(pendq) == 3:
                        emit_avl(hl)
                    ptok = kb.do("S", (lambda b=pbx, bb=bx, a=ts0: lambda e:
                                 e.activation(
                        p_b[b][:, a:1024], PS[bb][:, a:1024], ACT.Exp,
                        scale=CSC))(),
                        waits=[stok, p_free[pbx]], inc=1)
                    sT_free[bx] = ptok
                    if dbg and g == 0 and hl == 0 and i == 0 and half == 0:
                        ptok = kb.do("Y", (lambda b=pbx: lambda e:
                                     e.dma_start(
                            dbg_p, p_b[b][:, :]))(), waits=[ptok], inc=16)
                    pendq.append((pbx, sl, chunks, ptok))
                    if tcount == 1 and pend_b is not None:
                        pend_b()
                        pend_b = None
            while pendq:
                emit_avl(hl)
            Brow_free = tok_avl
            def epi_a(hl=hl, h=h, sT0=sT_free[0], sT1=sT_free[1],
                      tok_avl_=tok_avl):
                nonlocal rrow_free
                gtok = None
                for ch_ in range(2):
                    gtok = kb.do("T", (lambda hh=hl, c=ch_ * 512: lambda e:
                                 e.matmul(
                        PS[0][:, c:c + 512], gwb[:, :], qT[:, hh, c:c + 512],
                        start=True, stop=True))(),
                        waits=[tok_gwb, sT0, sT1],
                        inc=(1 if ch_ == 1 else 0))
                tsg1 = kb.do("S", lambda e: e.activation(
                    sgt[:, :], PS[0][:, :], ACT.Tanh, bias=gbh[:, 0:1],
                    scale=0.5), waits=[gtok, sgt_free], inc=1)
                trc = kb.do("V", lambda e: e.reciprocal(
                    rrow[:, :], PS[3][0:1, :]),
                    waits=[tok_avl_, rrow_free], inc=1)
                t_last[0] = tsg1
                t_last[1] = sT1
                qT_free[hl] = gtok
                return tsg1, trc

            def epi_b(hl=hl, h=h, tok_avl_=tok_avl, ab=None):
                nonlocal rrow_free, rcb_free, sigT_free, sgt_free
                tsg1, trc = ab
                rct = None
                for ch_ in range(2):
                    rct = kb.do("T", (lambda c=ch_ * 512: lambda e: e.matmul(
                        PS[3][:, c:c + 512], onr[0:1, :],
                        rrow[0:1, c:c + 512],
                        start=True, stop=True))(), waits=[trc],
                        inc=(1 if ch_ == 1 else 0))
                rrow_free = rct
                trcb = kb.do("V", lambda e: e.tensor_copy(
                    rcb[:, :], PS[3][:, :]), waits=[rct], inc=1)
                t_last[3] = trcb
                tt1 = kb.do("V", lambda e: e.tensor_tensor(
                    t1[:, :], PS[2][:, :], rcb[:, :], ALU.mult),
                    waits=[tok_avl_, trcb], inc=1)
                t_last[2] = tt1
                rcb_free = tt1
                tsig = kb.do("V", lambda e: e.tensor_scalar(
                    sigT[:, :], sgt[:, :], 0.5, 0.5, ALU.mult, ALU.add),
                    waits=[tsg1, sigT_free], inc=1)
                sgt_free = tsig
                gat_done[h] = kb.do("V", (lambda hh=h: lambda e:
                                    e.tensor_tensor(
                    gat[:, hh, :], t1[:, :], sigT[:, :], ALU.mult))(),
                    waits=[tsig], inc=1)
                sigT_free = gat_done[h]

            if hl == 0:
                pend_b = (lambda f=epi_b, a=epi_a(): lambda: f(ab=a))()
                kT_free[hl] = sT_free[1]
            else:
                pend_epi = (epi_a, epi_b)
                kT_free[hl] = sT_free[1]
                t_last[0] = sT_free[0]
                t_last[1] = sT_free[1]
        vg_free = tok_avl
    if pend_epi is not None:
        _ea, _eb = pend_epi
        _eb(ab=_ea())
        pend_epi = None

    if dbg:
        gat_done[H - 1] = kb.do("Y", lambda e: e.dma_start(
            dbg_gat, gat[:, :, :]), waits=[gat_done[H - 1]], inc=16)

    wo_cast = [None] * KC
    for kc in range(KC):
        bx = kc % 2
        two = kb.do("Y", (lambda k=kc, b=bx: lambda e: e.dma_start(
            wst[b][:, :], wo_d[k]))(), waits=[wst_free[bx]], inc=16)
        wo_cast[kc] = kb.do("S", (lambda k=kc, b=bx: lambda e: e.copy(
            wo_b[:, k, :], wst[b][:, :]))(),
            waits=[two, h1T_last_read], inc=1)
        wst_free[bx] = wo_cast[kc]

    kb.fence(engs=["Y", "G"])
    xo_free = [None, None]
    x2_free = [None, None]
    h2t_free = [None, None]
    sp_toks = [None] * NTO
    pA_free = [t_last[0], t_last[1], t_last[2], t_last[3], vg_free]
    pB_free = []
    h2T_evac = None
    last_evac = None
    pend4 = None

    def emit_tr4():
        nonlocal h2T_evac, last_evac, pA_free, pB_free
        pairp, bxp, ttp, ap3p, tr3p = pend4
        ttok = None
        for kc in range(KC):
            ttok = kb.do("T", (lambda p=pairp[0], b=bxp, k=kc: lambda e:
                         e.transpose(
                p[:, 64 * k:64 * k + 64].bitcast(BF16),
                h2t[b][:, k * 128:(k + 1) * 128], idn[:, :]))(),
                waits=[ap3p, tr3p],
                inc=(1 if kc == KC - 1 else 0))
        h2t_free[bxp] = ttok
        ev = kb.do("V", (lambda p=pairp[0], t=ttp: lambda e: e.tensor_copy(
            h2T[:, :, t * 128:(t + 1) * 128], p[:, :].bitcast(BF16)))(),
            waits=[ttok], inc=1)
        h2T_evac = ev
        last_evac = ev
        if bxp == 0:
            pA_free = [ev]
        else:
            pB_free = [ev]

    for tt in range(NTO):
        bx = tt % 2
        pair = [(PS[0], PS[1]), (PS[2], PS[3])][bx]
        txo = kb.do("Y", (lambda t=tt, b=bx: lambda e: e.dma_start(
            xow[b][:, :], x_l[t]))(), waits=[xo_free[bx]], inc=16)
        otok = None
        ofree = (pA_free if bx == 0 else pB_free) + [h2T_evac]
        for kc in range(KC):
            for ech in range(4):
                pt = pair[ech // 2]
                col = (ech % 2) * 512
                otok = kb.do("T", (lambda p=pt, c=col, k=kc, t=tt, q=ech:
                             lambda e: e.matmul(
                    p[:, c:c + 512], gat[:, k, t * 128:(t + 1) * 128],
                    wo_b[:, k, 512 * q:512 * q + 512],
                    start=(k == 0), stop=(k == KC - 1)))(),
                    waits=[gat_done[H - 1], wo_cast[kc]] + ofree,
                    inc=(1 if (kc == KC - 1 and ech == 3) else 0))
        pA_free, pB_free = (([], pB_free) if bx == 0 else (pA_free, []))
        if pend4 is not None:
            emit_tr4()
        kb.do("V", (lambda p=pair[0]: lambda e: e.tensor_copy(
            ln2o[:, 0:1024], p[:, :]))(), waits=[otok], inc=0)
        ev1 = kb.do("V", (lambda p=pair[1]: lambda e: e.tensor_copy(
            ln2o[:, 1024:2048], p[:, :]))(), inc=1)
        if dbg and tt == 0:
            ev1 = kb.do("Y", lambda e: e.dma_start(dbg_ao, ln2o[:, :]),
                        waits=[ev1], inc=16)
        tr_, tn = ln_stats([ln2o[:, :]], [ev1])
        if dbg and tt == 0:
            kb.do("Y", lambda e: e.dma_start(dbg_st, st24[:, :]),
                  waits=[tr_], inc=16)
            kb.do("V", lambda e: e.tensor_copy(mv2d[:, 0:2], mv2[:, :]), inc=0)
            tn = kb.do("V", lambda e: e.tensor_copy(mv2d[:, 2:3], rstd[:, :]),
                       waits=[tn], inc=1)
            kb.do("Y", lambda e: e.dma_start(dbg_mv, mv2d[:, 0:3]),
                  waits=[tn], inc=16)
        ap2 = ln_apply(ln2o[:, :], ln2o[:, :], tn)
        if dbg and tt == 0:
            ap2 = kb.do("Y", lambda e: e.dma_start(dbg_ln, ln2o[:, :]),
                        waits=[ap2], inc=16)
            kb.do("Y", (lambda b=bx: lambda e: e.dma_start(
                dbg_xo, xow[b][:, :]))(), waits=[txo], inc=16)
        tadd = kb.do("V", (lambda b=bx: lambda e: e.tensor_tensor(
            x2t[b][:, :], xow[b][:, :], ln2o[:, :], ALU.add))(),
            waits=[txo, ap2, x2_free[bx]], inc=1)
        xo_free[bx] = tadd
        tr3, tn3 = ln_stats([x2t[bx][:, :]], [tadd])
        ap3 = ln_apply(h2t[bx][:, :], x2t[bx][:, :], tn3,
                       waits=[h2t_free[bx]])
        sp_toks[tt] = kb.do("G", (lambda t=tt, b=bx: lambda e: e.dma_start(
            x2_d[t], x2t[b][:, :]))(), waits=[tadd, ap3, tr3],
            inc=16, chan="GD")
        x2_free[bx] = sp_toks[tt]
        pend4 = (pair, bx, tt, ap3, tr3)
    emit_tr4()
    pend4 = None

    tgs = kb.do("Y", lambda e: e.dma_start(gsv[:, :], gs_d), inc=16)
    tus = kb.do("Y", lambda e: e.dma_start(usv[:, :], us_d), inc=16)
    pre_g0 = kb.do("Y", lambda e: e.dma_start(wst[0][:, :], wg_d[0]),
                   waits=[wo_cast[14]], inc=16)
    pre_u0 = kb.do("Y", lambda e: e.dma_start(wst[1][:, :], wu_d[0]),
                   waits=[wo_cast[15]], inc=16)

    kb.fence()

    wgu_free = [None, None]
    wgb_free = [None, None]
    wub_free = [None, None]
    sg_free = [None, None]
    uu_free = [None, None]
    hdn_toks = [None] * ITN
    pgu_free = {0: [], 1: []}
    gtok = utok = None
    for it in range(ITN):
        bx = it % 2
        if it == 0:
            tg_, sg_src = pre_g0, 0
        else:
            tg_ = kb.do("Y", (lambda i=it, b=bx: lambda e: e.dma_start(
                wst[b][:, :], wg_d[i]))(), waits=[wgu_free[bx]], inc=16)
            sg_src = bx
        cg = kb.do("S", (lambda b=bx, sc=sg_src: lambda e: e.copy(
            wgb[b][:, :, :], wst[sc][:, :]))(),
            waits=[tg_, wgb_free[bx]], inc=1)
        wgu_free[sg_src] = cg
        if it == 0:
            tu_, su_src = pre_u0, 1
        else:
            tu_ = kb.do("Y", (lambda i=it, b=bx: lambda e: e.dma_start(
                wst[b][:, :], wu_d[i]))(), waits=[wgu_free[bx]], inc=16)
            su_src = bx
        cu = kb.do("S", (lambda b=bx, sc=su_src: lambda e: e.copy(
            wub[b][:, :, :], wst[sc][:, :]))(),
            waits=[tu_, wub_free[bx]], inc=1)
        wgu_free[su_src] = cu
        pg, pu = (PS[0], PS[1]) if bx == 0 else (PS[2], PS[3])
        gtok = None
        for kc in range(KC):
            for ch_ in range(2):
                gtok = kb.do("T", (lambda p=pg, b=bx, k=kc, c=ch_ * 512:
                             lambda e: e.matmul(
                    p[:, c:c + 512], wgb[b][:, k, :], h2T[:, k, c:c + 512],
                    start=(k == 0), stop=(k == KC - 1)))(),
                    waits=[cg, last_evac] + pgu_free[bx],
                    inc=(1 if (kc == KC - 1 and ch_ == 1) else 0))
        wgb_free[bx] = gtok
        utok = None
        for kc in range(KC):
            for ch_ in range(2):
                utok = kb.do("T", (lambda p=pu, b=bx, k=kc, c=ch_ * 512:
                             lambda e: e.matmul(
                    p[:, c:c + 512], wub[b][:, k, :], h2T[:, k, c:c + 512],
                    start=(k == 0), stop=(k == KC - 1)))(),
                    waits=[cu],
                    inc=(1 if (kc == KC - 1 and ch_ == 1) else 0))
        wub_free[bx] = utok
        pgu_free[bx] = []
        tsg_ = kb.do("S", (lambda p=pg, b=bx, i=it: lambda e: e.activation(
            sg_b[b][:, :], p[:, :], ACT.Silu, scale=gsv[:, i:i + 1]))(),
            waits=[gtok, tgs, sg_free[bx]], inc=1)
        tuu = kb.do("V", (lambda p=pu, b=bx: lambda e: e.tensor_copy(
            uu_b[b][:, :], p[:, :]))(), waits=[utok, uu_free[bx]], inc=1)
        dst = hdna if it < 16 else hdnb
        di = it if it < 16 else it - 16
        extra = ([gat_done[H - 1]] if it < 16
                 else [sp_toks[NTO - 1], sp_toks[NTO - 2]])
        hdn_toks[it] = kb.do("V", (lambda dd=dst, i=di, b=bx: lambda e:
                             e.tensor_tensor(
            dd[:, i, :], sg_b[b][:, :], uu_b[b][:, :], ALU.mult))(),
            waits=[tsg_] + extra, inc=1)
        sg_free[bx] = hdn_toks[it]
        uu_free[bx] = hdn_toks[it]
        pgu_free[bx] = [tsg_, tuu, hdn_toks[it]]

    tds = kb.do("Y", lambda e: e.dma_start(ds_b[:, :], ds_d),
                waits=[gtok, utok], inc=16)
    pre_dw = []
    for i in range(3):
        pre_dw.append(kb.do("Y", (lambda s=i: lambda e: e.dma_start(
            dwst[s][:, :], wd_d[0, s]))(), waits=[wgu_free[0]], inc=16))

    kb.fence()

    dwst_free = [None, None, None]
    dwb_free = [None, None]
    bank_free = {}
    ffo_toks = [None] * NTO
    pend_ffo = None

    def emit_ffo(gp, mml):
        for tt in range(NTO):
            pt = PS[tt // 2]
            col = (tt % 2) * 512
            ev = kb.do("V", (lambda p=pt, c=col, t=tt, g=gp: lambda e:
                       e.tensor_tensor(
                ffo[:, t, 512 * g:512 * g + 512], p[:, c:c + 512],
                ds_b[:, 512 * g:512 * g + 512], ALU.mult))(),
                waits=[mml, tds], inc=1)
            bank_free[(gp + 1, tt)] = ev
            if gp == 3:
                ffo_toks[tt] = ev

    for g4 in range(4):
        mmlast = None
        for ic in range(ITN):
            i3 = (g4 * ITN + ic) % 3
            b2 = (g4 * ITN + ic) % 2
            if g4 == 0 and ic < 3:
                td_ = pre_dw[ic]
            else:
                td_ = kb.do("Y", (lambda g=g4, i=ic, s=i3: lambda e:
                            e.dma_start(
                    dwst[s][:, :], wd_d[g, i]))(),
                    waits=[dwst_free[i3]], inc=16)
            cd = kb.do("V", (lambda s=i3, b=b2, i=ic: lambda e: e.tensor_scalar(
                dwb[b][:, :], dwst[s][:, :], usv[:, i:i + 1], None,
                ALU.mult))(), waits=[td_, tus, dwb_free[b2]], inc=1)
            dwst_free[i3] = cd
            if ic == 0 and pend_ffo is not None:
                emit_ffo(*pend_ffo)
                pend_ffo = None
            src = hdna if ic < 16 else hdnb
            si = ic if ic < 16 else ic - 16
            mt = None
            for tt in range(NTO):
                pt = PS[tt // 2]
                col = (tt % 2) * 512
                mt = kb.do("T", (lambda p=pt, c=col, s=src, j=si, t=tt, b=b2,
                           st_=(ic == 0), sp=(ic == ITN - 1): lambda e:
                           e.matmul(
                    p[:, c:c + 512], s[:, j, t * 128:(t + 1) * 128],
                    dwb[b][:, :], start=st_, stop=sp))(),
                    waits=[cd, hdn_toks[ic], bank_free.get((g4, tt))],
                    inc=(1 if tt == NTO - 1 else 0))
            dwb_free[b2] = mt
            mmlast = mt
        pend_ffo = (g4, mmlast)
    emit_ffo(*pend_ffo)
    pend_ffo = None

    kb.fence()

    x2r_free = [None, None]
    ost_free = [None, None]
    ty = None
    for tt in range(NTO):
        bx = tt % 2
        trx = kb.do("G", (lambda t=tt, b=bx: lambda e: e.dma_start(
            x2r[b][:, :], x2_d[t]))(), waits=[x2r_free[bx], sp_toks[tt]],
            inc=16, chan="GD")
        tr_, tn = ln_stats([ffo[:, tt, :]], [ffo_toks[tt]])
        ap_ = ln_apply(ln4o[:, :], ffo[:, tt, :], tn)
        to_ = kb.do("V", (lambda b=bx: lambda e: e.tensor_tensor(
            ost[b][:, :], x2r[b][:, :], ln4o[:, :], ALU.add))(),
            waits=[trx, ap_, ost_free[bx]], inc=1)
        x2r_free[bx] = to_
        ty = kb.do("Y", (lambda t=tt, b=bx: lambda e: e.dma_start(
            y_d[t], ost[b][:, :]))(), waits=[to_], inc=16)
        ost_free[bx] = ty

    final = [(s, s.val) for s in kb.ch["Y"].lanes + kb.ch["GD"].lanes
             if s.val > 0]
    kb.do("Y", lambda e: e.nop(), waits=final, inc=0)

    with nc.Block() as blk:
        blk.tensor(lambda e: kb.replay("T", e))
        blk.vector(lambda e: kb.replay("V", e))
        blk.scalar(lambda e: kb.replay("S", e))
        blk.sync(lambda e: kb.replay("Y", e))
        blk.gpsimd(lambda e: kb.replay("G", e))
    return nc, {e: len(kb.ops[e]) for e in kb.ops}


def _prep(L0_attn_x, L0_attn_q_w, L0_attn_k_w, L0_attn_v_w, L0_attn_o_w,
          L0_attn_gate_w, L0_attn_gate_b, L0_ffn_gate_w, L0_ffn_up_w,
          L0_ffn_down_w, L0_ffn_gate_s, L0_ffn_up_s, L0_ffn_down_s):
    f = np.float32
    asc = np.ascontiguousarray
    x = np.asarray(L0_attn_x, f)

    def lhs_blocks(w, nt):
        return asc(np.asarray(w, f).reshape(nt, 128, KC, 128)
                   .transpose(0, 3, 2, 1))

    wq = lhs_blocks(L0_attn_q_w, H)
    wk = lhs_blocks(L0_attn_k_w, H)
    wv = asc(np.asarray(L0_attn_v_w, f).reshape(NG, GH * 128, KC, 128)
             .transpose(0, 3, 2, 1))
    wo = asc(np.asarray(L0_attn_o_w, f).reshape(D, KC, 128).transpose(1, 2, 0))
    wg = lhs_blocks(L0_ffn_gate_w, ITN)
    wu = lhs_blocks(L0_ffn_up_w, ITN)
    wd = asc(np.asarray(L0_ffn_down_w, f).reshape(4, 512, ITN, 128)
             .transpose(0, 2, 3, 1))
    gwt = asc(np.asarray(L0_attn_gate_w, f).T)
    gb = asc(np.asarray(L0_attn_gate_b, f).reshape(128, 1))
    gs = asc(np.asarray(L0_ffn_gate_s, f).reshape(ITN, 128).T)
    us = asc(np.asarray(L0_ffn_up_s, f).reshape(ITN, 128).T)
    ds = asc(np.broadcast_to(np.asarray(L0_ffn_down_s, f), (128, D)))
    tri = asc(np.triu(np.ones((128, 128), BF)))
    trin = asc(np.triu(np.full((128, 128), -30000.0, BF), 1))
    one = np.ones((128, 1), BF)
    idn = np.eye(128, dtype=BF)
    eps = np.full((128, 1), EPS, f)
    onr = np.ones((1, 128), f)
    onrb = np.ones((1, 128), BF)
    shared = dict(wq_d=wq, wk_d=wk, wv_d=wv, wo_d=wo, wg_d=wg, wu_d=wu,
                  wd_d=wd, gwt_d=gwt, gb_d=gb, gs_d=gs, us_d=us, ds_d=ds,
                  tri_d=tri, one_d=one, idn_d=idn, eps_d=eps,
                  onr_d=onr, onrb_d=onrb, trin_d=trin)
    in_maps = []
    for c in range(8):
        b, par = c // 2, c % 2
        perm = ([2 * j + par for j in range(8)]
                + [2 * j + (1 - par) for j in range(8)])
        xl = asc(x[b].reshape(STN, 128, D)[perm])
        ot = (np.ones if par == 1 else np.zeros)((128, 128), BF)
        otn = (np.zeros((128, 128)) if par == 1
               else np.full((128, 128), -30000.0)).astype(BF)
        in_maps.append(dict(shared, x_l=xl, oth_d=asc(ot), othn_d=asc(otn)))
    return in_maps


def kernel(**inputs):
    if "nc" not in _CACHE:
        _CACHE["nc"] = build_program()[0]
    nc = _CACHE["nc"]
    in_maps = _prep(**inputs)
    res = run_bass_kernel_spmd(nc, in_maps, core_ids=list(range(8)))
    y = np.empty((BATCH, SEQ, D), np.float32)
    for c in range(8):
        b, par = c // 2, c % 2
        out = np.asarray(res.results[c]["y_d"])
        yv = y[b].reshape(STN, 128, D)
        for j in range(NTO):
            yv[2 * j + par] = out[j]
    return y


if __name__ == "__main__":
    nc, counts = build_program()
    print("instructions:", nc.n_instructions(), counts)


# version 28
# speedup vs baseline: 1.0075x; 1.0075x over previous
import numpy as np
import ml_dtypes
import concourse.bass as bass
import concourse.mybir as mybir
from concourse.bass_utils import run_bass_kernel_spmd

F32 = mybir.dt.float32
BF16 = mybir.dt.bfloat16
ALU = mybir.AluOpType
ACT = mybir.ActivationFunctionType
AXX = mybir.AxisListType.X
BF = ml_dtypes.bfloat16

D = 2048
SEQ = 2048
BATCH = 4
H = 16
HD = 128
KC = 16
II = 5632
ITN = 44
NTO = 8
STN = 16
GH = 2
NG = H // GH
EPS = 1e-5
CSC = 1.0 / float(np.sqrt(128.0))

_CACHE = {}


class Sub:
    __slots__ = ("eng", "sem", "val", "dma", "name")

    def __init__(self, eng, dma, name):
        self.eng = eng
        self.sem = None
        self.val = 0
        self.dma = dma
        self.name = name


class Ch:

    def __init__(self, eng, dma, nlanes, name):
        self.dma = dma
        self.lanes = [Sub(eng, dma, f"{name}{i}") for i in range(nlanes)]
        self.rr = 0

    def issue(self, amt):
        if self.dma:
            s = self.lanes[self.rr]
            self.rr = (self.rr + 1) % len(self.lanes)
        else:
            s = self.lanes[0]
        s.val += amt
        return s, (s, s.val)


class KB:

    ENGS = ("T", "V", "S", "Y", "G")

    def __init__(self):
        self.ch = {
            "T": Ch("T", False, 1, "cT"), "V": Ch("V", False, 1, "cV"),
            "S": Ch("S", False, 1, "cS"), "Y": Ch("Y", True, 8, "cY"),
            "G": Ch("G", False, 1, "cG"), "GD": Ch("G", True, 4, "cD"),
        }
        self.ops = {e: [] for e in self.ENGS}
        self.waited = {}
        self.pending = {e: [] for e in self.ENGS}

    def all_subs(self):
        for c in self.ch.values():
            yield from c.lanes

    def do(self, eng, fn, waits=(), inc=0, chan=None):
        extra = []
        tok_out = None
        sub = None
        if inc:
            ch = self.ch[chan or eng]
            if ch.dma:
                lane = ch.lanes[ch.rr]
                if lane.val > 0:
                    extra.append((lane, lane.val))
            sub, tok_out = ch.issue(inc)
        wl = []
        for tok in list(waits) + self.pending[eng] + extra:
            if tok is None:
                continue
            c, v = tok
            if c.eng == eng and not c.dma:
                continue
            key = (eng, id(c))
            if self.waited.get(key, 0) >= v:
                continue
            self.waited[key] = v
            wl.append((c, v))
        self.pending[eng] = []
        self.ops[eng].append((wl, fn, sub, inc))
        return tok_out

    def fence(self, engs=None, skip=()):
        ex = set()
        for c in skip:
            ex.update(id(l) for l in self.ch[c].lanes)
        snap = [(s, s.val) for s in self.all_subs()
                if s.val > 0 and id(s) not in ex]
        for e in engs or self.ENGS:
            self.pending[e].extend(snap)

    def replay(self, eng, engobj):
        first = True
        for wl, fn, sub, amt in self.ops[eng]:
            if eng == "V" and not first:
                engobj.drain()
            first = False
            for c, v in wl:
                engobj.wait_ge(c.sem, v)
            ins = fn(engobj)
            if sub is not None:
                ins.then_inc(sub.sem, amt)


def build_program(dbg=False):
    nc = bass.Bass("TRN2", target_bir_lowering=False, debug=False,
               detect_race_conditions=False)
    kb = KB()

    dt = nc.dram_tensor
    x_l = dt("x_l", [STN, 128, D], F32, kind="ExternalInput").ap()
    wq_d = dt("wq_d", [H, 128, KC, 128], F32, kind="ExternalInput").ap()
    wk_d = dt("wk_d", [H, 128, KC, 128], F32, kind="ExternalInput").ap()
    wv_d = dt("wv_d", [NG, 128, KC, GH * 128], F32, kind="ExternalInput").ap()
    wo_d = dt("wo_d", [KC, 128, D], F32, kind="ExternalInput").ap()
    wg_d = dt("wg_d", [ITN, 128, KC, 128], F32, kind="ExternalInput").ap()
    wu_d = dt("wu_d", [ITN, 128, KC, 128], F32, kind="ExternalInput").ap()
    wd_d = dt("wd_d", [4, ITN, 128, 512], F32, kind="ExternalInput").ap()
    gwt_d = dt("gwt_d", [128, 128], F32, kind="ExternalInput").ap()
    gb_d = dt("gb_d", [128, 1], F32, kind="ExternalInput").ap()
    gs_d = dt("gs_d", [128, ITN], F32, kind="ExternalInput").ap()
    us_d = dt("us_d", [128, ITN], F32, kind="ExternalInput").ap()
    ds_d = dt("ds_d", [128, D], F32, kind="ExternalInput").ap()
    tri_d = dt("tri_d", [128, 128], BF16, kind="ExternalInput").ap()
    oth_d = dt("oth_d", [128, 128], BF16, kind="ExternalInput").ap()
    trin_d = dt("trin_d", [128, 128], BF16, kind="ExternalInput").ap()
    othn_d = dt("othn_d", [128, 128], BF16, kind="ExternalInput").ap()
    one_d = dt("one_d", [128, 1], BF16, kind="ExternalInput").ap()
    idn_d = dt("idn_d", [128, 128], BF16, kind="ExternalInput").ap()
    eps_d = dt("eps_d", [128, 1], F32, kind="ExternalInput").ap()
    onr_d = dt("onr_d", [1, 128], F32, kind="ExternalInput").ap()
    onrb_d = dt("onrb_d", [1, 128], BF16, kind="ExternalInput").ap()
    y_d = dt("y_d", [NTO, 128, D], F32, kind="ExternalOutput").ap()
    if dbg:
        dbg_ao = dt("dbg_ao", [128, D], F32, kind="ExternalOutput").ap()
        dbg_ln = dt("dbg_ln", [128, D], F32, kind="ExternalOutput").ap()
        dbg_mv = dt("dbg_mv", [128, 3], F32, kind="ExternalOutput").ap()
        dbg_st = dt("dbg_st", [128, 24], F32, kind="ExternalOutput").ap()
        dbg_xo = dt("dbg_xo", [128, D], F32, kind="ExternalOutput").ap()
        dbg_h1T = dt("dbg_h1T", [128, KC, SEQ], BF16, kind="ExternalOutput").ap()
        dbg_kT = dt("dbg_kT", [128, GH, SEQ], BF16, kind="ExternalOutput").ap()
        dbg_qT = dt("dbg_qT", [128, GH, 1024], BF16, kind="ExternalOutput").ap()
        dbg_vg = dt("dbg_vg", [128, STN, GH * 128], BF16, kind="ExternalOutput").ap()
        dbg_gat = dt("dbg_gat", [128, KC, 1024], BF16, kind="ExternalOutput").ap()
        dbg_Bb = dt("dbg_Bb", [1, 1024], BF16, kind="ExternalOutput").ap()
        dbg_p = dt("dbg_p", [128, 1024], BF16, kind="ExternalOutput").ap()
    x2_d = dt("x2_d", [NTO, 128, D], F32,
              kind="ExternalOutput" if dbg else "Internal").ap()

    PS = [nc.alloc_psum_tensor(f"ps{i}", [128, 1024], F32) for i in range(4)]

    base = 16512

    def sb(name, shape, dty, at):
        return nc.alloc_sbuf_tensor_at(name, shape, dty, offset=base + at)

    S1, S2, S3, S4 = 0, 65536, 155648, 188416
    KS = 204800
    h1T = sb("h1T", [128, KC, SEQ], BF16, S1)
    wo_b = sb("wo_b", [128, KC, D], BF16, S1)
    sg_b = [sb(f"sg{i}", [128, 1024], BF16, S1 + i * 2048) for i in range(2)]
    uu_b = [sb(f"uu{i}", [128, 1024], BF16, S1 + 4096 + i * 2048) for i in range(2)]
    wgb = [sb(f"wgb{i}", [128, KC, 128], BF16, S1 + 8192 + i * 4096) for i in range(2)]
    wub = [sb(f"wub{i}", [128, KC, 128], BF16, S1 + 16384 + i * 4096) for i in range(2)]
    ffo = sb("ffo", [128, NTO, D], F32, S1)
    xst = [sb(f"xst{i}", [128, D], F32, S3 + i * 8192) for i in range(2)]
    h1t = [sb(f"h1t{i}", [128, D], BF16, S3 + 16384 + i * 4096) for i in range(2)]
    kT = sb("kT", [128, GH, SEQ], BF16, S2 + 0)
    vg = sb("vg", [128, STN, GH * 128], BF16, S2 + 8192)
    qT = sb("qT", [128, GH, 1024], BF16, S2 + 16384)
    wq_s = [sb(f"wq_s{i}", [128, GH, KC, 128], BF16, S2 + 20480 + 8192 * i)
            for i in range(2)]
    wk_s = [sb(f"wk_s{i}", [128, GH, KC, 128], BF16, S2 + 36864 + 8192 * i)
            for i in range(2)]
    wvb = [sb(f"wvb{i}", [128, KC, GH * 128], BF16, S2 + 53248 + 8192 * i)
           for i in range(2)]
    p_b = [sb(f"p{i}", [128, 1024], BF16, S2 + 69632 + i * 2048) for i in range(4)]
    sigT = sb("sigT", [128, 1024], BF16, S2 + 77824)
    t1 = sb("t1", [128, 1024], F32, S2 + 79872)
    Brow2 = [sb(f"Brow{i}", [1, 1024], BF16, S2 + 83968 + 2048 * i)
             for i in range(2)]
    rcb = sb("rcb", [128, 1024], BF16, S2 + 88064)
    rrow = sb("rrow", [1, 1024], F32, S2 + 79872)
    ln2o = sb("ln2o", [128, D], F32, S2 + 0)
    h2t = [sb(f"h2t{i}", [128, D], BF16, S2 + 8192 + i * 4096) for i in range(2)]
    xow = [sb(f"xow{i}", [128, D], F32, S2 + 16384 + i * 8192) for i in range(2)]
    x2t = [sb(f"x2t{i}", [128, D], F32, S2 + 32768 + i * 8192) for i in range(2)]
    h2T = sb("h2T", [128, KC, 1024], BF16, S2 + 57344)
    hdnb = sb("hdnb", [128, ITN - 16, 1024], BF16, S2 + 0)
    ds_b = sb("ds_b", [128, D], F32, S2 + 57344)
    ln4o = sb("ln4o", [128, D], F32, S2 + 0)
    x2r = [sb(f"x2r{i}", [128, D], F32, S2 + 16384 + i * 8192) for i in range(2)]
    ost = [sb(f"ost{i}", [128, D], F32, S2 + 32768 + i * 8192) for i in range(2)]
    gat = sb("gat", [128, KC, 1024], BF16, S3)
    hdna = sb("hdna", [128, 16, 1024], BF16, S3)
    wst = [sb(f"wst{i}", [128, D], F32, S4 + i * 8192) for i in range(2)]
    dwst = [sb(f"dwst{i}", [128, 512], F32, S4 + i * 2048) for i in range(3)]
    dwb = [sb(f"dwb{i}", [128, 512], BF16, S4 + 6144 + i * 1024) for i in range(2)]
    ko = KS
    tri = sb("tri", [128, 128], BF16, ko); ko += 256
    trin = sb("trin", [128, 128], BF16, ko); ko += 256
    othn = sb("othn", [128, 128], BF16, ko); ko += 256
    oth = sb("oth", [128, 128], BF16, ko); ko += 256
    idn = sb("idn", [128, 128], BF16, ko); ko += 256
    onec = sb("onec", [128, 1], BF16, ko); ko += 32
    gwt = sb("gwt", [128, 128], F32, ko); ko += 512
    gwb = sb("gwb", [128, 128], BF16, ko); ko += 256
    gbv = sb("gbv", [128, 1], F32, ko); ko += 32
    gbh = sb("gbh", [128, 1], F32, ko); ko += 32
    gsv = sb("gsv", [128, ITN], F32, ko); ko += 192
    usv = sb("usv", [128, ITN], F32, ko); ko += 192
    st24 = sb("st24", [128, 24], F32, ko); ko += 96
    mv2 = sb("mv2", [128, 2], F32, ko); ko += 32
    sqv = sb("sqv", [128, 1], F32, ko); ko += 32
    rstd = sb("rstd", [128, 1], F32, ko); ko += 32
    ngm = sb("ngm", [128, 1], F32, ko); ko += 32
    km = sb("km", [128, 1], BF16, ko); ko += 32
    kmb = sb("kmb", [128, 1], BF16, ko); ko += 32
    epsv = sb("epsv", [128, 1], F32, ko); ko += 32
    mv2d = sb("mv2d", [128, 4], F32, ko); ko += 64
    ko = (ko + 31) // 32 * 32
    sgt = sb("sgt", [128, 1024], F32, ko)
    absq = sb("absq", [128, 1024], BF16, ko)
    ko += 4096
    onr = sb("onr", [1, 128], F32, ko); ko += 512
    onrb = sb("onrb", [1, 128], BF16, ko); ko += 256
    assert base + ko <= 229344

    for s in kb.all_subs():
        s.sem = nc.alloc_semaphore(name=s.name)

    def ln_stats(src_aps, tok_ready):
        tread = None
        i = 0
        for ap in src_aps:
            w = ap.shape[-1]
            for j in range(w // 512):
                tread = kb.do(
                    "V",
                    (lambda a=ap, jj=j, ii=i: lambda e: e.bn_stats(
                        st24[:, 6 * ii:6 * ii + 6],
                        a[:, 512 * jj:512 * jj + 512]))(),
                    waits=list(tok_ready), inc=1)
                i += 1
        assert i == 4
        tag = kb.do("V", lambda e: e.bn_aggr(mv2[:, :], st24[:, :]), inc=1)
        ts_ = kb.do("S", lambda e: e.activation(sqv[:, :], mv2[:, 1:2], ACT.Sqrt,
                                                bias=epsv[:, 0:1]),
                    waits=[tag, tok_eps], inc=1)
        tn = kb.do("V", lambda e: e.reciprocal(rstd[:, :], sqv[:, :]),
                   waits=[ts_], inc=1)
        return tread, tn

    def ln_apply(dst, src, tok_n, waits=()):
        return kb.do("V", lambda e: e.tensor_scalar(
            dst, src, mv2[:, 0:1], rstd[:, 0:1], ALU.subtract, ALU.mult),
            waits=[tok_n] + list(waits), inc=1)

    tok_idn = kb.do("Y", lambda e: e.dma_start(idn[:, :], idn_d), inc=16)
    tok_gw = kb.do("Y", lambda e: e.dma_start(gwt[:, :], gwt_d), inc=16)
    tok_gb = kb.do("Y", lambda e: e.dma_start(gbv[:, :], gb_d), inc=16)
    kb.do("Y", lambda e: e.dma_start(tri[:, :], tri_d), inc=16)
    kb.do("Y", lambda e: e.dma_start(oth[:, :], oth_d), inc=16)
    kb.do("Y", lambda e: e.dma_start(trin[:, :], trin_d), inc=16)
    kb.do("Y", lambda e: e.dma_start(othn[:, :], othn_d), inc=16)
    kb.do("Y", lambda e: e.dma_start(onec[:, :], one_d), inc=16)
    tok_eps = kb.do("Y", lambda e: e.dma_start(epsv[:, :], eps_d), inc=16)
    kb.do("Y", lambda e: e.dma_start(onr[:, :], onr_d), inc=16)
    kb.do("Y", lambda e: e.dma_start(onrb[:, :], onrb_d), inc=16)
    tok_gwb = kb.do("V", lambda e: e.tensor_copy(gwb[:, :], gwt[:, :]),
                    waits=[tok_gw], inc=1)
    kb.do("V", lambda e: e.tensor_scalar(gbh[:, :], gbv[:, :], 0.5, None,
                                         ALU.mult), waits=[tok_gb], inc=0)

    wst_free = [None, None]
    wq_free = [None, None]
    wk_free = [None, None]
    wvb_free = [None, None]

    wcast = {}

    def load_group_w(g):
        if g >= NG:
            return
        p2 = g % 2
        toks = []
        for hl in range(GH):
            h = g * GH + hl
            bx = hl % 2
            tq = kb.do("Y", (lambda hh=h, b=bx: lambda e: e.dma_start(
                wst[b][:, :], wq_d[hh]))(), waits=[wst_free[bx]], inc=16)
            cq = kb.do("S", (lambda b=bx, p=p2, hh=hl: lambda e: e.copy(
                wq_s[p][:, hh, :, :], wst[b][:, :]))(),
                waits=[tq, wq_free[p2]], inc=1)
            wst_free[bx] = cq
            tk = kb.do("Y", (lambda hh=h, b=bx: lambda e: e.dma_start(
                wst[b][:, :], wk_d[hh]))(), waits=[wst_free[bx]], inc=16)
            ck = kb.do("S", (lambda b=bx, p=p2, hh=hl: lambda e: e.copy(
                wk_s[p][:, hh, :, :], wst[b][:, :]))(),
                waits=[tk, wk_free[p2]], inc=1)
            wst_free[bx] = ck
            toks.append((cq, ck))
        cv = None
        for half in range(2):
            bx = half
            tv = kb.do("Y", (lambda gg=g, hf=half, b=bx: lambda e: e.dma_start(
                wst[b][:, :], wv_d[gg, :, 8 * hf:8 * hf + 8, :]))(),
                waits=[wst_free[bx]], inc=16)
            cv = kb.do("S", (lambda p=p2, hf=half, b=bx: lambda e: e.copy(
                wvb[p][:, 8 * hf:8 * hf + 8, :], wst[b][:, :]))(),
                waits=[tv, wvb_free[p2]], inc=1)
            wst_free[bx] = cv
        wcast[g] = (toks, cv)

    load_group_w(0)

    tr_evac = [None, None]
    xst_free = [None, None]
    h1T_done = []
    h1t_free = [None, None]
    pend_ev = None

    def flush_ev():
        nonlocal pend_ev
        pp, pi, ptk = pend_ev
        ev = kb.do("V", (lambda p=pp, i=pi: lambda e: e.tensor_copy(
            h1T[:, :, i * 128:(i + 1) * 128], p[:, :].bitcast(BF16)))(),
            waits=[ptk], inc=1)
        tr_evac[pi % 2] = ev
        h1T_done.append(ev)
        pend_ev = None
        return ev

    for idx in range(STN):
        bx = idx % 2
        tok_x = kb.do("Y", (lambda i=idx, b=bx: lambda e: e.dma_start(
            xst[b][:, :], x_l[i]))(), waits=[xst_free[bx]], inc=16)
        tr_ = None
        for j in range(4):
            tr_ = kb.do("V", (lambda b=bx, jj=j: lambda e: e.bn_stats(
                st24[:, 6 * jj:6 * jj + 6],
                xst[b][:, 512 * jj:512 * jj + 512]))(),
                waits=[tok_x], inc=(1 if j == 3 else 0))
        tag = kb.do("V", lambda e: e.bn_aggr(mv2[:, :], st24[:, :]), inc=1)
        ts_ = kb.do("S", lambda e: e.activation(sqv[:, :], mv2[:, 1:2],
                                                ACT.Sqrt, bias=epsv[:, 0:1]),
                    waits=[tag, tok_eps], inc=1)
        tn = kb.do("V", lambda e: e.reciprocal(rstd[:, :], sqv[:, :]),
                   waits=[ts_], inc=1)
        ta = kb.do("V", (lambda b=bx: lambda e: e.tensor_scalar(
            h1t[b][:, :], xst[b][:, :], mv2[:, 0:1], rstd[:, 0:1],
            ALU.subtract, ALU.mult))(),
            waits=[tn, h1t_free[bx]], inc=1)
        xst_free[bx] = ta
        pt = PS[2 + idx % 2]
        ttok = None
        for kc in range(KC):
            ttok = kb.do("T", (lambda p=pt, b=bx, k=kc: lambda e: e.transpose(
                p[:, 64 * k:64 * k + 64].bitcast(BF16),
                h1t[b][:, k * 128:(k + 1) * 128], idn[:, :]))(),
                waits=[ta, tr_evac[idx % 2], tok_idn],
                inc=(1 if kc == KC - 1 else 0))
        h1t_free[bx] = ttok
        ev = kb.do("S", (lambda p=pt, i=idx: lambda e: e.copy(
            h1T[:, :, i * 128:(i + 1) * 128], p[:, :].bitcast(BF16)))(),
            waits=[ttok], inc=1)
        tr_evac[idx % 2] = ev
        h1T_done.append(ev)
    h1T_ready = ev
    if dbg:
        kb.do("Y", lambda e: e.dma_start(dbg_h1T, h1T[:, :, :]),
              waits=[h1T_ready], inc=16)

    kb.fence()

    kT_free = [None] * GH
    qT_free = [None] * GH
    vg_free = None
    gat_done = [None] * H
    absq_free = None
    Brow_free = None
    Bb_free = []
    kmb_free = None
    sigT_free = None
    sgt_free = None
    rrow_free = None
    rcb_free = None
    h1T_last_read = None
    t_last = {0: None, 1: None, 2: None, 3: None}
    tok_avl = None
    pend_epi = None
    pend_b = None
    tBr2 = [None, None]
    eqh = [None, None]
    tkm2 = [None, None]
    tabs2 = [None, None]
    kmh = [km, kmb]
    kmb_free2 = [None, None]

    for g in range(NG):
        p2 = g % 2
        wq_cast = [wcast[g][0][hl][0] for hl in range(GH)]
        wk_cast = [wcast[g][0][hl][1] for hl in range(GH)]
        vcast = wcast[g][1]
        if pend_epi is not None:
            epi_a_prev, epi_b_prev = pend_epi
            pend_bp = (lambda f=epi_b_prev, a=epi_a_prev(): lambda: f(ab=a))()
            pend_epi = None
        else:
            pend_bp = None
        for hl in range(GH):
            h = g * GH + hl
            bx = hl % 2
            ktok = None
            for ch_ in range(4):
                pt = PS[0] if ch_ < 2 else PS[1]
                col = (ch_ % 2) * 512
                for kc in range(KC):
                    ktok = kb.do("T", (lambda p=pt, c=col, k=kc, cc=ch_,
                                 pp=p2, hh=hl: lambda e: e.matmul(
                        p[:, c:c + 512], wk_s[pp][:, hh, k, :],
                        h1T[:, k, cc * 512:cc * 512 + 512],
                        start=(k == 0), stop=(k == KC - 1)))(),
                        waits=[wk_cast[hl],
                               h1T_done[4 * ch_ + 3] if g == 0
                               else h1T_ready,
                               t_last[0 if ch_ < 2 else 1]],
                        inc=(1 if (kc == KC - 1 and ch_ == 3) else 0))
                if ch_ == 0 and hl == 0 and pend_bp is not None:
                    pend_bp()
                    pend_bp = None
            ek = None
            for half in range(2):
                ek = kb.do("V", (lambda hh=hl, hf=half: lambda e: e.tensor_copy(
                    kT[:, hh, hf * 1024:hf * 1024 + 1024], PS[hf][:, :]))(),
                    waits=[ktok, kT_free[hl]], inc=1)
            t_last[0] = ek
            t_last[1] = ek
            tkm2[hl] = kb.do("V", (lambda hh=hl: lambda e: e.reduce_max(
                kmh[hh][:, :], kT[:, hh, :], AXX,
                apply_absolute_value=True))(),
                waits=[kmb_free2[hl], ek], inc=1)
            qp = PS[2 + hl]
            qtok = None
            for ch_ in range(2):
                for kc in range(KC):
                    qtok = kb.do("T", (lambda p=qp, c=ch_ * 512, k=kc,
                                 pp=p2, hh=hl: lambda e: e.matmul(
                        p[:, c:c + 512], wq_s[pp][:, hh, k, :],
                        h1T[:, k, c:c + 512],
                        start=(k == 0), stop=(k == KC - 1)))(),
                        waits=[wq_cast[hl], t_last[2 + hl]],
                        inc=(1 if (kc == KC - 1 and ch_ == 1) else 0))
            eq = kb.do("V", (lambda hh=hl, p=qp: lambda e: e.tensor_copy(
                qT[:, hh, :], p[:, :]))(), waits=[qtok, qT_free[hl]], inc=1)
            t_last[2 + hl] = eq
            eqh[hl] = eq
            if hl == 0:
                tabs2[0] = kb.do("S", lambda e: e.activation(
                    absq[:, :], qT[:, 0, :], ACT.Abs, scale=1.01),
                    waits=[absq_free, sgt_free, eq], inc=1)
            wq_free[p2] = qtok
            wk_free[p2] = ktok
            h1T_last_read = qtok
        if dbg and g == 0:
            kb.do("Y", lambda e: e.dma_start(dbg_kT, kT[:, :, :]),
                  waits=[t_last[1], ek], inc=16)
            kb.do("Y", lambda e: e.dma_start(dbg_qT, qT[:, :, :]),
                  waits=[eq], inc=16)

        tBr2 = [None, None]
        for hl in range(GH):
            if tabs2[hl] is None:
                tabs2[hl] = kb.do("S", (lambda hh=hl: lambda e: e.activation(
                    absq[:, :], qT[:, hh, :], ACT.Abs, scale=1.01))(),
                    waits=[absq_free, sgt_free, eqh[hl]], inc=1)
            btok = None
            for ch_ in range(2):
                btok = kb.do("T", (lambda c=ch_ * 512, hh=hl: lambda e:
                             e.matmul(
                    PS[2 + hh][0:1, c:c + 512], kmh[hh][:, 0:1],
                    absq[:, c:c + 512],
                    start=True, stop=True))(),
                    waits=[tkm2[hl], tabs2[hl], t_last[2 + hl]],
                    inc=(1 if ch_ == 1 else 0))
            kmb_free2[hl] = btok
            absq_free = btok
            tBr2[hl] = kb.do("V", (lambda hh=hl: lambda e: e.tensor_scalar(
                Brow2[hh][:, :], PS[2 + hh][0:1, :], -1.0, None, ALU.mult))(),
                waits=[btok, Brow_free], inc=1)
            if dbg and g == 0 and hl == 0:
                kb.do("Y", (lambda hh=hl: lambda e: e.dma_start(
                    dbg_Bb, Brow2[hh][:, :]))(), waits=[tBr2[hl]], inc=16)
        t_last[2] = tBr2[0]
        t_last[3] = tBr2[1]
        tabs2 = [None, None]
        load_group_w(g + 1)

        vev = None
        for sl in range(STN):
            pt = PS[0]
            col = (sl % 2) * 512
            vtok = None
            for kc in range(KC):
                vtok = kb.do("T", (lambda p=pt, c=col, k=kc, s=sl, pp=p2:
                             lambda e: e.matmul(
                    p[:, c:c + GH * 128], h1T[:, k, s * 128:(s + 1) * 128],
                    wvb[pp][:, k, :], start=(k == 0), stop=(k == KC - 1)))(),
                    waits=[vcast, t_last[0], h1T_ready],
                    inc=(1 if kc == KC - 1 else 0))
            vev = kb.do("V", (lambda p=pt, c=col, s=sl: lambda e: e.tensor_copy(
                vg[:, s, :], p[:, c:c + GH * 128]))(),
                waits=[vtok, vg_free], inc=1)
            if sl % 2 == 1:
                t_last[0] = vev
        t_last[0] = vev
        wvb_free[p2] = vtok
        h1T_last_read = vtok
        if dbg and g == 0:
            kb.do("Y", lambda e: e.dma_start(dbg_vg, vg[:, :, :]),
                  waits=[vev], inc=16)

        for hl in range(GH):
            h = g * GH + hl
            Brow = Brow2[hl]
            tBr = tBr2[hl]
            p_free = [None, None, None, None]
            av_first = {0: True, 1: True}
            l_first = {0: True, 1: True}
            av_rem = {0: 8, 1: 16}
            l_rem = {0: 8, 1: 16}
            sT_free = [t_last[0], t_last[1]]
            tcount = 0
            pendq = []

            def emit_avl(hh):
                nonlocal tok_avl
                bxp, slp, chp, ptokp = pendq.pop(0)
                for (c0, c1) in chp:
                    bank = 0 if c0 < 512 else 1
                    av_rem[bank] -= 1
                    kb.do("T", (lambda b=bxp, s=slp, a=c0, z=c1,
                                st_=av_first[bank],
                                sp=(av_rem[bank] == 0): lambda e: e.matmul(
                        PS[2][:, a:z], vg[:, s, hh * 128:(hh + 1) * 128],
                        p_b[b][:, a:z], start=st_, stop=sp))(),
                        waits=[ptokp, t_last[2]], inc=0)
                    av_first[bank] = False
                for (c0, c1) in chp:
                    bank = 0 if c0 < 512 else 1
                    l_rem[bank] -= 1
                    tok_avl = kb.do("T", (lambda b=bxp, a=c0, z=c1,
                                    st_=l_first[bank],
                                    sp=(l_rem[bank] == 0): lambda e:
                                    e.matmul(
                        PS[3][0:1, a:z], onec[:, :], p_b[b][:, a:z],
                        start=st_, stop=sp))(),
                        waits=[t_last[3], tBr],
                        inc=(1 if (c0, c1) == chp[-1] else 0))
                    l_first[bank] = False
                p_free[bxp] = tok_avl

            for i in range(8):
                ts0 = i * 128
                chunks = ([(ts0, 512), (512, 1024)] if i < 4
                          else [(ts0, 1024)])
                for half in range(2):
                    sl = i + 8 * half
                    bx = tcount % 2
                    pbx = tcount % 4
                    tcount += 1
                    stok = None
                    for (c0, c1) in chunks:
                        kb.do("T", (lambda b=bx, a=c0, z=c1, br=Brow:
                              lambda e: e.matmul(
                            PS[b][:, a:z], onrb[0:1, :], br[0:1, a:z],
                            start=True, stop=False))(),
                            waits=[sT_free[bx], tBr], inc=0)
                    mk = trin if half == 0 else othn
                    kb.do("T", (lambda b=bx, a=ts0, m=mk: lambda e: e.matmul(
                        PS[b][:, a:a + 128], m[:, :], idn[:, :],
                        start=False, stop=False))(), inc=0)
                    for (c0, c1) in chunks:
                        stok = kb.do("T", (lambda b=bx, hh=hl, s=sl, a=c0,
                                     z=c1: lambda e: e.matmul(
                            PS[b][:, a:z], kT[:, hh, s * 128:(s + 1) * 128],
                            qT[:, hh, a:z], start=False, stop=True))(),
                            waits=[],
                            inc=(1 if (c0, c1) == chunks[-1] else 0))
                    if len(pendq) == 3:
                        emit_avl(hl)
                    ptok = kb.do("S", (lambda b=pbx, bb=bx, a=ts0: lambda e:
                                 e.activation(
                        p_b[b][:, a:1024], PS[bb][:, a:1024], ACT.Exp,
                        scale=CSC))(),
                        waits=[stok, p_free[pbx]], inc=1)
                    sT_free[bx] = ptok
                    if dbg and g == 0 and hl == 0 and i == 0 and half == 0:
                        ptok = kb.do("Y", (lambda b=pbx: lambda e:
                                     e.dma_start(
                            dbg_p, p_b[b][:, :]))(), waits=[ptok], inc=16)
                    pendq.append((pbx, sl, chunks, ptok))
                    if tcount == 1 and pend_b is not None:
                        pend_b()
                        pend_b = None
            while pendq:
                emit_avl(hl)
            Brow_free = tok_avl
            def epi_a(hl=hl, h=h, sT0=sT_free[0], sT1=sT_free[1],
                      tok_avl_=tok_avl):
                nonlocal rrow_free
                gtok = None
                for ch_ in range(2):
                    gtok = kb.do("T", (lambda hh=hl, c=ch_ * 512: lambda e:
                                 e.matmul(
                        PS[0][:, c:c + 512], gwb[:, :], qT[:, hh, c:c + 512],
                        start=True, stop=True))(),
                        waits=[tok_gwb, sT0, sT1],
                        inc=(1 if ch_ == 1 else 0))
                tsg1 = kb.do("S", lambda e: e.activation(
                    sgt[:, :], PS[0][:, :], ACT.Tanh, bias=gbh[:, 0:1],
                    scale=0.5), waits=[gtok, sgt_free], inc=1)
                trc = kb.do("V", lambda e: e.reciprocal(
                    rrow[:, :], PS[3][0:1, :]),
                    waits=[tok_avl_, rrow_free], inc=1)
                t_last[0] = tsg1
                t_last[1] = sT1
                qT_free[hl] = gtok
                return tsg1, trc

            def epi_b(hl=hl, h=h, tok_avl_=tok_avl, ab=None):
                nonlocal rrow_free, rcb_free, sigT_free, sgt_free
                tsg1, trc = ab
                rct = None
                for ch_ in range(2):
                    rct = kb.do("T", (lambda c=ch_ * 512: lambda e: e.matmul(
                        PS[3][:, c:c + 512], onr[0:1, :],
                        rrow[0:1, c:c + 512],
                        start=True, stop=True))(), waits=[trc],
                        inc=(1 if ch_ == 1 else 0))
                rrow_free = rct
                trcb = kb.do("V", lambda e: e.tensor_copy(
                    rcb[:, :], PS[3][:, :]), waits=[rct], inc=1)
                t_last[3] = trcb
                tt1 = kb.do("V", lambda e: e.tensor_tensor(
                    t1[:, :], PS[2][:, :], rcb[:, :], ALU.mult),
                    waits=[tok_avl_, trcb], inc=1)
                t_last[2] = tt1
                rcb_free = tt1
                tsig = kb.do("V", lambda e: e.tensor_scalar(
                    sigT[:, :], sgt[:, :], 0.5, 0.5, ALU.mult, ALU.add),
                    waits=[tsg1, sigT_free], inc=1)
                sgt_free = tsig
                gat_done[h] = kb.do("V", (lambda hh=h: lambda e:
                                    e.tensor_tensor(
                    gat[:, hh, :], t1[:, :], sigT[:, :], ALU.mult))(),
                    waits=[tsig], inc=1)
                sigT_free = gat_done[h]

            if hl == 0:
                pend_b = (lambda f=epi_b, a=epi_a(): lambda: f(ab=a))()
                kT_free[hl] = sT_free[1]
            else:
                pend_epi = (epi_a, epi_b)
                kT_free[hl] = sT_free[1]
                t_last[0] = sT_free[0]
                t_last[1] = sT_free[1]
        vg_free = tok_avl
    if pend_epi is not None:
        _ea, _eb = pend_epi
        _eb(ab=_ea())
        pend_epi = None

    if dbg:
        gat_done[H - 1] = kb.do("Y", lambda e: e.dma_start(
            dbg_gat, gat[:, :, :]), waits=[gat_done[H - 1]], inc=16)

    wo_cast = [None] * KC
    for kc in range(KC):
        bx = kc % 2
        two = kb.do("Y", (lambda k=kc, b=bx: lambda e: e.dma_start(
            wst[b][:, :], wo_d[k]))(), waits=[wst_free[bx]], inc=16)
        wo_cast[kc] = kb.do("S", (lambda k=kc, b=bx: lambda e: e.copy(
            wo_b[:, k, :], wst[b][:, :]))(),
            waits=[two, h1T_last_read], inc=1)
        wst_free[bx] = wo_cast[kc]

    kb.fence(engs=["Y", "G"])
    xo_free = [None, None]
    x2_free = [None, None]
    h2t_free = [None, None]
    sp_toks = [None] * NTO
    pA_free = [t_last[0], t_last[1], t_last[2], t_last[3], vg_free]
    pB_free = []
    h2T_evac = None
    last_evac = None
    pend4 = None

    def emit_tr4():
        nonlocal h2T_evac, last_evac, pA_free, pB_free
        pairp, bxp, ttp, ap3p, tr3p = pend4
        ttok = None
        for kc in range(KC):
            ttok = kb.do("T", (lambda p=pairp[0], b=bxp, k=kc: lambda e:
                         e.transpose(
                p[:, 64 * k:64 * k + 64].bitcast(BF16),
                h2t[b][:, k * 128:(k + 1) * 128], idn[:, :]))(),
                waits=[ap3p, tr3p],
                inc=(1 if kc == KC - 1 else 0))
        h2t_free[bxp] = ttok
        ev = kb.do("S", (lambda p=pairp[0], t=ttp: lambda e: e.copy(
            h2T[:, :, t * 128:(t + 1) * 128], p[:, :].bitcast(BF16)))(),
            waits=[ttok], inc=1)
        h2T_evac = ev
        last_evac = ev
        if bxp == 0:
            pA_free = [ev]
        else:
            pB_free = [ev]

    for tt in range(NTO):
        bx = tt % 2
        pair = [(PS[0], PS[1]), (PS[2], PS[3])][bx]
        txo = kb.do("Y", (lambda t=tt, b=bx: lambda e: e.dma_start(
            xow[b][:, :], x_l[t]))(), waits=[xo_free[bx]], inc=16)
        otok = None
        ofree = (pA_free if bx == 0 else pB_free) + [h2T_evac]
        for kc in range(KC):
            for ech in range(4):
                pt = pair[ech // 2]
                col = (ech % 2) * 512
                otok = kb.do("T", (lambda p=pt, c=col, k=kc, t=tt, q=ech:
                             lambda e: e.matmul(
                    p[:, c:c + 512], gat[:, k, t * 128:(t + 1) * 128],
                    wo_b[:, k, 512 * q:512 * q + 512],
                    start=(k == 0), stop=(k == KC - 1)))(),
                    waits=[gat_done[H - 1], wo_cast[kc]] + ofree,
                    inc=(1 if (kc == KC - 1 and ech == 3) else 0))
        pA_free, pB_free = (([], pB_free) if bx == 0 else (pA_free, []))
        if pend4 is not None:
            emit_tr4()
        kb.do("S", (lambda p=pair[0]: lambda e: e.copy(
            ln2o[:, 0:1024], p[:, :]))(), waits=[otok], inc=0)
        ev1 = kb.do("S", (lambda p=pair[1]: lambda e: e.copy(
            ln2o[:, 1024:2048], p[:, :]))(), inc=1)
        if dbg and tt == 0:
            ev1 = kb.do("Y", lambda e: e.dma_start(dbg_ao, ln2o[:, :]),
                        waits=[ev1], inc=16)
        tr_, tn = ln_stats([ln2o[:, :]], [ev1])
        if dbg and tt == 0:
            kb.do("Y", lambda e: e.dma_start(dbg_st, st24[:, :]),
                  waits=[tr_], inc=16)
            kb.do("V", lambda e: e.tensor_copy(mv2d[:, 0:2], mv2[:, :]), inc=0)
            tn = kb.do("V", lambda e: e.tensor_copy(mv2d[:, 2:3], rstd[:, :]),
                       waits=[tn], inc=1)
            kb.do("Y", lambda e: e.dma_start(dbg_mv, mv2d[:, 0:3]),
                  waits=[tn], inc=16)
        ap2 = ln_apply(ln2o[:, :], ln2o[:, :], tn)
        if dbg and tt == 0:
            ap2 = kb.do("Y", lambda e: e.dma_start(dbg_ln, ln2o[:, :]),
                        waits=[ap2], inc=16)
            kb.do("Y", (lambda b=bx: lambda e: e.dma_start(
                dbg_xo, xow[b][:, :]))(), waits=[txo], inc=16)
        tadd = kb.do("V", (lambda b=bx: lambda e: e.tensor_tensor(
            x2t[b][:, :], xow[b][:, :], ln2o[:, :], ALU.add))(),
            waits=[txo, ap2, x2_free[bx]], inc=1)
        xo_free[bx] = tadd
        tr3, tn3 = ln_stats([x2t[bx][:, :]], [tadd])
        ap3 = ln_apply(h2t[bx][:, :], x2t[bx][:, :], tn3,
                       waits=[h2t_free[bx]])
        sp_toks[tt] = kb.do("G", (lambda t=tt, b=bx: lambda e: e.dma_start(
            x2_d[t], x2t[b][:, :]))(), waits=[tadd, ap3, tr3],
            inc=16, chan="GD")
        x2_free[bx] = sp_toks[tt]
        pend4 = (pair, bx, tt, ap3, tr3)
    emit_tr4()
    pend4 = None

    tgs = kb.do("Y", lambda e: e.dma_start(gsv[:, :], gs_d), inc=16)
    tus = kb.do("Y", lambda e: e.dma_start(usv[:, :], us_d), inc=16)
    pre_g0 = kb.do("Y", lambda e: e.dma_start(wst[0][:, :], wg_d[0]),
                   waits=[wo_cast[14]], inc=16)
    pre_u0 = kb.do("Y", lambda e: e.dma_start(wst[1][:, :], wu_d[0]),
                   waits=[wo_cast[15]], inc=16)
    kb.fence(skip=("GD",))

    wgu_free = [None, None]
    wgb_free = [None, None]
    wub_free = [None, None]
    sg_free = [None, None]
    uu_free = [None, None]
    hdn_toks = [None] * ITN
    pgu_free = {0: [], 1: []}
    gtok = utok = None
    for it in range(ITN):
        bx = it % 2
        if it == 0:
            tg_, sg_src = pre_g0, 0
        else:
            tg_ = kb.do("Y", (lambda i=it, b=bx: lambda e: e.dma_start(
                wst[b][:, :], wg_d[i]))(), waits=[wgu_free[bx]], inc=16)
            sg_src = bx
        cg = kb.do("S", (lambda b=bx, sc=sg_src: lambda e: e.copy(
            wgb[b][:, :, :], wst[sc][:, :]))(),
            waits=[tg_, wgb_free[bx]], inc=1)
        wgu_free[sg_src] = cg
        if it == 0:
            tu_, su_src = pre_u0, 1
        else:
            tu_ = kb.do("Y", (lambda i=it, b=bx: lambda e: e.dma_start(
                wst[b][:, :], wu_d[i]))(), waits=[wgu_free[bx]], inc=16)
            su_src = bx
        cu = kb.do("S", (lambda b=bx, sc=su_src: lambda e: e.copy(
            wub[b][:, :, :], wst[sc][:, :]))(),
            waits=[tu_, wub_free[bx]], inc=1)
        wgu_free[su_src] = cu
        pg, pu = (PS[0], PS[1]) if bx == 0 else (PS[2], PS[3])
        gtok = None
        for kc in range(KC):
            for ch_ in range(2):
                gtok = kb.do("T", (lambda p=pg, b=bx, k=kc, c=ch_ * 512:
                             lambda e: e.matmul(
                    p[:, c:c + 512], wgb[b][:, k, :], h2T[:, k, c:c + 512],
                    start=(k == 0), stop=(k == KC - 1)))(),
                    waits=[cg, last_evac] + pgu_free[bx],
                    inc=(1 if (kc == KC - 1 and ch_ == 1) else 0))
        wgb_free[bx] = gtok
        utok = None
        for kc in range(KC):
            for ch_ in range(2):
                utok = kb.do("T", (lambda p=pu, b=bx, k=kc, c=ch_ * 512:
                             lambda e: e.matmul(
                    p[:, c:c + 512], wub[b][:, k, :], h2T[:, k, c:c + 512],
                    start=(k == 0), stop=(k == KC - 1)))(),
                    waits=[cu],
                    inc=(1 if (kc == KC - 1 and ch_ == 1) else 0))
        wub_free[bx] = utok
        pgu_free[bx] = []
        tsg_ = kb.do("S", (lambda p=pg, b=bx, i=it: lambda e: e.activation(
            sg_b[b][:, :], p[:, :], ACT.Silu, scale=gsv[:, i:i + 1]))(),
            waits=[gtok, tgs, sg_free[bx]], inc=1)
        tuu = kb.do("V", (lambda p=pu, b=bx: lambda e: e.tensor_copy(
            uu_b[b][:, :], p[:, :]))(), waits=[utok, uu_free[bx]], inc=1)
        dst = hdna if it < 16 else hdnb
        di = it if it < 16 else it - 16
        extra = ([gat_done[H - 1]] if it < 16
                 else [sp_toks[NTO - 1], sp_toks[NTO - 2]])
        hdn_toks[it] = kb.do("V", (lambda dd=dst, i=di, b=bx: lambda e:
                             e.tensor_tensor(
            dd[:, i, :], sg_b[b][:, :], uu_b[b][:, :], ALU.mult))(),
            waits=[tsg_] + extra, inc=1)
        sg_free[bx] = hdn_toks[it]
        uu_free[bx] = hdn_toks[it]
        pgu_free[bx] = [tsg_, tuu, hdn_toks[it]]

    tds = kb.do("G", lambda e: e.dma_start(ds_b[:, :], ds_d),
                waits=[gtok, utok], inc=16, chan="GD")
    pre_dw = []
    for i in range(3):
        pre_dw.append(kb.do("Y", (lambda s=i: lambda e: e.dma_start(
            dwst[s][:, :], wd_d[0, s]))(), waits=[wgu_free[0]], inc=16))

    kb.fence(skip=("GD",))

    dwst_free = [None, None, None]
    dwb_free = [None, None]
    bank_free = {}
    ffo_toks = [None] * NTO
    pend_ffo = None

    def emit_ffo(gp, mml):
        for tt in range(NTO):
            pt = PS[tt // 2]
            col = (tt % 2) * 512
            ev = kb.do("V", (lambda p=pt, c=col, t=tt, g=gp: lambda e:
                       e.tensor_tensor(
                ffo[:, t, 512 * g:512 * g + 512], p[:, c:c + 512],
                ds_b[:, 512 * g:512 * g + 512], ALU.mult))(),
                waits=[mml, tds], inc=1)
            bank_free[(gp + 1, tt)] = ev
            if gp == 3:
                ffo_toks[tt] = ev

    for g4 in range(4):
        mmlast = None
        for ic in range(ITN):
            i3 = (g4 * ITN + ic) % 3
            b2 = (g4 * ITN + ic) % 2
            if g4 == 0 and ic < 3:
                td_ = pre_dw[ic]
            else:
                td_ = kb.do("Y", (lambda g=g4, i=ic, s=i3: lambda e:
                            e.dma_start(
                    dwst[s][:, :], wd_d[g, i]))(),
                    waits=[dwst_free[i3]], inc=16)
            cd = kb.do("V", (lambda s=i3, b=b2, i=ic: lambda e: e.tensor_scalar(
                dwb[b][:, :], dwst[s][:, :], usv[:, i:i + 1], None,
                ALU.mult))(), waits=[td_, tus, dwb_free[b2]], inc=1)
            dwst_free[i3] = cd
            if ic == 0 and pend_ffo is not None:
                emit_ffo(*pend_ffo)
                pend_ffo = None
            src = hdna if ic < 16 else hdnb
            si = ic if ic < 16 else ic - 16
            mt = None
            for tt in range(NTO):
                pt = PS[tt // 2]
                col = (tt % 2) * 512
                mt = kb.do("T", (lambda p=pt, c=col, s=src, j=si, t=tt, b=b2,
                           st_=(ic == 0), sp=(ic == ITN - 1): lambda e:
                           e.matmul(
                    p[:, c:c + 512], s[:, j, t * 128:(t + 1) * 128],
                    dwb[b][:, :], start=st_, stop=sp))(),
                    waits=[cd, hdn_toks[ic], bank_free.get((g4, tt))],
                    inc=(1 if tt == NTO - 1 else 0))
            dwb_free[b2] = mt
            mmlast = mt
        pend_ffo = (g4, mmlast)
    emit_ffo(*pend_ffo)
    pend_ffo = None

    kb.fence()

    x2r_free = [None, None]
    ost_free = [None, None]
    ty = None
    for tt in range(NTO):
        bx = tt % 2
        trx = kb.do("G", (lambda t=tt, b=bx: lambda e: e.dma_start(
            x2r[b][:, :], x2_d[t]))(), waits=[x2r_free[bx], sp_toks[tt]],
            inc=16, chan="GD")
        tr_, tn = ln_stats([ffo[:, tt, :]], [ffo_toks[tt]])
        ap_ = ln_apply(ln4o[:, :], ffo[:, tt, :], tn)
        to_ = kb.do("V", (lambda b=bx: lambda e: e.tensor_tensor(
            ost[b][:, :], x2r[b][:, :], ln4o[:, :], ALU.add))(),
            waits=[trx, ap_, ost_free[bx]], inc=1)
        x2r_free[bx] = to_
        ty = kb.do("Y", (lambda t=tt, b=bx: lambda e: e.dma_start(
            y_d[t], ost[b][:, :]))(), waits=[to_], inc=16)
        ost_free[bx] = ty

    final = [(s, s.val) for s in kb.ch["Y"].lanes + kb.ch["GD"].lanes
             if s.val > 0]
    kb.do("Y", lambda e: e.nop(), waits=final, inc=0)

    with nc.Block() as blk:
        blk.tensor(lambda e: kb.replay("T", e))
        blk.vector(lambda e: kb.replay("V", e))
        blk.scalar(lambda e: kb.replay("S", e))
        blk.sync(lambda e: kb.replay("Y", e))
        blk.gpsimd(lambda e: kb.replay("G", e))
    return nc, {e: len(kb.ops[e]) for e in kb.ops}


def _prep(L0_attn_x, L0_attn_q_w, L0_attn_k_w, L0_attn_v_w, L0_attn_o_w,
          L0_attn_gate_w, L0_attn_gate_b, L0_ffn_gate_w, L0_ffn_up_w,
          L0_ffn_down_w, L0_ffn_gate_s, L0_ffn_up_s, L0_ffn_down_s):
    f = np.float32
    asc = np.ascontiguousarray
    x = np.asarray(L0_attn_x, f)

    def lhs_blocks(w, nt):
        return asc(np.asarray(w, f).reshape(nt, 128, KC, 128)
                   .transpose(0, 3, 2, 1))

    wq = lhs_blocks(L0_attn_q_w, H)
    wk = lhs_blocks(L0_attn_k_w, H)
    wv = asc(np.asarray(L0_attn_v_w, f).reshape(NG, GH * 128, KC, 128)
             .transpose(0, 3, 2, 1))
    wo = asc(np.asarray(L0_attn_o_w, f).reshape(D, KC, 128).transpose(1, 2, 0))
    wg = lhs_blocks(L0_ffn_gate_w, ITN)
    wu = lhs_blocks(L0_ffn_up_w, ITN)
    wd = asc(np.asarray(L0_ffn_down_w, f).reshape(4, 512, ITN, 128)
             .transpose(0, 2, 3, 1))
    gwt = asc(np.asarray(L0_attn_gate_w, f).T)
    gb = asc(np.asarray(L0_attn_gate_b, f).reshape(128, 1))
    gs = asc(np.asarray(L0_ffn_gate_s, f).reshape(ITN, 128).T)
    us = asc(np.asarray(L0_ffn_up_s, f).reshape(ITN, 128).T)
    ds = asc(np.broadcast_to(np.asarray(L0_ffn_down_s, f), (128, D)))
    tri = asc(np.triu(np.ones((128, 128), BF)))
    trin = asc(np.triu(np.full((128, 128), -30000.0, BF), 1))
    one = np.ones((128, 1), BF)
    idn = np.eye(128, dtype=BF)
    eps = np.full((128, 1), EPS, f)
    onr = np.ones((1, 128), f)
    onrb = np.ones((1, 128), BF)
    shared = dict(wq_d=wq, wk_d=wk, wv_d=wv, wo_d=wo, wg_d=wg, wu_d=wu,
                  wd_d=wd, gwt_d=gwt, gb_d=gb, gs_d=gs, us_d=us, ds_d=ds,
                  tri_d=tri, one_d=one, idn_d=idn, eps_d=eps,
                  onr_d=onr, onrb_d=onrb, trin_d=trin)
    in_maps = []
    for c in range(8):
        b, par = c // 2, c % 2
        perm = ([2 * j + par for j in range(8)]
                + [2 * j + (1 - par) for j in range(8)])
        xl = asc(x[b].reshape(STN, 128, D)[perm])
        ot = (np.ones if par == 1 else np.zeros)((128, 128), BF)
        otn = (np.zeros((128, 128)) if par == 1
               else np.full((128, 128), -30000.0)).astype(BF)
        in_maps.append(dict(shared, x_l=xl, oth_d=asc(ot), othn_d=asc(otn)))
    return in_maps


def kernel(**inputs):
    if "nc" not in _CACHE:
        _CACHE["nc"] = build_program()[0]
    nc = _CACHE["nc"]
    in_maps = _prep(**inputs)
    res = run_bass_kernel_spmd(nc, in_maps, core_ids=list(range(8)))
    y = np.empty((BATCH, SEQ, D), np.float32)
    for c in range(8):
        b, par = c // 2, c % 2
        out = np.asarray(res.results[c]["y_d"])
        yv = y[b].reshape(STN, 128, D)
        for j in range(NTO):
            yv[2 * j + par] = out[j]
    return y


if __name__ == "__main__":
    nc, counts = build_program()
    print("instructions:", nc.n_instructions(), counts)


# version 29
# speedup vs baseline: 1.2052x; 1.1962x over previous
import numpy as np
import ml_dtypes
import concourse.bass as bass
import concourse.mybir as mybir
from concourse.bass_utils import run_bass_kernel_spmd

F32 = mybir.dt.float32
BF16 = mybir.dt.bfloat16
ALU = mybir.AluOpType
ACT = mybir.ActivationFunctionType
AXX = mybir.AxisListType.X
BF = ml_dtypes.bfloat16

D = 2048
SEQ = 2048
BATCH = 4
H = 16
HD = 128
KC = 16
II = 5632
ITN = 44
NTO = 8
STN = 16
GH = 2
NG = H // GH
EPS = 1e-5
CSC = 1.0 / float(np.sqrt(128.0))

_CACHE = {}


class Sub:
    __slots__ = ("eng", "sem", "val", "dma", "name")

    def __init__(self, eng, dma, name):
        self.eng = eng
        self.sem = None
        self.val = 0
        self.dma = dma
        self.name = name


class Ch:

    def __init__(self, eng, dma, nlanes, name):
        self.dma = dma
        self.lanes = [Sub(eng, dma, f"{name}{i}") for i in range(nlanes)]
        self.rr = 0

    def issue(self, amt):
        if self.dma:
            s = self.lanes[self.rr]
            self.rr = (self.rr + 1) % len(self.lanes)
        else:
            s = self.lanes[0]
        s.val += amt
        return s, (s, s.val)


class KB:

    ENGS = ("T", "V", "S", "Y", "G")

    def __init__(self):
        self.ch = {
            "T": Ch("T", False, 1, "cT"), "V": Ch("V", False, 1, "cV"),
            "S": Ch("S", False, 1, "cS"), "Y": Ch("Y", True, 8, "cY"),
            "G": Ch("G", False, 1, "cG"), "GD": Ch("G", True, 4, "cD"),
        }
        self.ops = {e: [] for e in self.ENGS}
        self.waited = {}
        self.pending = {e: [] for e in self.ENGS}

    def all_subs(self):
        for c in self.ch.values():
            yield from c.lanes

    def do(self, eng, fn, waits=(), inc=0, chan=None):
        extra = []
        tok_out = None
        sub = None
        if inc:
            ch = self.ch[chan or eng]
            if ch.dma:
                lane = ch.lanes[ch.rr]
                if lane.val > 0:
                    extra.append((lane, lane.val))
            sub, tok_out = ch.issue(inc)
        wl = []
        for tok in list(waits) + self.pending[eng] + extra:
            if tok is None:
                continue
            c, v = tok
            if c.eng == eng and not c.dma:
                continue
            key = (eng, id(c))
            if self.waited.get(key, 0) >= v:
                continue
            self.waited[key] = v
            wl.append((c, v))
        self.pending[eng] = []
        self.ops[eng].append((wl, fn, sub, inc))
        return tok_out

    def fence(self, engs=None, skip=()):
        ex = set()
        for c in skip:
            ex.update(id(l) for l in self.ch[c].lanes)
        snap = [(s, s.val) for s in self.all_subs()
                if s.val > 0 and id(s) not in ex]
        for e in engs or self.ENGS:
            self.pending[e].extend(snap)

    def replay(self, eng, engobj):
        first = True
        for wl, fn, sub, amt in self.ops[eng]:
            if eng == "V" and not first:
                engobj.drain()
            first = False
            for c, v in wl:
                engobj.wait_ge(c.sem, v)
            ins = fn(engobj)
            if sub is not None:
                ins.then_inc(sub.sem, amt)


def build_program(dbg=False):
    nc = bass.Bass("TRN2", target_bir_lowering=False, debug=False,
               detect_race_conditions=False)
    kb = KB()

    dt = nc.dram_tensor
    x_l = dt("x_l", [STN, 128, D], F32, kind="ExternalInput").ap()
    wq_d = dt("wq_d", [H, 128, KC, 128], F32, kind="ExternalInput").ap()
    wk_d = dt("wk_d", [H, 128, KC, 128], F32, kind="ExternalInput").ap()
    wv_d = dt("wv_d", [NG, 128, KC, GH * 128], F32, kind="ExternalInput").ap()
    wo_d = dt("wo_d", [KC, 128, D], F32, kind="ExternalInput").ap()
    wg_d = dt("wg_d", [ITN, 128, KC, 128], F32, kind="ExternalInput").ap()
    wu_d = dt("wu_d", [ITN, 128, KC, 128], F32, kind="ExternalInput").ap()
    wd_d = dt("wd_d", [4, ITN, 128, 512], F32, kind="ExternalInput").ap()
    gwt_d = dt("gwt_d", [128, 128], F32, kind="ExternalInput").ap()
    gb_d = dt("gb_d", [128, 1], F32, kind="ExternalInput").ap()
    gs_d = dt("gs_d", [128, ITN], F32, kind="ExternalInput").ap()
    us_d = dt("us_d", [128, ITN], F32, kind="ExternalInput").ap()
    ds_d = dt("ds_d", [128, D], F32, kind="ExternalInput").ap()
    tri_d = dt("tri_d", [128, 128], BF16, kind="ExternalInput").ap()
    oth_d = dt("oth_d", [128, 128], BF16, kind="ExternalInput").ap()
    trin_d = dt("trin_d", [128, 128], BF16, kind="ExternalInput").ap()
    othn_d = dt("othn_d", [128, 128], BF16, kind="ExternalInput").ap()
    one_d = dt("one_d", [128, 1], BF16, kind="ExternalInput").ap()
    idn_d = dt("idn_d", [128, 128], BF16, kind="ExternalInput").ap()
    eps_d = dt("eps_d", [128, 1], F32, kind="ExternalInput").ap()
    onr_d = dt("onr_d", [1, 128], F32, kind="ExternalInput").ap()
    onrb_d = dt("onrb_d", [1, 128], BF16, kind="ExternalInput").ap()
    y_d = dt("y_d", [NTO, 128, D], F32, kind="ExternalOutput").ap()
    if dbg:
        dbg_ao = dt("dbg_ao", [128, D], F32, kind="ExternalOutput").ap()
        dbg_ln = dt("dbg_ln", [128, D], F32, kind="ExternalOutput").ap()
        dbg_mv = dt("dbg_mv", [128, 3], F32, kind="ExternalOutput").ap()
        dbg_st = dt("dbg_st", [128, 24], F32, kind="ExternalOutput").ap()
        dbg_xo = dt("dbg_xo", [128, D], F32, kind="ExternalOutput").ap()
        dbg_h1T = dt("dbg_h1T", [128, KC, SEQ], BF16, kind="ExternalOutput").ap()
        dbg_kT = dt("dbg_kT", [128, GH, SEQ], BF16, kind="ExternalOutput").ap()
        dbg_qT = dt("dbg_qT", [128, GH, 1024], BF16, kind="ExternalOutput").ap()
        dbg_vg = dt("dbg_vg", [128, STN, GH * 128], BF16, kind="ExternalOutput").ap()
        dbg_gat = dt("dbg_gat", [128, KC, 1024], BF16, kind="ExternalOutput").ap()
        dbg_Bb = dt("dbg_Bb", [1, 1024], BF16, kind="ExternalOutput").ap()
        dbg_p = dt("dbg_p", [128, 1024], BF16, kind="ExternalOutput").ap()
    x2_d = dt("x2_d", [NTO, 128, D], F32,
              kind="ExternalOutput" if dbg else "Internal").ap()

    PS = [nc.alloc_psum_tensor(f"ps{i}", [128, 1024], F32) for i in range(4)]

    base = 16512

    def sb(name, shape, dty, at):
        return nc.alloc_sbuf_tensor_at(name, shape, dty, offset=base + at)

    S1, S2, S3, S4 = 0, 65536, 155648, 188416
    KS = 204800
    h1T = sb("h1T", [128, KC, SEQ], BF16, S1)
    wo_b = sb("wo_b", [128, KC, D], BF16, S1)
    sg_b = [sb(f"sg{i}", [128, 1024], BF16, S1 + i * 2048) for i in range(2)]
    uu_b = [sb(f"uu{i}", [128, 1024], BF16, S1 + 4096 + i * 2048) for i in range(2)]
    wgb = [sb(f"wgb{i}", [128, KC, 128], BF16, S1 + 8192 + i * 4096) for i in range(2)]
    wub = [sb(f"wub{i}", [128, KC, 128], BF16, S1 + 16384 + i * 4096) for i in range(2)]
    ffo = sb("ffo", [128, NTO, D], F32, S1)
    xst = [sb(f"xst{i}", [128, D], F32, S3 + i * 8192) for i in range(2)]
    h1t = [sb(f"h1t{i}", [128, D], BF16, S3 + 16384 + i * 4096) for i in range(2)]
    kT = sb("kT", [128, GH, SEQ], BF16, S2 + 0)
    vg = sb("vg", [128, STN, GH * 128], BF16, S2 + 8192)
    qT = sb("qT", [128, GH, 1024], BF16, S2 + 16384)
    wq_s = [sb(f"wq_s{i}", [128, GH, KC, 128], BF16, S2 + 20480 + 8192 * i)
            for i in range(2)]
    wk_s = [sb(f"wk_s{i}", [128, GH, KC, 128], BF16, S2 + 36864 + 8192 * i)
            for i in range(2)]
    wvb = [sb(f"wvb{i}", [128, KC, GH * 128], BF16, S2 + 53248 + 8192 * i)
           for i in range(2)]
    p_b = [sb(f"p{i}", [128, 1024], BF16, S2 + 69632 + i * 2048) for i in range(4)]
    sigT = sb("sigT", [128, 1024], BF16, S2 + 77824)
    t1 = sb("t1", [128, 1024], F32, S2 + 79872)
    Brow2 = [sb(f"Brow{i}", [1, 1024], BF16, S2 + 83968 + 2048 * i)
             for i in range(2)]
    rcb = sb("rcb", [128, 1024], BF16, S2 + 88064)
    rrow = sb("rrow", [1, 1024], F32, S2 + 79872)
    ln2o = sb("ln2o", [128, D], F32, S2 + 0)
    h2t = [sb(f"h2t{i}", [128, D], BF16, S2 + 8192 + i * 4096) for i in range(2)]
    xow = [sb(f"xow{i}", [128, D], F32, S2 + 16384 + i * 8192) for i in range(2)]
    x2t = [sb(f"x2t{i}", [128, D], F32, S2 + 32768 + i * 8192) for i in range(2)]
    h2T = sb("h2T", [128, KC, 1024], BF16, S2 + 57344)
    hdnb = sb("hdnb", [128, ITN - 16, 1024], BF16, S2 + 0)
    ds_b = sb("ds_b", [128, D], F32, S2 + 57344)
    ln4o = sb("ln4o", [128, D], F32, S2 + 0)
    x2r = [sb(f"x2r{i}", [128, D], F32, S2 + 16384 + i * 8192) for i in range(2)]
    ost = [sb(f"ost{i}", [128, D], F32, S2 + 32768 + i * 8192) for i in range(2)]
    gat = sb("gat", [128, KC, 1024], BF16, S3)
    hdna = sb("hdna", [128, 16, 1024], BF16, S3)
    wst = [sb(f"wst{i}", [128, D], F32, S4 + i * 8192) for i in range(2)]
    dwst = [sb(f"dwst{i}", [128, 512], F32, S4 + i * 2048) for i in range(3)]
    dwb = [sb(f"dwb{i}", [128, 512], BF16, S4 + 6144 + i * 1024) for i in range(2)]
    ko = KS
    tri = sb("tri", [128, 128], BF16, ko); ko += 256
    trin = sb("trin", [128, 128], BF16, ko); ko += 256
    othn = sb("othn", [128, 128], BF16, ko); ko += 256
    oth = sb("oth", [128, 128], BF16, ko); ko += 256
    idn = sb("idn", [128, 128], BF16, ko); ko += 256
    onec = sb("onec", [128, 1], BF16, ko); ko += 32
    gwt = sb("gwt", [128, 128], F32, ko); ko += 512
    gwb = sb("gwb", [128, 128], BF16, ko); ko += 256
    gbv = sb("gbv", [128, 1], F32, ko); ko += 32
    gbh = sb("gbh", [128, 1], F32, ko); ko += 32
    gsv = sb("gsv", [128, ITN], F32, ko); ko += 192
    usv = sb("usv", [128, ITN], F32, ko); ko += 192
    st24 = sb("st24", [128, 24], F32, ko); ko += 96
    mv2 = sb("mv2", [128, 2], F32, ko); ko += 32
    sqv = sb("sqv", [128, 1], F32, ko); ko += 32
    rstd = sb("rstd", [128, 1], F32, ko); ko += 32
    ngm = sb("ngm", [128, 1], F32, ko); ko += 32
    km = sb("km", [128, 1], BF16, ko); ko += 32
    kmb = sb("kmb", [128, 1], BF16, ko); ko += 32
    epsv = sb("epsv", [128, 1], F32, ko); ko += 32
    mv2d = sb("mv2d", [128, 4], F32, ko); ko += 64
    ko = (ko + 31) // 32 * 32
    sgt = sb("sgt", [128, 1024], F32, ko)
    absq = sb("absq", [128, 1024], BF16, ko)
    ko += 4096
    onr = sb("onr", [1, 128], F32, ko); ko += 512
    onrb = sb("onrb", [1, 128], BF16, ko); ko += 256
    assert base + ko <= 229344

    for s in kb.all_subs():
        s.sem = nc.alloc_semaphore(name=s.name)

    def ln_stats(src_aps, tok_ready):
        tread = None
        i = 0
        for ap in src_aps:
            w = ap.shape[-1]
            for j in range(w // 512):
                tread = kb.do(
                    "V",
                    (lambda a=ap, jj=j, ii=i: lambda e: e.bn_stats(
                        st24[:, 6 * ii:6 * ii + 6],
                        a[:, 512 * jj:512 * jj + 512]))(),
                    waits=list(tok_ready), inc=1)
                i += 1
        assert i == 4
        tag = kb.do("V", lambda e: e.bn_aggr(mv2[:, :], st24[:, :]), inc=1)
        ts_ = kb.do("S", lambda e: e.activation(sqv[:, :], mv2[:, 1:2], ACT.Sqrt,
                                                bias=epsv[:, 0:1]),
                    waits=[tag, tok_eps], inc=1)
        tn = kb.do("V", lambda e: e.reciprocal(rstd[:, :], sqv[:, :]),
                   waits=[ts_], inc=1)
        return tread, tn

    def ln_apply(dst, src, tok_n, waits=()):
        return kb.do("V", lambda e: e.tensor_scalar(
            dst, src, mv2[:, 0:1], rstd[:, 0:1], ALU.subtract, ALU.mult),
            waits=[tok_n] + list(waits), inc=1)

    tok_idn = kb.do("Y", lambda e: e.dma_start(idn[:, :], idn_d), inc=16)
    tok_gw = kb.do("Y", lambda e: e.dma_start(gwt[:, :], gwt_d), inc=16)
    tok_gb = kb.do("Y", lambda e: e.dma_start(gbv[:, :], gb_d), inc=16)
    kb.do("Y", lambda e: e.dma_start(tri[:, :], tri_d), inc=16)
    kb.do("Y", lambda e: e.dma_start(oth[:, :], oth_d), inc=16)
    kb.do("Y", lambda e: e.dma_start(trin[:, :], trin_d), inc=16)
    kb.do("Y", lambda e: e.dma_start(othn[:, :], othn_d), inc=16)
    kb.do("Y", lambda e: e.dma_start(onec[:, :], one_d), inc=16)
    tok_eps = kb.do("Y", lambda e: e.dma_start(epsv[:, :], eps_d), inc=16)
    kb.do("Y", lambda e: e.dma_start(onr[:, :], onr_d), inc=16)
    kb.do("Y", lambda e: e.dma_start(onrb[:, :], onrb_d), inc=16)
    tok_gwb = kb.do("V", lambda e: e.tensor_copy(gwb[:, :], gwt[:, :]),
                    waits=[tok_gw], inc=1)
    kb.do("V", lambda e: e.tensor_scalar(gbh[:, :], gbv[:, :], 0.5, None,
                                         ALU.mult), waits=[tok_gb], inc=0)

    wst_free = [None, None]
    wq_free = [None, None]
    wk_free = [None, None]
    wvb_free = [None, None]

    wcast = {}

    def load_group_w(g):
        if g >= NG:
            return
        p2 = g % 2
        toks = []
        for hl in range(GH):
            h = g * GH + hl
            bx = hl % 2
            tq = kb.do("Y", (lambda hh=h, b=bx: lambda e: e.dma_start(
                wst[b][:, :], wq_d[hh]))(), waits=[wst_free[bx]], inc=16)
            cq = kb.do("S", (lambda b=bx, p=p2, hh=hl: lambda e: e.copy(
                wq_s[p][:, hh, :, :], wst[b][:, :]))(),
                waits=[tq, wq_free[p2]], inc=1)
            wst_free[bx] = cq
            tk = kb.do("Y", (lambda hh=h, b=bx: lambda e: e.dma_start(
                wst[b][:, :], wk_d[hh]))(), waits=[wst_free[bx]], inc=16)
            ck = kb.do("S", (lambda b=bx, p=p2, hh=hl: lambda e: e.copy(
                wk_s[p][:, hh, :, :], wst[b][:, :]))(),
                waits=[tk, wk_free[p2]], inc=1)
            wst_free[bx] = ck
            toks.append((cq, ck))
        cv = None
        for half in range(2):
            bx = half
            tv = kb.do("Y", (lambda gg=g, hf=half, b=bx: lambda e: e.dma_start(
                wst[b][:, :], wv_d[gg, :, 8 * hf:8 * hf + 8, :]))(),
                waits=[wst_free[bx]], inc=16)
            cv = kb.do("S", (lambda p=p2, hf=half, b=bx: lambda e: e.copy(
                wvb[p][:, 8 * hf:8 * hf + 8, :], wst[b][:, :]))(),
                waits=[tv, wvb_free[p2]], inc=1)
            wst_free[bx] = cv
        wcast[g] = (toks, cv)

    load_group_w(0)

    tr_evac = [None, None]
    xst_free = [None, None]
    h1T_done = []
    h1t_free = [None, None]
    pend_ev = None

    def flush_ev():
        nonlocal pend_ev
        pp, pi, ptk = pend_ev
        ev = kb.do("V", (lambda p=pp, i=pi: lambda e: e.tensor_copy(
            h1T[:, :, i * 128:(i + 1) * 128], p[:, :].bitcast(BF16)))(),
            waits=[ptk], inc=1)
        tr_evac[pi % 2] = ev
        h1T_done.append(ev)
        pend_ev = None
        return ev

    for idx in range(STN):
        bx = idx % 2
        tok_x = kb.do("Y", (lambda i=idx, b=bx: lambda e: e.dma_start(
            xst[b][:, :], x_l[i]))(), waits=[xst_free[bx]], inc=16)
        tr_ = None
        for j in range(4):
            tr_ = kb.do("V", (lambda b=bx, jj=j: lambda e: e.bn_stats(
                st24[:, 6 * jj:6 * jj + 6],
                xst[b][:, 512 * jj:512 * jj + 512]))(),
                waits=[tok_x], inc=(1 if j == 3 else 0))
        tag = kb.do("V", lambda e: e.bn_aggr(mv2[:, :], st24[:, :]), inc=1)
        ts_ = kb.do("S", lambda e: e.activation(sqv[:, :], mv2[:, 1:2],
                                                ACT.Sqrt, bias=epsv[:, 0:1]),
                    waits=[tag, tok_eps], inc=1)
        tn = kb.do("V", lambda e: e.reciprocal(rstd[:, :], sqv[:, :]),
                   waits=[ts_], inc=1)
        ta = kb.do("V", (lambda b=bx: lambda e: e.tensor_scalar(
            h1t[b][:, :], xst[b][:, :], mv2[:, 0:1], rstd[:, 0:1],
            ALU.subtract, ALU.mult))(),
            waits=[tn, h1t_free[bx]], inc=1)
        xst_free[bx] = ta
        pt = PS[2 + idx % 2]
        ttok = None
        for kc in range(KC):
            ttok = kb.do("T", (lambda p=pt, b=bx, k=kc: lambda e: e.transpose(
                p[:, 64 * k:64 * k + 64].bitcast(BF16),
                h1t[b][:, k * 128:(k + 1) * 128], idn[:, :]))(),
                waits=[ta, tr_evac[idx % 2], tok_idn],
                inc=(1 if kc == KC - 1 else 0))
        h1t_free[bx] = ttok
        ev = kb.do("S", (lambda p=pt, i=idx: lambda e: e.copy(
            h1T[:, :, i * 128:(i + 1) * 128], p[:, :].bitcast(BF16)))(),
            waits=[ttok], inc=1)
        tr_evac[idx % 2] = ev
        h1T_done.append(ev)
    h1T_ready = ev
    if dbg:
        kb.do("Y", lambda e: e.dma_start(dbg_h1T, h1T[:, :, :]),
              waits=[h1T_ready], inc=16)

    kb.fence()

    kT_free = [None] * GH
    qT_free = [None] * GH
    vg_free = None
    gat_done = [None] * H
    absq_free = None
    Brow_free = None
    Bb_free = []
    kmb_free = None
    sigT_free = None
    sgt_free = None
    rrow_free = None
    rcb_free = None
    h1T_last_read = None
    t_last = {0: None, 1: None, 2: None, 3: None}
    tok_avl = None
    pend_epi = None
    pend_b = None
    tBr2 = [None, None]
    eqh = [None, None]
    tkm2 = [None, None]
    tabs2 = [None, None]
    kmh = [km, kmb]
    kmb_free2 = [None, None]

    for g in range(NG):
        p2 = g % 2
        wq_cast = [wcast[g][0][hl][0] for hl in range(GH)]
        wk_cast = [wcast[g][0][hl][1] for hl in range(GH)]
        vcast = wcast[g][1]
        if pend_epi is not None:
            epi_a_prev, epi_b_prev = pend_epi
            pend_bp = (lambda f=epi_b_prev, a=epi_a_prev(): lambda: f(ab=a))()
            pend_epi = None
        else:
            pend_bp = None
        for hl in range(GH):
            h = g * GH + hl
            bx = hl % 2
            ktok = None
            for ch_ in range(4):
                pt = PS[0] if ch_ < 2 else PS[1]
                col = (ch_ % 2) * 512
                for kc in range(KC):
                    ktok = kb.do("T", (lambda p=pt, c=col, k=kc, cc=ch_,
                                 pp=p2, hh=hl: lambda e: e.matmul(
                        p[:, c:c + 512], wk_s[pp][:, hh, k, :],
                        h1T[:, k, cc * 512:cc * 512 + 512],
                        start=(k == 0), stop=(k == KC - 1)))(),
                        waits=[wk_cast[hl],
                               h1T_done[4 * ch_ + 3] if g == 0
                               else h1T_ready,
                               t_last[0 if ch_ < 2 else 1]],
                        inc=(1 if (kc == KC - 1 and ch_ == 3) else 0))
                if ch_ == 0 and hl == 0 and pend_bp is not None:
                    pend_bp()
                    pend_bp = None
            ek = None
            for half in range(2):
                ek = kb.do("V", (lambda hh=hl, hf=half: lambda e: e.tensor_copy(
                    kT[:, hh, hf * 1024:hf * 1024 + 1024], PS[hf][:, :]))(),
                    waits=[ktok, kT_free[hl]], inc=1)
            t_last[0] = ek
            t_last[1] = ek
            tkm2[hl] = kb.do("V", (lambda hh=hl: lambda e: e.reduce_max(
                kmh[hh][:, :], kT[:, hh, :], AXX,
                apply_absolute_value=True))(),
                waits=[kmb_free2[hl], ek], inc=1)
            qp = PS[2 + hl]
            qtok = None
            for ch_ in range(2):
                for kc in range(KC):
                    qtok = kb.do("T", (lambda p=qp, c=ch_ * 512, k=kc,
                                 pp=p2, hh=hl: lambda e: e.matmul(
                        p[:, c:c + 512], wq_s[pp][:, hh, k, :],
                        h1T[:, k, c:c + 512],
                        start=(k == 0), stop=(k == KC - 1)))(),
                        waits=[wq_cast[hl], t_last[2 + hl]],
                        inc=(1 if (kc == KC - 1 and ch_ == 1) else 0))
            eq = kb.do("V", (lambda hh=hl, p=qp: lambda e: e.tensor_copy(
                qT[:, hh, :], p[:, :]))(), waits=[qtok, qT_free[hl]], inc=1)
            t_last[2 + hl] = eq
            eqh[hl] = eq
            if hl == 0:
                tabs2[0] = kb.do("S", lambda e: e.activation(
                    absq[:, :], qT[:, 0, :], ACT.Abs, scale=1.01),
                    waits=[absq_free, sgt_free, eq], inc=1)
            wq_free[p2] = qtok
            wk_free[p2] = ktok
            h1T_last_read = qtok
        if dbg and g == 0:
            kb.do("Y", lambda e: e.dma_start(dbg_kT, kT[:, :, :]),
                  waits=[t_last[1], ek], inc=16)
            kb.do("Y", lambda e: e.dma_start(dbg_qT, qT[:, :, :]),
                  waits=[eq], inc=16)

        tBr2 = [None, None]
        for hl in range(GH):
            if tabs2[hl] is None:
                tabs2[hl] = kb.do("S", (lambda hh=hl: lambda e: e.activation(
                    absq[:, :], qT[:, hh, :], ACT.Abs, scale=1.01))(),
                    waits=[absq_free, sgt_free, eqh[hl]], inc=1)
            btok = None
            for ch_ in range(2):
                btok = kb.do("T", (lambda c=ch_ * 512, hh=hl: lambda e:
                             e.matmul(
                    PS[2 + hh][0:1, c:c + 512], kmh[hh][:, 0:1],
                    absq[:, c:c + 512],
                    start=True, stop=True))(),
                    waits=[tkm2[hl], tabs2[hl], t_last[2 + hl]],
                    inc=(1 if ch_ == 1 else 0))
            kmb_free2[hl] = btok
            absq_free = btok
            tBr2[hl] = kb.do("V", (lambda hh=hl: lambda e: e.tensor_scalar(
                Brow2[hh][:, :], PS[2 + hh][0:1, :], -1.0, None, ALU.mult))(),
                waits=[btok, Brow_free], inc=1)
            if dbg and g == 0 and hl == 0:
                kb.do("Y", (lambda hh=hl: lambda e: e.dma_start(
                    dbg_Bb, Brow2[hh][:, :]))(), waits=[tBr2[hl]], inc=16)
        t_last[2] = tBr2[0]
        t_last[3] = tBr2[1]
        tabs2 = [None, None]
        load_group_w(g + 1)

        vev = None
        for sl in range(STN):
            pt = PS[0]
            col = (sl % 2) * 512
            vtok = None
            for kc in range(KC):
                vtok = kb.do("T", (lambda p=pt, c=col, k=kc, s=sl, pp=p2:
                             lambda e: e.matmul(
                    p[:, c:c + GH * 128], h1T[:, k, s * 128:(s + 1) * 128],
                    wvb[pp][:, k, :], start=(k == 0), stop=(k == KC - 1)))(),
                    waits=[vcast, t_last[0], h1T_ready],
                    inc=(1 if kc == KC - 1 else 0))
            vev = kb.do("V", (lambda p=pt, c=col, s=sl: lambda e: e.tensor_copy(
                vg[:, s, :], p[:, c:c + GH * 128]))(),
                waits=[vtok, vg_free], inc=1)
            if sl % 2 == 1:
                t_last[0] = vev
        t_last[0] = vev
        wvb_free[p2] = vtok
        h1T_last_read = vtok
        if dbg and g == 0:
            kb.do("Y", lambda e: e.dma_start(dbg_vg, vg[:, :, :]),
                  waits=[vev], inc=16)

        for hl in range(GH):
            h = g * GH + hl
            Brow = Brow2[hl]
            tBr = tBr2[hl]
            p_free = [None, None, None, None]
            av_first = {0: True, 1: True}
            l_first = {0: True, 1: True}
            av_rem = {0: 8, 1: 16}
            l_rem = {0: 8, 1: 16}
            sT_free = [t_last[0], t_last[1]]
            tcount = 0
            pendq = []

            def emit_avl(hh):
                nonlocal tok_avl
                bxp, slp, chp, ptokp = pendq.pop(0)
                for (c0, c1) in chp:
                    bank = 0 if c0 < 512 else 1
                    av_rem[bank] -= 1
                    kb.do("T", (lambda b=bxp, s=slp, a=c0, z=c1,
                                st_=av_first[bank],
                                sp=(av_rem[bank] == 0): lambda e: e.matmul(
                        PS[2][:, a:z], vg[:, s, hh * 128:(hh + 1) * 128],
                        p_b[b][:, a:z], start=st_, stop=sp))(),
                        waits=[ptokp, t_last[2]], inc=0)
                    av_first[bank] = False
                for (c0, c1) in chp:
                    bank = 0 if c0 < 512 else 1
                    l_rem[bank] -= 1
                    tok_avl = kb.do("T", (lambda b=bxp, a=c0, z=c1,
                                    st_=l_first[bank],
                                    sp=(l_rem[bank] == 0): lambda e:
                                    e.matmul(
                        PS[3][0:1, a:z], onec[:, :], p_b[b][:, a:z],
                        start=st_, stop=sp))(),
                        waits=[t_last[3], tBr],
                        inc=(1 if (c0, c1) == chp[-1] else 0))
                    l_first[bank] = False
                p_free[bxp] = tok_avl

            for i in range(8):
                ts0 = i * 128
                chunks = ([(ts0, 512), (512, 1024)] if i < 4
                          else [(ts0, 1024)])
                for half in range(2):
                    sl = i + 8 * half
                    bx = tcount % 2
                    pbx = tcount % 4
                    tcount += 1
                    stok = None
                    for (c0, c1) in chunks:
                        kb.do("T", (lambda b=bx, a=c0, z=c1, br=Brow:
                              lambda e: e.matmul(
                            PS[b][:, a:z], onrb[0:1, :], br[0:1, a:z],
                            start=True, stop=False))(),
                            waits=[sT_free[bx], tBr], inc=0)
                    mk = trin if half == 0 else othn
                    kb.do("T", (lambda b=bx, a=ts0, m=mk: lambda e: e.matmul(
                        PS[b][:, a:a + 128], m[:, :], idn[:, :],
                        start=False, stop=False))(), inc=0)
                    for (c0, c1) in chunks:
                        stok = kb.do("T", (lambda b=bx, hh=hl, s=sl, a=c0,
                                     z=c1: lambda e: e.matmul(
                            PS[b][:, a:z], kT[:, hh, s * 128:(s + 1) * 128],
                            qT[:, hh, a:z], start=False, stop=True))(),
                            waits=[],
                            inc=(1 if (c0, c1) == chunks[-1] else 0))
                    if len(pendq) == 3:
                        emit_avl(hl)
                    ptok = kb.do("S", (lambda b=pbx, bb=bx, a=ts0: lambda e:
                                 e.activation(
                        p_b[b][:, a:1024], PS[bb][:, a:1024], ACT.Exp,
                        scale=CSC))(),
                        waits=[stok, p_free[pbx]], inc=1)
                    sT_free[bx] = ptok
                    if dbg and g == 0 and hl == 0 and i == 0 and half == 0:
                        ptok = kb.do("Y", (lambda b=pbx: lambda e:
                                     e.dma_start(
                            dbg_p, p_b[b][:, :]))(), waits=[ptok], inc=16)
                    pendq.append((pbx, sl, chunks, ptok))
                    if tcount == 1 and pend_b is not None:
                        pend_b()
                        pend_b = None
            while pendq:
                emit_avl(hl)
            Brow_free = tok_avl
            def epi_a(hl=hl, h=h, sT0=sT_free[0], sT1=sT_free[1],
                      tok_avl_=tok_avl):
                nonlocal rrow_free
                gtok = None
                for ch_ in range(2):
                    gtok = kb.do("T", (lambda hh=hl, c=ch_ * 512: lambda e:
                                 e.matmul(
                        PS[0][:, c:c + 512], gwb[:, :], qT[:, hh, c:c + 512],
                        start=True, stop=True))(),
                        waits=[tok_gwb, sT0, sT1],
                        inc=(1 if ch_ == 1 else 0))
                tsg1 = kb.do("S", lambda e: e.activation(
                    sgt[:, :], PS[0][:, :], ACT.Tanh, bias=gbh[:, 0:1],
                    scale=0.5), waits=[gtok, sgt_free], inc=1)
                trc = kb.do("V", lambda e: e.reciprocal(
                    rrow[:, :], PS[3][0:1, :]),
                    waits=[tok_avl_, rrow_free], inc=1)
                t_last[0] = tsg1
                t_last[1] = sT1
                qT_free[hl] = gtok
                return tsg1, trc

            def epi_b(hl=hl, h=h, tok_avl_=tok_avl, ab=None):
                nonlocal rrow_free, rcb_free, sigT_free, sgt_free
                tsg1, trc = ab
                rct = None
                for ch_ in range(2):
                    rct = kb.do("T", (lambda c=ch_ * 512: lambda e: e.matmul(
                        PS[3][:, c:c + 512], onr[0:1, :],
                        rrow[0:1, c:c + 512],
                        start=True, stop=True))(), waits=[trc],
                        inc=(1 if ch_ == 1 else 0))
                rrow_free = rct
                trcb = kb.do("V", lambda e: e.tensor_copy(
                    rcb[:, :], PS[3][:, :]), waits=[rct], inc=1)
                t_last[3] = trcb
                tt1 = kb.do("V", lambda e: e.tensor_tensor(
                    t1[:, :], PS[2][:, :], rcb[:, :], ALU.mult),
                    waits=[tok_avl_, trcb], inc=1)
                t_last[2] = tt1
                rcb_free = tt1
                tsig = kb.do("V", lambda e: e.tensor_scalar(
                    sigT[:, :], sgt[:, :], 0.5, 0.5, ALU.mult, ALU.add),
                    waits=[tsg1, sigT_free], inc=1)
                sgt_free = tsig
                gat_done[h] = kb.do("V", (lambda hh=h: lambda e:
                                    e.tensor_tensor(
                    gat[:, hh, :], t1[:, :], sigT[:, :], ALU.mult))(),
                    waits=[tsig], inc=1)
                sigT_free = gat_done[h]

            if hl == 0:
                pend_b = (lambda f=epi_b, a=epi_a(): lambda: f(ab=a))()
                kT_free[hl] = sT_free[1]
            else:
                pend_epi = (epi_a, epi_b)
                kT_free[hl] = sT_free[1]
                t_last[0] = sT_free[0]
                t_last[1] = sT_free[1]
        vg_free = tok_avl
    if pend_epi is not None:
        _ea, _eb = pend_epi
        _eb(ab=_ea())
        pend_epi = None

    if dbg:
        gat_done[H - 1] = kb.do("Y", lambda e: e.dma_start(
            dbg_gat, gat[:, :, :]), waits=[gat_done[H - 1]], inc=16)

    wo_cast = [None] * KC
    for kc in range(KC):
        bx = kc % 2
        two = kb.do("Y", (lambda k=kc, b=bx: lambda e: e.dma_start(
            wst[b][:, :], wo_d[k]))(), waits=[wst_free[bx]], inc=16)
        wo_cast[kc] = kb.do("S", (lambda k=kc, b=bx: lambda e: e.copy(
            wo_b[:, k, :], wst[b][:, :]))(),
            waits=[two, h1T_last_read], inc=1)
        wst_free[bx] = wo_cast[kc]

    kb.fence(engs=["Y", "G"])
    xo_free = [None, None]
    x2_free = [None, None]
    h2t_free = [None, None]
    sp_toks = [None] * NTO
    pA_free = [t_last[0], t_last[1], vg_free]
    pB_free = [t_last[2], t_last[3]]
    h2T_evac = None
    last_evac = None
    pend4 = None

    def emit_tr4():
        nonlocal h2T_evac, last_evac, pA_free, pB_free
        pairp, bxp, ttp, ap3p, tr3p = pend4
        ttok = None
        for kc in range(KC):
            ttok = kb.do("T", (lambda p=pairp[0], b=bxp, k=kc: lambda e:
                         e.transpose(
                p[:, 64 * k:64 * k + 64].bitcast(BF16),
                h2t[b][:, k * 128:(k + 1) * 128], idn[:, :]))(),
                waits=[ap3p, tr3p],
                inc=(1 if kc == KC - 1 else 0))
        h2t_free[bxp] = ttok
        ev = kb.do("V", (lambda p=pairp[0], t=ttp: lambda e: e.tensor_copy(
            h2T[:, :, t * 128:(t + 1) * 128], p[:, :].bitcast(BF16)))(),
            waits=[ttok], inc=1)
        h2T_evac = ev
        last_evac = ev
        if bxp == 0:
            pA_free = [ev]
        else:
            pB_free = [ev]

    for tt in range(NTO):
        bx = tt % 2
        pair = [(PS[0], PS[1]), (PS[2], PS[3])][bx]
        txo = kb.do("Y", (lambda t=tt, b=bx: lambda e: e.dma_start(
            xow[b][:, :], x_l[t]))(), waits=[xo_free[bx]], inc=16)
        otok = None
        ofree = (pA_free if bx == 0 else pB_free) + [h2T_evac]
        for kc in range(KC):
            for ech in range(4):
                pt = pair[ech // 2]
                col = (ech % 2) * 512
                otok = kb.do("T", (lambda p=pt, c=col, k=kc, t=tt, q=ech:
                             lambda e: e.matmul(
                    p[:, c:c + 512], gat[:, k, t * 128:(t + 1) * 128],
                    wo_b[:, k, 512 * q:512 * q + 512],
                    start=(k == 0), stop=(k == KC - 1)))(),
                    waits=[gat_done[kc], wo_cast[kc]] + ofree,
                    inc=(1 if (kc == KC - 1 and ech == 3) else 0))
        pA_free, pB_free = (([], pB_free) if bx == 0 else (pA_free, []))
        if pend4 is not None:
            emit_tr4()
        kb.do("V", (lambda p=pair[0]: lambda e: e.tensor_copy(
            ln2o[:, 0:1024], p[:, :]))(), waits=[otok], inc=0)
        ev1 = kb.do("V", (lambda p=pair[1]: lambda e: e.tensor_copy(
            ln2o[:, 1024:2048], p[:, :]))(), inc=1)
        if dbg and tt == 0:
            ev1 = kb.do("Y", lambda e: e.dma_start(dbg_ao, ln2o[:, :]),
                        waits=[ev1], inc=16)
        tr_, tn = ln_stats([ln2o[:, :]], [ev1])
        if dbg and tt == 0:
            kb.do("Y", lambda e: e.dma_start(dbg_st, st24[:, :]),
                  waits=[tr_], inc=16)
            kb.do("V", lambda e: e.tensor_copy(mv2d[:, 0:2], mv2[:, :]), inc=0)
            tn = kb.do("V", lambda e: e.tensor_copy(mv2d[:, 2:3], rstd[:, :]),
                       waits=[tn], inc=1)
            kb.do("Y", lambda e: e.dma_start(dbg_mv, mv2d[:, 0:3]),
                  waits=[tn], inc=16)
        ap2 = ln_apply(ln2o[:, :], ln2o[:, :], tn)
        if dbg and tt == 0:
            ap2 = kb.do("Y", lambda e: e.dma_start(dbg_ln, ln2o[:, :]),
                        waits=[ap2], inc=16)
            kb.do("Y", (lambda b=bx: lambda e: e.dma_start(
                dbg_xo, xow[b][:, :]))(), waits=[txo], inc=16)
        tadd = kb.do("V", (lambda b=bx: lambda e: e.tensor_tensor(
            x2t[b][:, :], xow[b][:, :], ln2o[:, :], ALU.add))(),
            waits=[txo, ap2, x2_free[bx]], inc=1)
        xo_free[bx] = tadd
        tr3, tn3 = ln_stats([x2t[bx][:, :]], [tadd])
        ap3 = ln_apply(h2t[bx][:, :], x2t[bx][:, :], tn3,
                       waits=[h2t_free[bx]])
        sp_toks[tt] = kb.do("G", (lambda t=tt, b=bx: lambda e: e.dma_start(
            x2_d[t], x2t[b][:, :]))(), waits=[tadd, ap3, tr3],
            inc=16, chan="GD")
        x2_free[bx] = sp_toks[tt]
        pend4 = (pair, bx, tt, ap3, tr3)
    emit_tr4()
    pend4 = None

    tgs = kb.do("Y", lambda e: e.dma_start(gsv[:, :], gs_d), inc=16)
    tus = kb.do("Y", lambda e: e.dma_start(usv[:, :], us_d), inc=16)
    pre_g0 = kb.do("Y", lambda e: e.dma_start(wst[0][:, :], wg_d[0]),
                   waits=[wo_cast[14]], inc=16)
    pre_u0 = kb.do("Y", lambda e: e.dma_start(wst[1][:, :], wu_d[0]),
                   waits=[wo_cast[15]], inc=16)
    kb.fence(skip=("GD",))

    wgu_free = [None, None]
    wgb_free = [None, None]
    wub_free = [None, None]
    sg_free = [None, None]
    uu_free = [None, None]
    hdn_toks = [None] * ITN
    pgu_free = {0: [], 1: []}
    gtok = utok = None
    for it in range(ITN):
        bx = it % 2
        if it == 0:
            tg_, sg_src = pre_g0, 0
        else:
            tg_ = kb.do("Y", (lambda i=it, b=bx: lambda e: e.dma_start(
                wst[b][:, :], wg_d[i]))(), waits=[wgu_free[bx]], inc=16)
            sg_src = bx
        cg = kb.do("S", (lambda b=bx, sc=sg_src: lambda e: e.copy(
            wgb[b][:, :, :], wst[sc][:, :]))(),
            waits=[tg_, wgb_free[bx]], inc=1)
        wgu_free[sg_src] = cg
        if it == 0:
            tu_, su_src = pre_u0, 1
        else:
            tu_ = kb.do("Y", (lambda i=it, b=bx: lambda e: e.dma_start(
                wst[b][:, :], wu_d[i]))(), waits=[wgu_free[bx]], inc=16)
            su_src = bx
        cu = kb.do("S", (lambda b=bx, sc=su_src: lambda e: e.copy(
            wub[b][:, :, :], wst[sc][:, :]))(),
            waits=[tu_, wub_free[bx]], inc=1)
        wgu_free[su_src] = cu
        pg, pu = (PS[0], PS[1]) if bx == 0 else (PS[2], PS[3])
        gtok = None
        for kc in range(KC):
            for ch_ in range(2):
                gtok = kb.do("T", (lambda p=pg, b=bx, k=kc, c=ch_ * 512:
                             lambda e: e.matmul(
                    p[:, c:c + 512], wgb[b][:, k, :], h2T[:, k, c:c + 512],
                    start=(k == 0), stop=(k == KC - 1)))(),
                    waits=[cg, last_evac] + pgu_free[bx],
                    inc=(1 if (kc == KC - 1 and ch_ == 1) else 0))
        wgb_free[bx] = gtok
        utok = None
        for kc in range(KC):
            for ch_ in range(2):
                utok = kb.do("T", (lambda p=pu, b=bx, k=kc, c=ch_ * 512:
                             lambda e: e.matmul(
                    p[:, c:c + 512], wub[b][:, k, :], h2T[:, k, c:c + 512],
                    start=(k == 0), stop=(k == KC - 1)))(),
                    waits=[cu],
                    inc=(1 if (kc == KC - 1 and ch_ == 1) else 0))
        wub_free[bx] = utok
        pgu_free[bx] = []
        tsg_ = kb.do("S", (lambda p=pg, b=bx, i=it: lambda e: e.activation(
            sg_b[b][:, :], p[:, :], ACT.Silu, scale=gsv[:, i:i + 1]))(),
            waits=[gtok, tgs, sg_free[bx]], inc=1)
        tuu = kb.do("V", (lambda p=pu, b=bx: lambda e: e.tensor_copy(
            uu_b[b][:, :], p[:, :]))(), waits=[utok, uu_free[bx]], inc=1)
        dst = hdna if it < 16 else hdnb
        di = it if it < 16 else it - 16
        extra = ([gat_done[H - 1]] if it < 16
                 else [sp_toks[NTO - 1], sp_toks[NTO - 2]])
        hdn_toks[it] = kb.do("V", (lambda dd=dst, i=di, b=bx: lambda e:
                             e.tensor_tensor(
            dd[:, i, :], sg_b[b][:, :], uu_b[b][:, :], ALU.mult))(),
            waits=[tsg_] + extra, inc=1)
        sg_free[bx] = hdn_toks[it]
        uu_free[bx] = hdn_toks[it]
        pgu_free[bx] = [tsg_, tuu, hdn_toks[it]]

    tds = kb.do("G", lambda e: e.dma_start(ds_b[:, :], ds_d),
                waits=[gtok, utok], inc=16, chan="GD")
    pre_dw = []
    for i in range(3):
        pre_dw.append(kb.do("Y", (lambda s=i: lambda e: e.dma_start(
            dwst[s][:, :], wd_d[0, s]))(), waits=[wgu_free[0]], inc=16))

    kb.fence(skip=("GD",))

    dwst_free = [None, None, None]
    dwb_free = [None, None]
    bank_free = {}
    ffo_toks = [None] * NTO
    pend_ffo = None

    def emit_ffo(gp, mml):
        for tt in range(NTO):
            pt = PS[tt // 2]
            col = (tt % 2) * 512
            ev = kb.do("V", (lambda p=pt, c=col, t=tt, g=gp: lambda e:
                       e.tensor_tensor(
                ffo[:, t, 512 * g:512 * g + 512], p[:, c:c + 512],
                ds_b[:, 512 * g:512 * g + 512], ALU.mult))(),
                waits=[mml, tds], inc=1)
            bank_free[(gp + 1, tt)] = ev
            if gp == 3:
                ffo_toks[tt] = ev

    for g4 in range(4):
        mmlast = None
        for ic in range(ITN):
            i3 = (g4 * ITN + ic) % 3
            b2 = (g4 * ITN + ic) % 2
            if g4 == 0 and ic < 3:
                td_ = pre_dw[ic]
            else:
                td_ = kb.do("Y", (lambda g=g4, i=ic, s=i3: lambda e:
                            e.dma_start(
                    dwst[s][:, :], wd_d[g, i]))(),
                    waits=[dwst_free[i3]], inc=16)
            cd = kb.do("V", (lambda s=i3, b=b2, i=ic: lambda e: e.tensor_scalar(
                dwb[b][:, :], dwst[s][:, :], usv[:, i:i + 1], None,
                ALU.mult))(), waits=[td_, tus, dwb_free[b2]], inc=1)
            dwst_free[i3] = cd
            if ic == 0 and pend_ffo is not None:
                emit_ffo(*pend_ffo)
                pend_ffo = None
            src = hdna if ic < 16 else hdnb
            si = ic if ic < 16 else ic - 16
            mt = None
            for tt in range(NTO):
                pt = PS[tt // 2]
                col = (tt % 2) * 512
                mt = kb.do("T", (lambda p=pt, c=col, s=src, j=si, t=tt, b=b2,
                           st_=(ic == 0), sp=(ic == ITN - 1): lambda e:
                           e.matmul(
                    p[:, c:c + 512], s[:, j, t * 128:(t + 1) * 128],
                    dwb[b][:, :], start=st_, stop=sp))(),
                    waits=[cd, hdn_toks[ic], bank_free.get((g4, tt))],
                    inc=(1 if tt == NTO - 1 else 0))
            dwb_free[b2] = mt
            mmlast = mt
        pend_ffo = (g4, mmlast)
    emit_ffo(*pend_ffo)
    pend_ffo = None

    kb.fence()

    x2r_free = [None, None]
    ost_free = [None, None]
    ty = None
    for tt in range(NTO):
        bx = tt % 2
        trx = kb.do("G", (lambda t=tt, b=bx: lambda e: e.dma_start(
            x2r[b][:, :], x2_d[t]))(), waits=[x2r_free[bx], sp_toks[tt]],
            inc=16, chan="GD")
        tr_, tn = ln_stats([ffo[:, tt, :]], [ffo_toks[tt]])
        ap_ = ln_apply(ln4o[:, :], ffo[:, tt, :], tn)
        to_ = kb.do("V", (lambda b=bx: lambda e: e.tensor_tensor(
            ost[b][:, :], x2r[b][:, :], ln4o[:, :], ALU.add))(),
            waits=[trx, ap_, ost_free[bx]], inc=1)
        x2r_free[bx] = to_
        ty = kb.do("Y", (lambda t=tt, b=bx: lambda e: e.dma_start(
            y_d[t], ost[b][:, :]))(), waits=[to_], inc=16)
        ost_free[bx] = ty

    final = [(s, s.val) for s in kb.ch["Y"].lanes + kb.ch["GD"].lanes
             if s.val > 0]
    kb.do("Y", lambda e: e.nop(), waits=final, inc=0)

    with nc.Block() as blk:
        blk.tensor(lambda e: kb.replay("T", e))
        blk.vector(lambda e: kb.replay("V", e))
        blk.scalar(lambda e: kb.replay("S", e))
        blk.sync(lambda e: kb.replay("Y", e))
        blk.gpsimd(lambda e: kb.replay("G", e))
    return nc, {e: len(kb.ops[e]) for e in kb.ops}


def _prep(L0_attn_x, L0_attn_q_w, L0_attn_k_w, L0_attn_v_w, L0_attn_o_w,
          L0_attn_gate_w, L0_attn_gate_b, L0_ffn_gate_w, L0_ffn_up_w,
          L0_ffn_down_w, L0_ffn_gate_s, L0_ffn_up_s, L0_ffn_down_s):
    f = np.float32
    asc = np.ascontiguousarray
    x = np.asarray(L0_attn_x, f)

    def lhs_blocks(w, nt):
        return asc(np.asarray(w, f).reshape(nt, 128, KC, 128)
                   .transpose(0, 3, 2, 1))

    wq = lhs_blocks(L0_attn_q_w, H)
    wk = lhs_blocks(L0_attn_k_w, H)
    wv = asc(np.asarray(L0_attn_v_w, f).reshape(NG, GH * 128, KC, 128)
             .transpose(0, 3, 2, 1))
    wo = asc(np.asarray(L0_attn_o_w, f).reshape(D, KC, 128).transpose(1, 2, 0))
    wg = lhs_blocks(L0_ffn_gate_w, ITN)
    wu = lhs_blocks(L0_ffn_up_w, ITN)
    wd = asc(np.asarray(L0_ffn_down_w, f).reshape(4, 512, ITN, 128)
             .transpose(0, 2, 3, 1))
    gwt = asc(np.asarray(L0_attn_gate_w, f).T)
    gb = asc(np.asarray(L0_attn_gate_b, f).reshape(128, 1))
    gs = asc(np.asarray(L0_ffn_gate_s, f).reshape(ITN, 128).T)
    us = asc(np.asarray(L0_ffn_up_s, f).reshape(ITN, 128).T)
    ds = asc(np.broadcast_to(np.asarray(L0_ffn_down_s, f), (128, D)))
    tri = asc(np.triu(np.ones((128, 128), BF)))
    trin = asc(np.triu(np.full((128, 128), -30000.0, BF), 1))
    one = np.ones((128, 1), BF)
    idn = np.eye(128, dtype=BF)
    eps = np.full((128, 1), EPS, f)
    onr = np.ones((1, 128), f)
    onrb = np.ones((1, 128), BF)
    shared = dict(wq_d=wq, wk_d=wk, wv_d=wv, wo_d=wo, wg_d=wg, wu_d=wu,
                  wd_d=wd, gwt_d=gwt, gb_d=gb, gs_d=gs, us_d=us, ds_d=ds,
                  tri_d=tri, one_d=one, idn_d=idn, eps_d=eps,
                  onr_d=onr, onrb_d=onrb, trin_d=trin)
    in_maps = []
    for c in range(8):
        b, par = c // 2, c % 2
        perm = ([2 * j + par for j in range(8)]
                + [2 * j + (1 - par) for j in range(8)])
        xl = asc(x[b].reshape(STN, 128, D)[perm])
        ot = (np.ones if par == 1 else np.zeros)((128, 128), BF)
        otn = (np.zeros((128, 128)) if par == 1
               else np.full((128, 128), -30000.0)).astype(BF)
        in_maps.append(dict(shared, x_l=xl, oth_d=asc(ot), othn_d=asc(otn)))
    return in_maps


def kernel(**inputs):
    if "nc" not in _CACHE:
        _CACHE["nc"] = build_program()[0]
    nc = _CACHE["nc"]
    in_maps = _prep(**inputs)
    res = run_bass_kernel_spmd(nc, in_maps, core_ids=list(range(8)))
    y = np.empty((BATCH, SEQ, D), np.float32)
    for c in range(8):
        b, par = c // 2, c % 2
        out = np.asarray(res.results[c]["y_d"])
        yv = y[b].reshape(STN, 128, D)
        for j in range(NTO):
            yv[2 * j + par] = out[j]
    return y


if __name__ == "__main__":
    nc, counts = build_program()
    print("instructions:", nc.n_instructions(), counts)


# version 30
# speedup vs baseline: 1.2080x; 1.0023x over previous
import numpy as np
import ml_dtypes
import concourse.bass as bass
import concourse.mybir as mybir
from concourse.bass_utils import run_bass_kernel_spmd

F32 = mybir.dt.float32
BF16 = mybir.dt.bfloat16
ALU = mybir.AluOpType
ACT = mybir.ActivationFunctionType
AXX = mybir.AxisListType.X
BF = ml_dtypes.bfloat16

D = 2048
SEQ = 2048
BATCH = 4
H = 16
HD = 128
KC = 16
II = 5632
ITN = 44
NTO = 8
STN = 16
GH = 2
NG = H // GH
EPS = 1e-5
CSC = 1.0 / float(np.sqrt(128.0))

_CACHE = {}


class Sub:
    __slots__ = ("eng", "sem", "val", "dma", "name")

    def __init__(self, eng, dma, name):
        self.eng = eng
        self.sem = None
        self.val = 0
        self.dma = dma
        self.name = name


class Ch:

    def __init__(self, eng, dma, nlanes, name):
        self.dma = dma
        self.lanes = [Sub(eng, dma, f"{name}{i}") for i in range(nlanes)]
        self.rr = 0

    def issue(self, amt):
        if self.dma:
            s = self.lanes[self.rr]
            self.rr = (self.rr + 1) % len(self.lanes)
        else:
            s = self.lanes[0]
        s.val += amt
        return s, (s, s.val)


class KB:

    ENGS = ("T", "V", "S", "Y", "G")

    def __init__(self):
        self.ch = {
            "T": Ch("T", False, 1, "cT"), "V": Ch("V", False, 1, "cV"),
            "S": Ch("S", False, 1, "cS"), "Y": Ch("Y", True, 8, "cY"),
            "G": Ch("G", False, 1, "cG"), "GD": Ch("G", True, 4, "cD"),
        }
        self.ops = {e: [] for e in self.ENGS}
        self.waited = {}
        self.pending = {e: [] for e in self.ENGS}

    def all_subs(self):
        for c in self.ch.values():
            yield from c.lanes

    def do(self, eng, fn, waits=(), inc=0, chan=None):
        extra = []
        tok_out = None
        sub = None
        if inc:
            ch = self.ch[chan or eng]
            if ch.dma:
                lane = ch.lanes[ch.rr]
                if lane.val > 0:
                    extra.append((lane, lane.val))
            sub, tok_out = ch.issue(inc)
        wl = []
        for tok in list(waits) + self.pending[eng] + extra:
            if tok is None:
                continue
            c, v = tok
            if c.eng == eng and not c.dma:
                continue
            key = (eng, id(c))
            if self.waited.get(key, 0) >= v:
                continue
            self.waited[key] = v
            wl.append((c, v))
        self.pending[eng] = []
        self.ops[eng].append((wl, fn, sub, inc))
        return tok_out

    def fence(self, engs=None, skip=()):
        ex = set()
        for c in skip:
            ex.update(id(l) for l in self.ch[c].lanes)
        snap = [(s, s.val) for s in self.all_subs()
                if s.val > 0 and id(s) not in ex]
        for e in engs or self.ENGS:
            self.pending[e].extend(snap)

    def replay(self, eng, engobj):
        first = True
        for wl, fn, sub, amt in self.ops[eng]:
            if eng == "V" and not first:
                engobj.drain()
            first = False
            for c, v in wl:
                engobj.wait_ge(c.sem, v)
            ins = fn(engobj)
            if sub is not None:
                ins.then_inc(sub.sem, amt)


def build_program(dbg=False):
    nc = bass.Bass("TRN2", target_bir_lowering=False, debug=False,
               detect_race_conditions=False)
    kb = KB()

    dt = nc.dram_tensor
    x_l = dt("x_l", [STN, 128, D], F32, kind="ExternalInput").ap()
    wq_d = dt("wq_d", [H, 128, KC, 128], F32, kind="ExternalInput").ap()
    wk_d = dt("wk_d", [H, 128, KC, 128], F32, kind="ExternalInput").ap()
    wv_d = dt("wv_d", [NG, 128, KC, GH * 128], F32, kind="ExternalInput").ap()
    wo_d = dt("wo_d", [KC, 128, D], F32, kind="ExternalInput").ap()
    wg_d = dt("wg_d", [ITN, 128, KC, 128], F32, kind="ExternalInput").ap()
    wu_d = dt("wu_d", [ITN, 128, KC, 128], F32, kind="ExternalInput").ap()
    wd_d = dt("wd_d", [4, ITN, 128, 512], F32, kind="ExternalInput").ap()
    gwt_d = dt("gwt_d", [128, 128], F32, kind="ExternalInput").ap()
    gb_d = dt("gb_d", [128, 1], F32, kind="ExternalInput").ap()
    gs_d = dt("gs_d", [128, ITN], F32, kind="ExternalInput").ap()
    us_d = dt("us_d", [128, ITN], F32, kind="ExternalInput").ap()
    ds_d = dt("ds_d", [128, D], F32, kind="ExternalInput").ap()
    tri_d = dt("tri_d", [128, 128], BF16, kind="ExternalInput").ap()
    oth_d = dt("oth_d", [128, 128], BF16, kind="ExternalInput").ap()
    trin_d = dt("trin_d", [128, 128], BF16, kind="ExternalInput").ap()
    othn_d = dt("othn_d", [128, 128], BF16, kind="ExternalInput").ap()
    one_d = dt("one_d", [128, 1], BF16, kind="ExternalInput").ap()
    idn_d = dt("idn_d", [128, 128], BF16, kind="ExternalInput").ap()
    eps_d = dt("eps_d", [128, 1], F32, kind="ExternalInput").ap()
    onr_d = dt("onr_d", [1, 128], F32, kind="ExternalInput").ap()
    onrb_d = dt("onrb_d", [1, 128], BF16, kind="ExternalInput").ap()
    y_d = dt("y_d", [NTO, 128, D], F32, kind="ExternalOutput").ap()
    if dbg:
        dbg_ao = dt("dbg_ao", [128, D], F32, kind="ExternalOutput").ap()
        dbg_ln = dt("dbg_ln", [128, D], F32, kind="ExternalOutput").ap()
        dbg_mv = dt("dbg_mv", [128, 3], F32, kind="ExternalOutput").ap()
        dbg_st = dt("dbg_st", [128, 24], F32, kind="ExternalOutput").ap()
        dbg_xo = dt("dbg_xo", [128, D], F32, kind="ExternalOutput").ap()
        dbg_h1T = dt("dbg_h1T", [128, KC, SEQ], BF16, kind="ExternalOutput").ap()
        dbg_kT = dt("dbg_kT", [128, GH, SEQ], BF16, kind="ExternalOutput").ap()
        dbg_qT = dt("dbg_qT", [128, GH, 1024], BF16, kind="ExternalOutput").ap()
        dbg_vg = dt("dbg_vg", [128, STN, GH * 128], BF16, kind="ExternalOutput").ap()
        dbg_gat = dt("dbg_gat", [128, KC, 1024], BF16, kind="ExternalOutput").ap()
        dbg_Bb = dt("dbg_Bb", [1, 1024], BF16, kind="ExternalOutput").ap()
        dbg_p = dt("dbg_p", [128, 1024], BF16, kind="ExternalOutput").ap()
    x2_d = dt("x2_d", [NTO, 128, D], F32,
              kind="ExternalOutput" if dbg else "Internal").ap()

    PS = [nc.alloc_psum_tensor(f"ps{i}", [128, 1024], F32) for i in range(4)]

    base = 16512

    def sb(name, shape, dty, at):
        return nc.alloc_sbuf_tensor_at(name, shape, dty, offset=base + at)

    S1, S2, S3, S4 = 0, 65536, 155648, 188416
    KS = 204800
    h1T = sb("h1T", [128, KC, SEQ], BF16, S1)
    wo_b = sb("wo_b", [128, KC, D], BF16, S1)
    sg_b = [sb(f"sg{i}", [128, 1024], BF16, S1 + i * 2048) for i in range(2)]
    uu_b = [sb(f"uu{i}", [128, 1024], BF16, S1 + 4096 + i * 2048) for i in range(2)]
    wgb = [sb(f"wgb{i}", [128, KC, 128], BF16, S1 + 8192 + i * 4096) for i in range(2)]
    wub = [sb(f"wub{i}", [128, KC, 128], BF16, S1 + 16384 + i * 4096) for i in range(2)]
    ffo = sb("ffo", [128, NTO, D], F32, S1)
    xst = [sb(f"xst{i}", [128, D], F32, S3 + i * 8192) for i in range(2)]
    h1t = [sb(f"h1t{i}", [128, D], BF16, S3 + 16384 + i * 4096) for i in range(2)]
    kT = sb("kT", [128, GH, SEQ], BF16, S2 + 0)
    vg = sb("vg", [128, STN, GH * 128], BF16, S2 + 8192)
    qT = sb("qT", [128, GH, 1024], BF16, S2 + 16384)
    wq_s = [sb(f"wq_s{i}", [128, GH, KC, 128], BF16, S2 + 20480 + 8192 * i)
            for i in range(2)]
    wk_s = [sb(f"wk_s{i}", [128, GH, KC, 128], BF16, S2 + 36864 + 8192 * i)
            for i in range(2)]
    wvb = [sb(f"wvb{i}", [128, KC, GH * 128], BF16, S2 + 53248 + 8192 * i)
           for i in range(2)]
    p_b = [sb(f"p{i}", [128, 1024], BF16, S2 + 69632 + i * 2048) for i in range(4)]
    sigT = sb("sigT", [128, 1024], BF16, S2 + 77824)
    t1 = sb("t1", [128, 1024], F32, S2 + 79872)
    Brow2 = [sb(f"Brow{i}", [1, 1024], BF16, S2 + 83968 + 2048 * i)
             for i in range(2)]
    rcb = sb("rcb", [128, 1024], BF16, S2 + 88064)
    rrow = sb("rrow", [1, 1024], F32, S2 + 79872)
    ln2o = sb("ln2o", [128, D], F32, S2 + 0)
    h2t = [sb(f"h2t{i}", [128, D], BF16, S2 + 8192 + i * 4096) for i in range(2)]
    xow = [sb(f"xow{i}", [128, D], F32, S2 + 16384 + i * 8192) for i in range(2)]
    x2t = [sb(f"x2t{i}", [128, D], F32, S2 + 32768 + i * 8192) for i in range(2)]
    h2T = sb("h2T", [128, KC, 1024], BF16, S2 + 57344)
    hdnb = sb("hdnb", [128, ITN - 16, 1024], BF16, S2 + 0)
    ds_b = sb("ds_b", [128, D], F32, S2 + 57344)
    ln4o = sb("ln4o", [128, D], F32, S2 + 0)
    x2r = [sb(f"x2r{i}", [128, D], F32, S2 + 16384 + i * 8192) for i in range(2)]
    ost = [sb(f"ost{i}", [128, D], F32, S2 + 32768 + i * 8192) for i in range(2)]
    gat = sb("gat", [128, KC, 1024], BF16, S3)
    hdna = sb("hdna", [128, 16, 1024], BF16, S3)
    wst = [sb(f"wst{i}", [128, D], F32, S4 + i * 8192) for i in range(2)]
    dwst = [sb(f"dwst{i}", [128, 512], F32, S4 + i * 2048) for i in range(3)]
    dwb = [sb(f"dwb{i}", [128, 512], BF16, S4 + 6144 + i * 1024) for i in range(2)]
    ko = KS
    tri = sb("tri", [128, 128], BF16, ko); ko += 256
    trin = sb("trin", [128, 128], BF16, ko); ko += 256
    othn = sb("othn", [128, 128], BF16, ko); ko += 256
    oth = sb("oth", [128, 128], BF16, ko); ko += 256
    idn = sb("idn", [128, 128], BF16, ko); ko += 256
    onec = sb("onec", [128, 1], BF16, ko); ko += 32
    gwt = sb("gwt", [128, 128], F32, ko); ko += 512
    gwb = sb("gwb", [128, 128], BF16, ko); ko += 256
    gbv = sb("gbv", [128, 1], F32, ko); ko += 32
    gbh = sb("gbh", [128, 1], F32, ko); ko += 32
    gsv = sb("gsv", [128, ITN], F32, ko); ko += 192
    usv = sb("usv", [128, ITN], F32, ko); ko += 192
    st24 = sb("st24", [128, 24], F32, ko); ko += 96
    mv2 = sb("mv2", [128, 2], F32, ko); ko += 32
    sqv = sb("sqv", [128, 1], F32, ko); ko += 32
    rstd = sb("rstd", [128, 1], F32, ko); ko += 32
    ngm = sb("ngm", [128, 1], F32, ko); ko += 32
    km = sb("km", [128, 1], BF16, ko); ko += 32
    kmb = sb("kmb", [128, 1], BF16, ko); ko += 32
    epsv = sb("epsv", [128, 1], F32, ko); ko += 32
    mv2d = sb("mv2d", [128, 4], F32, ko); ko += 64
    ko = (ko + 31) // 32 * 32
    sgt = sb("sgt", [128, 1024], F32, ko)
    absq = sb("absq", [128, 1024], BF16, ko)
    ko += 4096
    onr = sb("onr", [1, 128], F32, ko); ko += 512
    onrb = sb("onrb", [1, 128], BF16, ko); ko += 256
    assert base + ko <= 229344

    for s in kb.all_subs():
        s.sem = nc.alloc_semaphore(name=s.name)

    def ln_stats(src_aps, tok_ready):
        tread = None
        i = 0
        for ap in src_aps:
            w = ap.shape[-1]
            for j in range(w // 512):
                tread = kb.do(
                    "V",
                    (lambda a=ap, jj=j, ii=i: lambda e: e.bn_stats(
                        st24[:, 6 * ii:6 * ii + 6],
                        a[:, 512 * jj:512 * jj + 512]))(),
                    waits=list(tok_ready), inc=1)
                i += 1
        assert i == 4
        tag = kb.do("V", lambda e: e.bn_aggr(mv2[:, :], st24[:, :]), inc=1)
        ts_ = kb.do("S", lambda e: e.activation(sqv[:, :], mv2[:, 1:2], ACT.Sqrt,
                                                bias=epsv[:, 0:1]),
                    waits=[tag, tok_eps], inc=1)
        tn = kb.do("V", lambda e: e.reciprocal(rstd[:, :], sqv[:, :]),
                   waits=[ts_], inc=1)
        return tread, tn

    def ln_apply(dst, src, tok_n, waits=()):
        return kb.do("V", lambda e: e.tensor_scalar(
            dst, src, mv2[:, 0:1], rstd[:, 0:1], ALU.subtract, ALU.mult),
            waits=[tok_n] + list(waits), inc=1)

    tok_idn = kb.do("Y", lambda e: e.dma_start(idn[:, :], idn_d), inc=16)
    tok_gw = kb.do("Y", lambda e: e.dma_start(gwt[:, :], gwt_d), inc=16)
    tok_gb = kb.do("Y", lambda e: e.dma_start(gbv[:, :], gb_d), inc=16)
    kb.do("Y", lambda e: e.dma_start(tri[:, :], tri_d), inc=16)
    kb.do("Y", lambda e: e.dma_start(oth[:, :], oth_d), inc=16)
    kb.do("Y", lambda e: e.dma_start(trin[:, :], trin_d), inc=16)
    kb.do("Y", lambda e: e.dma_start(othn[:, :], othn_d), inc=16)
    kb.do("Y", lambda e: e.dma_start(onec[:, :], one_d), inc=16)
    tok_eps = kb.do("Y", lambda e: e.dma_start(epsv[:, :], eps_d), inc=16)
    kb.do("Y", lambda e: e.dma_start(onr[:, :], onr_d), inc=16)
    kb.do("Y", lambda e: e.dma_start(onrb[:, :], onrb_d), inc=16)
    tok_gwb = kb.do("V", lambda e: e.tensor_copy(gwb[:, :], gwt[:, :]),
                    waits=[tok_gw], inc=1)
    kb.do("V", lambda e: e.tensor_scalar(gbh[:, :], gbv[:, :], 0.5, None,
                                         ALU.mult), waits=[tok_gb], inc=0)

    wst_free = [None, None]
    wq_free = [None, None]
    wk_free = [None, None]
    wvb_free = [None, None]

    wcast = {}

    def load_group_w(g):
        if g >= NG:
            return
        p2 = g % 2
        toks = []
        for hl in range(GH):
            h = g * GH + hl
            bx = hl % 2
            tq = kb.do("Y", (lambda hh=h, b=bx: lambda e: e.dma_start(
                wst[b][:, :], wq_d[hh]))(), waits=[wst_free[bx]], inc=16)
            cq = kb.do("S", (lambda b=bx, p=p2, hh=hl: lambda e: e.copy(
                wq_s[p][:, hh, :, :], wst[b][:, :]))(),
                waits=[tq, wq_free[p2]], inc=1)
            wst_free[bx] = cq
            tk = kb.do("Y", (lambda hh=h, b=bx: lambda e: e.dma_start(
                wst[b][:, :], wk_d[hh]))(), waits=[wst_free[bx]], inc=16)
            ck = kb.do("S", (lambda b=bx, p=p2, hh=hl: lambda e: e.copy(
                wk_s[p][:, hh, :, :], wst[b][:, :]))(),
                waits=[tk, wk_free[p2]], inc=1)
            wst_free[bx] = ck
            toks.append((cq, ck))
        cv = None
        for half in range(2):
            bx = half
            tv = kb.do("Y", (lambda gg=g, hf=half, b=bx: lambda e: e.dma_start(
                wst[b][:, :], wv_d[gg, :, 8 * hf:8 * hf + 8, :]))(),
                waits=[wst_free[bx]], inc=16)
            cv = kb.do("S", (lambda p=p2, hf=half, b=bx: lambda e: e.copy(
                wvb[p][:, 8 * hf:8 * hf + 8, :], wst[b][:, :]))(),
                waits=[tv, wvb_free[p2]], inc=1)
            wst_free[bx] = cv
        wcast[g] = (toks, cv)

    load_group_w(0)

    tr_evac = [None, None]
    xst_free = [None, None]
    h1T_done = []
    h1t_free = [None, None]
    pend_ev = None

    def flush_ev():
        nonlocal pend_ev
        pp, pi, ptk = pend_ev
        ev = kb.do("V", (lambda p=pp, i=pi: lambda e: e.tensor_copy(
            h1T[:, :, i * 128:(i + 1) * 128], p[:, :].bitcast(BF16)))(),
            waits=[ptk], inc=1)
        tr_evac[pi % 2] = ev
        h1T_done.append(ev)
        pend_ev = None
        return ev

    for idx in range(STN):
        bx = idx % 2
        tok_x = kb.do("Y", (lambda i=idx, b=bx: lambda e: e.dma_start(
            xst[b][:, :], x_l[i]))(), waits=[xst_free[bx]], inc=16)
        tr_ = None
        for j in range(4):
            tr_ = kb.do("V", (lambda b=bx, jj=j: lambda e: e.bn_stats(
                st24[:, 6 * jj:6 * jj + 6],
                xst[b][:, 512 * jj:512 * jj + 512]))(),
                waits=[tok_x], inc=(1 if j == 3 else 0))
        tag = kb.do("V", lambda e: e.bn_aggr(mv2[:, :], st24[:, :]), inc=1)
        ts_ = kb.do("S", lambda e: e.activation(sqv[:, :], mv2[:, 1:2],
                                                ACT.Sqrt, bias=epsv[:, 0:1]),
                    waits=[tag, tok_eps], inc=1)
        tn = kb.do("V", lambda e: e.reciprocal(rstd[:, :], sqv[:, :]),
                   waits=[ts_], inc=1)
        ta = kb.do("V", (lambda b=bx: lambda e: e.tensor_scalar(
            h1t[b][:, :], xst[b][:, :], mv2[:, 0:1], rstd[:, 0:1],
            ALU.subtract, ALU.mult))(),
            waits=[tn, h1t_free[bx]], inc=1)
        xst_free[bx] = ta
        pt = PS[2 + idx % 2]
        ttok = None
        for kc in range(KC):
            ttok = kb.do("T", (lambda p=pt, b=bx, k=kc: lambda e: e.transpose(
                p[:, 64 * k:64 * k + 64].bitcast(BF16),
                h1t[b][:, k * 128:(k + 1) * 128], idn[:, :]))(),
                waits=[ta, tr_evac[idx % 2], tok_idn],
                inc=(1 if kc == KC - 1 else 0))
        h1t_free[bx] = ttok
        ev = kb.do("S", (lambda p=pt, i=idx: lambda e: e.copy(
            h1T[:, :, i * 128:(i + 1) * 128], p[:, :].bitcast(BF16)))(),
            waits=[ttok], inc=1)
        tr_evac[idx % 2] = ev
        h1T_done.append(ev)
    h1T_ready = ev
    if dbg:
        kb.do("Y", lambda e: e.dma_start(dbg_h1T, h1T[:, :, :]),
              waits=[h1T_ready], inc=16)

    kb.fence()

    kT_free = [None] * GH
    qT_free = [None] * GH
    vg_free = None
    gat_done = [None] * H
    absq_free = None
    Brow_free = None
    Bb_free = []
    kmb_free = None
    sigT_free = None
    sgt_free = None
    rrow_free = None
    rcb_free = None
    h1T_last_read = None
    t_last = {0: None, 1: None, 2: None, 3: None}
    tok_avl = None
    pend_epi = None
    pend_b = None
    tBr2 = [None, None]
    eqh = [None, None]
    tkm2 = [None, None]
    tabs2 = [None, None]
    kmh = [km, kmb]
    kmb_free2 = [None, None]

    for g in range(NG):
        p2 = g % 2
        wq_cast = [wcast[g][0][hl][0] for hl in range(GH)]
        wk_cast = [wcast[g][0][hl][1] for hl in range(GH)]
        vcast = wcast[g][1]
        if pend_epi is not None:
            epi_a_prev, epi_b_prev = pend_epi
            pend_bp = (lambda f=epi_b_prev, a=epi_a_prev(): lambda: f(ab=a))()
            pend_epi = None
        else:
            pend_bp = None
        for hl in range(GH):
            h = g * GH + hl
            bx = hl % 2
            ktok = None
            for ch_ in range(4):
                pt = PS[0] if ch_ < 2 else PS[1]
                col = (ch_ % 2) * 512
                for kc in range(KC):
                    ktok = kb.do("T", (lambda p=pt, c=col, k=kc, cc=ch_,
                                 pp=p2, hh=hl: lambda e: e.matmul(
                        p[:, c:c + 512], wk_s[pp][:, hh, k, :],
                        h1T[:, k, cc * 512:cc * 512 + 512],
                        start=(k == 0), stop=(k == KC - 1)))(),
                        waits=[wk_cast[hl],
                               h1T_done[4 * ch_ + 3] if g == 0
                               else h1T_ready,
                               t_last[0 if ch_ < 2 else 1]],
                        inc=(1 if (kc == KC - 1 and ch_ == 3) else 0))
                if ch_ == 0 and hl == 0 and pend_bp is not None:
                    pend_bp()
                    pend_bp = None
            ek = None
            for half in range(2):
                ek = kb.do("V", (lambda hh=hl, hf=half: lambda e: e.tensor_copy(
                    kT[:, hh, hf * 1024:hf * 1024 + 1024], PS[hf][:, :]))(),
                    waits=[ktok, kT_free[hl]], inc=1)
            t_last[0] = ek
            t_last[1] = ek
            tkm2[hl] = kb.do("V", (lambda hh=hl: lambda e: e.reduce_max(
                kmh[hh][:, :], kT[:, hh, :], AXX,
                apply_absolute_value=True))(),
                waits=[kmb_free2[hl], ek], inc=1)
            qp = PS[2 + hl]
            qtok = None
            for ch_ in range(2):
                for kc in range(KC):
                    qtok = kb.do("T", (lambda p=qp, c=ch_ * 512, k=kc,
                                 pp=p2, hh=hl: lambda e: e.matmul(
                        p[:, c:c + 512], wq_s[pp][:, hh, k, :],
                        h1T[:, k, c:c + 512],
                        start=(k == 0), stop=(k == KC - 1)))(),
                        waits=[wq_cast[hl], t_last[2 + hl]],
                        inc=(1 if (kc == KC - 1 and ch_ == 1) else 0))
            eq = kb.do("V", (lambda hh=hl, p=qp: lambda e: e.tensor_copy(
                qT[:, hh, :], p[:, :]))(), waits=[qtok, qT_free[hl]], inc=1)
            t_last[2 + hl] = eq
            eqh[hl] = eq
            if hl == 0:
                tabs2[0] = kb.do("S", lambda e: e.activation(
                    absq[:, :], qT[:, 0, :], ACT.Abs, scale=1.01),
                    waits=[absq_free, sgt_free, eq], inc=1)
            wq_free[p2] = qtok
            wk_free[p2] = ktok
            h1T_last_read = qtok
        if dbg and g == 0:
            kb.do("Y", lambda e: e.dma_start(dbg_kT, kT[:, :, :]),
                  waits=[t_last[1], ek], inc=16)
            kb.do("Y", lambda e: e.dma_start(dbg_qT, qT[:, :, :]),
                  waits=[eq], inc=16)

        tBr2 = [None, None]
        for hl in range(GH):
            if tabs2[hl] is None:
                tabs2[hl] = kb.do("S", (lambda hh=hl: lambda e: e.activation(
                    absq[:, :], qT[:, hh, :], ACT.Abs, scale=1.01))(),
                    waits=[absq_free, sgt_free, eqh[hl]], inc=1)
            btok = None
            for ch_ in range(2):
                btok = kb.do("T", (lambda c=ch_ * 512, hh=hl: lambda e:
                             e.matmul(
                    PS[2 + hh][0:1, c:c + 512], kmh[hh][:, 0:1],
                    absq[:, c:c + 512],
                    start=True, stop=True))(),
                    waits=[tkm2[hl], tabs2[hl], t_last[2 + hl]],
                    inc=(1 if ch_ == 1 else 0))
            kmb_free2[hl] = btok
            absq_free = btok
            tBr2[hl] = kb.do("V", (lambda hh=hl: lambda e: e.tensor_scalar(
                Brow2[hh][:, :], PS[2 + hh][0:1, :], -1.0, None, ALU.mult))(),
                waits=[btok, Brow_free], inc=1)
            if dbg and g == 0 and hl == 0:
                kb.do("Y", (lambda hh=hl: lambda e: e.dma_start(
                    dbg_Bb, Brow2[hh][:, :]))(), waits=[tBr2[hl]], inc=16)
        t_last[2] = tBr2[0]
        t_last[3] = tBr2[1]
        tabs2 = [None, None]
        load_group_w(g + 1)

        vev = None
        for sl in range(STN):
            pt = PS[0]
            col = (sl % 2) * 512
            vtok = None
            for kc in range(KC):
                vtok = kb.do("T", (lambda p=pt, c=col, k=kc, s=sl, pp=p2:
                             lambda e: e.matmul(
                    p[:, c:c + GH * 128], h1T[:, k, s * 128:(s + 1) * 128],
                    wvb[pp][:, k, :], start=(k == 0), stop=(k == KC - 1)))(),
                    waits=[vcast, t_last[0], h1T_ready],
                    inc=(1 if kc == KC - 1 else 0))
            vev = kb.do("V", (lambda p=pt, c=col, s=sl: lambda e: e.tensor_copy(
                vg[:, s, :], p[:, c:c + GH * 128]))(),
                waits=[vtok, vg_free], inc=1)
            if sl % 2 == 1:
                t_last[0] = vev
        t_last[0] = vev
        wvb_free[p2] = vtok
        h1T_last_read = vtok
        if dbg and g == 0:
            kb.do("Y", lambda e: e.dma_start(dbg_vg, vg[:, :, :]),
                  waits=[vev], inc=16)

        for hl in range(GH):
            h = g * GH + hl
            Brow = Brow2[hl]
            tBr = tBr2[hl]
            p_free = [None, None, None, None]
            av_first = {0: True, 1: True}
            l_first = {0: True, 1: True}
            av_rem = {0: 8, 1: 16}
            l_rem = {0: 8, 1: 16}
            sT_free = [t_last[0], t_last[1]]
            tcount = 0
            pendq = []

            def emit_avl(hh):
                nonlocal tok_avl
                bxp, slp, chp, ptokp = pendq.pop(0)
                for (c0, c1) in chp:
                    bank = 0 if c0 < 512 else 1
                    av_rem[bank] -= 1
                    kb.do("T", (lambda b=bxp, s=slp, a=c0, z=c1,
                                st_=av_first[bank],
                                sp=(av_rem[bank] == 0): lambda e: e.matmul(
                        PS[2][:, a:z], vg[:, s, hh * 128:(hh + 1) * 128],
                        p_b[b][:, a:z], start=st_, stop=sp))(),
                        waits=[ptokp, t_last[2]], inc=0)
                    av_first[bank] = False
                for (c0, c1) in chp:
                    bank = 0 if c0 < 512 else 1
                    l_rem[bank] -= 1
                    tok_avl = kb.do("T", (lambda b=bxp, a=c0, z=c1,
                                    st_=l_first[bank],
                                    sp=(l_rem[bank] == 0): lambda e:
                                    e.matmul(
                        PS[3][0:1, a:z], onec[:, :], p_b[b][:, a:z],
                        start=st_, stop=sp))(),
                        waits=[t_last[3], tBr],
                        inc=(1 if (c0, c1) == chp[-1] else 0))
                    l_first[bank] = False
                p_free[bxp] = tok_avl

            for i in range(8):
                ts0 = i * 128
                chunks = ([(ts0, 512), (512, 1024)] if i < 4
                          else [(ts0, 1024)])
                for half in range(2):
                    sl = i + 8 * half
                    bx = tcount % 2
                    pbx = tcount % 4
                    tcount += 1
                    stok = None
                    for (c0, c1) in chunks:
                        kb.do("T", (lambda b=bx, a=c0, z=c1, br=Brow:
                              lambda e: e.matmul(
                            PS[b][:, a:z], onrb[0:1, :], br[0:1, a:z],
                            start=True, stop=False))(),
                            waits=[sT_free[bx], tBr], inc=0)
                    mk = trin if half == 0 else othn
                    kb.do("T", (lambda b=bx, a=ts0, m=mk: lambda e: e.matmul(
                        PS[b][:, a:a + 128], m[:, :], idn[:, :],
                        start=False, stop=False))(), inc=0)
                    for (c0, c1) in chunks:
                        stok = kb.do("T", (lambda b=bx, hh=hl, s=sl, a=c0,
                                     z=c1: lambda e: e.matmul(
                            PS[b][:, a:z], kT[:, hh, s * 128:(s + 1) * 128],
                            qT[:, hh, a:z], start=False, stop=True))(),
                            waits=[],
                            inc=(1 if (c0, c1) == chunks[-1] else 0))
                    if len(pendq) == 3:
                        emit_avl(hl)
                    ptok = kb.do("S", (lambda b=pbx, bb=bx, a=ts0: lambda e:
                                 e.activation(
                        p_b[b][:, a:1024], PS[bb][:, a:1024], ACT.Exp,
                        scale=CSC))(),
                        waits=[stok, p_free[pbx]], inc=1)
                    sT_free[bx] = ptok
                    if dbg and g == 0 and hl == 0 and i == 0 and half == 0:
                        ptok = kb.do("Y", (lambda b=pbx: lambda e:
                                     e.dma_start(
                            dbg_p, p_b[b][:, :]))(), waits=[ptok], inc=16)
                    pendq.append((pbx, sl, chunks, ptok))
                    if tcount == 1 and pend_b is not None:
                        pend_b()
                        pend_b = None
            while pendq:
                emit_avl(hl)
            Brow_free = tok_avl
            def epi_a(hl=hl, h=h, sT0=sT_free[0], sT1=sT_free[1],
                      tok_avl_=tok_avl):
                nonlocal rrow_free
                gtok = None
                for ch_ in range(2):
                    gtok = kb.do("T", (lambda hh=hl, c=ch_ * 512: lambda e:
                                 e.matmul(
                        PS[0][:, c:c + 512], gwb[:, :], qT[:, hh, c:c + 512],
                        start=True, stop=True))(),
                        waits=[tok_gwb, sT0, sT1],
                        inc=(1 if ch_ == 1 else 0))
                tsg1 = kb.do("S", lambda e: e.activation(
                    sgt[:, :], PS[0][:, :], ACT.Tanh, bias=gbh[:, 0:1],
                    scale=0.5), waits=[gtok, sgt_free], inc=1)
                trc = kb.do("V", lambda e: e.reciprocal(
                    rrow[:, :], PS[3][0:1, :]),
                    waits=[tok_avl_, rrow_free], inc=1)
                t_last[0] = tsg1
                t_last[1] = sT1
                qT_free[hl] = gtok
                return tsg1, trc

            def epi_b(hl=hl, h=h, tok_avl_=tok_avl, ab=None):
                nonlocal rrow_free, rcb_free, sigT_free, sgt_free
                tsg1, trc = ab
                rct = None
                for ch_ in range(2):
                    rct = kb.do("T", (lambda c=ch_ * 512: lambda e: e.matmul(
                        PS[3][:, c:c + 512], onr[0:1, :],
                        rrow[0:1, c:c + 512],
                        start=True, stop=True))(), waits=[trc],
                        inc=(1 if ch_ == 1 else 0))
                rrow_free = rct
                trcb = kb.do("V", lambda e: e.tensor_copy(
                    rcb[:, :], PS[3][:, :]), waits=[rct], inc=1)
                t_last[3] = trcb
                tt1 = kb.do("V", lambda e: e.tensor_tensor(
                    t1[:, :], PS[2][:, :], rcb[:, :], ALU.mult),
                    waits=[tok_avl_, trcb], inc=1)
                t_last[2] = tt1
                rcb_free = tt1
                tsig = kb.do("V", lambda e: e.tensor_scalar(
                    sigT[:, :], sgt[:, :], 0.5, 0.5, ALU.mult, ALU.add),
                    waits=[tsg1, sigT_free], inc=1)
                sgt_free = tsig
                gat_done[h] = kb.do("V", (lambda hh=h: lambda e:
                                    e.tensor_tensor(
                    gat[:, hh, :], t1[:, :], sigT[:, :], ALU.mult))(),
                    waits=[tsig], inc=1)
                sigT_free = gat_done[h]

            if hl == 0:
                pend_b = (lambda f=epi_b, a=epi_a(): lambda: f(ab=a))()
                kT_free[hl] = sT_free[1]
            else:
                pend_epi = (epi_a, epi_b)
                kT_free[hl] = sT_free[1]
                t_last[0] = sT_free[0]
                t_last[1] = sT_free[1]
        vg_free = tok_avl
    if pend_epi is not None:
        _ea, _eb = pend_epi
        _eb(ab=_ea())
        pend_epi = None

    if dbg:
        gat_done[H - 1] = kb.do("Y", lambda e: e.dma_start(
            dbg_gat, gat[:, :, :]), waits=[gat_done[H - 1]], inc=16)

    wo_cast = [None] * KC
    for kc in range(KC):
        bx = kc % 2
        two = kb.do("Y", (lambda k=kc, b=bx: lambda e: e.dma_start(
            wst[b][:, :], wo_d[k]))(), waits=[wst_free[bx]], inc=16)
        wo_cast[kc] = kb.do("S", (lambda k=kc, b=bx: lambda e: e.copy(
            wo_b[:, k, :], wst[b][:, :]))(),
            waits=[two, h1T_last_read], inc=1)
        wst_free[bx] = wo_cast[kc]

    kb.fence(engs=["Y", "G"])
    xo_free = [None, None]
    x2_free = [None, None]
    h2t_free = [None, None]
    sp_toks = [None] * NTO
    pA_free = [t_last[0], t_last[1], t_last[2], t_last[3], vg_free]
    pB_free = []
    h2T_evac = None
    last_evac = None
    pend4 = None

    def emit_tr4():
        nonlocal h2T_evac, last_evac, pA_free, pB_free
        pairp, bxp, ttp, ap3p, tr3p = pend4
        ttok = None
        for kc in range(KC):
            ttok = kb.do("T", (lambda p=pairp[0], b=bxp, k=kc: lambda e:
                         e.transpose(
                p[:, 64 * k:64 * k + 64].bitcast(BF16),
                h2t[b][:, k * 128:(k + 1) * 128], idn[:, :]))(),
                waits=[ap3p, tr3p],
                inc=(1 if kc == KC - 1 else 0))
        h2t_free[bxp] = ttok
        ev = kb.do("V", (lambda p=pairp[0], t=ttp: lambda e: e.tensor_copy(
            h2T[:, :, t * 128:(t + 1) * 128], p[:, :].bitcast(BF16)))(),
            waits=[ttok], inc=1)
        h2T_evac = ev
        last_evac = ev
        if bxp == 0:
            pA_free = [ev]
        else:
            pB_free = [ev]

    for tt in range(NTO):
        bx = tt % 2
        pair = [(PS[0], PS[1]), (PS[2], PS[3])][bx]
        txo = kb.do("Y", (lambda t=tt, b=bx: lambda e: e.dma_start(
            xow[b][:, :], x_l[t]))(), waits=[xo_free[bx]], inc=16)
        otok = None
        ofree = (pA_free if bx == 0 else pB_free) + [h2T_evac]
        for kc in range(KC):
            for ech in range(4):
                pt = pair[ech // 2]
                col = (ech % 2) * 512
                otok = kb.do("T", (lambda p=pt, c=col, k=kc, t=tt, q=ech:
                             lambda e: e.matmul(
                    p[:, c:c + 512], gat[:, k, t * 128:(t + 1) * 128],
                    wo_b[:, k, 512 * q:512 * q + 512],
                    start=(k == 0), stop=(k == KC - 1)))(),
                    waits=[gat_done[H - 1], wo_cast[kc]] + ofree,
                    inc=(1 if (kc == KC - 1 and ech == 3) else 0))
        pA_free, pB_free = (([], pB_free) if bx == 0 else (pA_free, []))
        if pend4 is not None:
            emit_tr4()
        kb.do("V", (lambda p=pair[0]: lambda e: e.tensor_copy(
            ln2o[:, 0:1024], p[:, :]))(), waits=[otok], inc=0)
        ev1 = kb.do("V", (lambda p=pair[1]: lambda e: e.tensor_copy(
            ln2o[:, 1024:2048], p[:, :]))(), inc=1)
        if dbg and tt == 0:
            ev1 = kb.do("Y", lambda e: e.dma_start(dbg_ao, ln2o[:, :]),
                        waits=[ev1], inc=16)
        tr_, tn = ln_stats([ln2o[:, :]], [ev1])
        if dbg and tt == 0:
            kb.do("Y", lambda e: e.dma_start(dbg_st, st24[:, :]),
                  waits=[tr_], inc=16)
            kb.do("V", lambda e: e.tensor_copy(mv2d[:, 0:2], mv2[:, :]), inc=0)
            tn = kb.do("V", lambda e: e.tensor_copy(mv2d[:, 2:3], rstd[:, :]),
                       waits=[tn], inc=1)
            kb.do("Y", lambda e: e.dma_start(dbg_mv, mv2d[:, 0:3]),
                  waits=[tn], inc=16)
        ap2 = ln_apply(ln2o[:, :], ln2o[:, :], tn)
        if dbg and tt == 0:
            ap2 = kb.do("Y", lambda e: e.dma_start(dbg_ln, ln2o[:, :]),
                        waits=[ap2], inc=16)
            kb.do("Y", (lambda b=bx: lambda e: e.dma_start(
                dbg_xo, xow[b][:, :]))(), waits=[txo], inc=16)
        tadd = kb.do("V", (lambda b=bx: lambda e: e.tensor_tensor(
            x2t[b][:, :], xow[b][:, :], ln2o[:, :], ALU.add))(),
            waits=[txo, ap2, x2_free[bx]], inc=1)
        xo_free[bx] = tadd
        tr3, tn3 = ln_stats([x2t[bx][:, :]], [tadd])
        ap3 = ln_apply(h2t[bx][:, :], x2t[bx][:, :], tn3,
                       waits=[h2t_free[bx]])
        sp_toks[tt] = kb.do("G", (lambda t=tt, b=bx: lambda e: e.dma_start(
            x2_d[t], x2t[b][:, :]))(), waits=[tadd, ap3, tr3],
            inc=16, chan="GD")
        x2_free[bx] = sp_toks[tt]
        pend4 = (pair, bx, tt, ap3, tr3)
    emit_tr4()
    pend4 = None

    tgs = kb.do("Y", lambda e: e.dma_start(gsv[:, :], gs_d), inc=16)
    tus = kb.do("Y", lambda e: e.dma_start(usv[:, :], us_d), inc=16)
    pre_g0 = kb.do("Y", lambda e: e.dma_start(wst[0][:, :], wg_d[0]),
                   waits=[wo_cast[14]], inc=16)
    pre_u0 = kb.do("Y", lambda e: e.dma_start(wst[1][:, :], wu_d[0]),
                   waits=[wo_cast[15]], inc=16)
    kb.fence(skip=("GD",))

    wgu_free = [None, None]
    wgb_free = [None, None]
    wub_free = [None, None]
    sg_free = [None, None]
    uu_free = [None, None]
    hdn_toks = [None] * ITN
    pgu_free = {0: [], 1: []}
    gtok = utok = None
    for it in range(ITN):
        bx = it % 2
        if it == 0:
            tg_, sg_src = pre_g0, 0
        else:
            tg_ = kb.do("Y", (lambda i=it, b=bx: lambda e: e.dma_start(
                wst[b][:, :], wg_d[i]))(), waits=[wgu_free[bx]], inc=16)
            sg_src = bx
        cg = kb.do("S", (lambda b=bx, sc=sg_src: lambda e: e.copy(
            wgb[b][:, :, :], wst[sc][:, :]))(),
            waits=[tg_, wgb_free[bx]], inc=1)
        wgu_free[sg_src] = cg
        if it == 0:
            tu_, su_src = pre_u0, 1
        else:
            tu_ = kb.do("Y", (lambda i=it, b=bx: lambda e: e.dma_start(
                wst[b][:, :], wu_d[i]))(), waits=[wgu_free[bx]], inc=16)
            su_src = bx
        cu = kb.do("S", (lambda b=bx, sc=su_src: lambda e: e.copy(
            wub[b][:, :, :], wst[sc][:, :]))(),
            waits=[tu_, wub_free[bx]], inc=1)
        wgu_free[su_src] = cu
        pg, pu = (PS[0], PS[1]) if bx == 0 else (PS[2], PS[3])
        gtok = None
        for kc in range(KC):
            for ch_ in range(2):
                gtok = kb.do("T", (lambda p=pg, b=bx, k=kc, c=ch_ * 512:
                             lambda e: e.matmul(
                    p[:, c:c + 512], wgb[b][:, k, :], h2T[:, k, c:c + 512],
                    start=(k == 0), stop=(k == KC - 1)))(),
                    waits=[cg, last_evac] + pgu_free[bx],
                    inc=(1 if (kc == KC - 1 and ch_ == 1) else 0))
        wgb_free[bx] = gtok
        utok = None
        for kc in range(KC):
            for ch_ in range(2):
                utok = kb.do("T", (lambda p=pu, b=bx, k=kc, c=ch_ * 512:
                             lambda e: e.matmul(
                    p[:, c:c + 512], wub[b][:, k, :], h2T[:, k, c:c + 512],
                    start=(k == 0), stop=(k == KC - 1)))(),
                    waits=[cu],
                    inc=(1 if (kc == KC - 1 and ch_ == 1) else 0))
        wub_free[bx] = utok
        pgu_free[bx] = []
        tsg_ = kb.do("S", (lambda p=pg, b=bx, i=it: lambda e: e.activation(
            sg_b[b][:, :], p[:, :], ACT.Silu, scale=gsv[:, i:i + 1]))(),
            waits=[gtok, tgs, sg_free[bx]], inc=1)
        tuu = kb.do("V", (lambda p=pu, b=bx: lambda e: e.tensor_copy(
            uu_b[b][:, :], p[:, :]))(), waits=[utok, uu_free[bx]], inc=1)
        dst = hdna if it < 16 else hdnb
        di = it if it < 16 else it - 16
        extra = ([gat_done[H - 1]] if it < 16
                 else [sp_toks[NTO - 1], sp_toks[NTO - 2]])
        hdn_toks[it] = kb.do("V", (lambda dd=dst, i=di, b=bx: lambda e:
                             e.tensor_tensor(
            dd[:, i, :], sg_b[b][:, :], uu_b[b][:, :], ALU.mult))(),
            waits=[tsg_] + extra, inc=1)
        sg_free[bx] = hdn_toks[it]
        uu_free[bx] = hdn_toks[it]
        pgu_free[bx] = [tsg_, tuu, hdn_toks[it]]

    tds = kb.do("G", lambda e: e.dma_start(ds_b[:, :], ds_d),
                waits=[gtok, utok], inc=16, chan="GD")
    pre_dw = []
    for i in range(3):
        pre_dw.append(kb.do("Y", (lambda s=i: lambda e: e.dma_start(
            dwst[s][:, :], wd_d[0, s]))(), waits=[wgu_free[0]], inc=16))

    kb.fence(skip=("GD",))

    dwst_free = [None, None, None]
    dwb_free = [None, None]
    bank_free = {}
    ffo_toks = [None] * NTO
    pend_ffo = None
    mm_tt = [None] * NTO

    def emit_ffo(gp, mml):
        for tt in range(NTO):
            pt = PS[tt // 2]
            col = (tt % 2) * 512
            ev = kb.do("V", (lambda p=pt, c=col, t=tt, g=gp: lambda e:
                       e.tensor_tensor(
                ffo[:, t, 512 * g:512 * g + 512], p[:, c:c + 512],
                ds_b[:, 512 * g:512 * g + 512], ALU.mult))(),
                waits=[mml[tt], tds], inc=1)
            bank_free[(gp + 1, tt)] = ev
            if gp == 3:
                ffo_toks[tt] = ev

    for g4 in range(4):
        mmlast = None
        for ic in range(ITN):
            i3 = (g4 * ITN + ic) % 3
            b2 = (g4 * ITN + ic) % 2
            if g4 == 0 and ic < 3:
                td_ = pre_dw[ic]
            else:
                td_ = kb.do("Y", (lambda g=g4, i=ic, s=i3: lambda e:
                            e.dma_start(
                    dwst[s][:, :], wd_d[g, i]))(),
                    waits=[dwst_free[i3]], inc=16)
            cd = kb.do("V", (lambda s=i3, b=b2, i=ic: lambda e: e.tensor_scalar(
                dwb[b][:, :], dwst[s][:, :], usv[:, i:i + 1], None,
                ALU.mult))(), waits=[td_, tus, dwb_free[b2]], inc=1)
            dwst_free[i3] = cd
            if ic == 0 and pend_ffo is not None:
                emit_ffo(*pend_ffo)
                pend_ffo = None
            src = hdna if ic < 16 else hdnb
            si = ic if ic < 16 else ic - 16
            mt = None
            last_ic = ic == ITN - 1
            for tt in range(NTO):
                pt = PS[tt // 2]
                col = (tt % 2) * 512
                mt = kb.do("T", (lambda p=pt, c=col, s=src, j=si, t=tt, b=b2,
                           st_=(ic == 0), sp=last_ic: lambda e:
                           e.matmul(
                    p[:, c:c + 512], s[:, j, t * 128:(t + 1) * 128],
                    dwb[b][:, :], start=st_, stop=sp))(),
                    waits=[cd, hdn_toks[ic], bank_free.get((g4, tt))],
                    inc=(1 if (last_ic or tt == NTO - 1) else 0))
                if last_ic:
                    mm_tt[tt] = mt
            dwb_free[b2] = mt
            mmlast = mt
        pend_ffo = (g4, list(mm_tt))
    emit_ffo(*pend_ffo)
    pend_ffo = None

    kb.fence()

    x2r_free = [None, None]
    ost_free = [None, None]
    ty = None
    for tt in range(NTO):
        bx = tt % 2
        trx = kb.do("G", (lambda t=tt, b=bx: lambda e: e.dma_start(
            x2r[b][:, :], x2_d[t]))(), waits=[x2r_free[bx], sp_toks[tt]],
            inc=16, chan="GD")
        tr_, tn = ln_stats([ffo[:, tt, :]], [ffo_toks[tt]])
        ap_ = ln_apply(ln4o[:, :], ffo[:, tt, :], tn)
        to_ = kb.do("V", (lambda b=bx: lambda e: e.tensor_tensor(
            ost[b][:, :], x2r[b][:, :], ln4o[:, :], ALU.add))(),
            waits=[trx, ap_, ost_free[bx]], inc=1)
        x2r_free[bx] = to_
        ty = kb.do("Y", (lambda t=tt, b=bx: lambda e: e.dma_start(
            y_d[t], ost[b][:, :]))(), waits=[to_], inc=16)
        ost_free[bx] = ty

    final = [(s, s.val) for s in kb.ch["Y"].lanes + kb.ch["GD"].lanes
             if s.val > 0]
    kb.do("Y", lambda e: e.nop(), waits=final, inc=0)

    with nc.Block() as blk:
        blk.tensor(lambda e: kb.replay("T", e))
        blk.vector(lambda e: kb.replay("V", e))
        blk.scalar(lambda e: kb.replay("S", e))
        blk.sync(lambda e: kb.replay("Y", e))
        blk.gpsimd(lambda e: kb.replay("G", e))
    return nc, {e: len(kb.ops[e]) for e in kb.ops}


def _prep(L0_attn_x, L0_attn_q_w, L0_attn_k_w, L0_attn_v_w, L0_attn_o_w,
          L0_attn_gate_w, L0_attn_gate_b, L0_ffn_gate_w, L0_ffn_up_w,
          L0_ffn_down_w, L0_ffn_gate_s, L0_ffn_up_s, L0_ffn_down_s):
    f = np.float32
    asc = np.ascontiguousarray
    x = np.asarray(L0_attn_x, f)

    def lhs_blocks(w, nt):
        return asc(np.asarray(w, f).reshape(nt, 128, KC, 128)
                   .transpose(0, 3, 2, 1))

    wq = lhs_blocks(L0_attn_q_w, H)
    wk = lhs_blocks(L0_attn_k_w, H)
    wv = asc(np.asarray(L0_attn_v_w, f).reshape(NG, GH * 128, KC, 128)
             .transpose(0, 3, 2, 1))
    wo = asc(np.asarray(L0_attn_o_w, f).reshape(D, KC, 128).transpose(1, 2, 0))
    wg = lhs_blocks(L0_ffn_gate_w, ITN)
    wu = lhs_blocks(L0_ffn_up_w, ITN)
    wd = asc(np.asarray(L0_ffn_down_w, f).reshape(4, 512, ITN, 128)
             .transpose(0, 2, 3, 1))
    gwt = asc(np.asarray(L0_attn_gate_w, f).T)
    gb = asc(np.asarray(L0_attn_gate_b, f).reshape(128, 1))
    gs = asc(np.asarray(L0_ffn_gate_s, f).reshape(ITN, 128).T)
    us = asc(np.asarray(L0_ffn_up_s, f).reshape(ITN, 128).T)
    ds = asc(np.broadcast_to(np.asarray(L0_ffn_down_s, f), (128, D)))
    tri = asc(np.triu(np.ones((128, 128), BF)))
    trin = asc(np.triu(np.full((128, 128), -30000.0, BF), 1))
    one = np.ones((128, 1), BF)
    idn = np.eye(128, dtype=BF)
    eps = np.full((128, 1), EPS, f)
    onr = np.ones((1, 128), f)
    onrb = np.ones((1, 128), BF)
    shared = dict(wq_d=wq, wk_d=wk, wv_d=wv, wo_d=wo, wg_d=wg, wu_d=wu,
                  wd_d=wd, gwt_d=gwt, gb_d=gb, gs_d=gs, us_d=us, ds_d=ds,
                  tri_d=tri, one_d=one, idn_d=idn, eps_d=eps,
                  onr_d=onr, onrb_d=onrb, trin_d=trin)
    in_maps = []
    for c in range(8):
        b, par = c // 2, c % 2
        perm = ([2 * j + par for j in range(8)]
                + [2 * j + (1 - par) for j in range(8)])
        xl = asc(x[b].reshape(STN, 128, D)[perm])
        ot = (np.ones if par == 1 else np.zeros)((128, 128), BF)
        otn = (np.zeros((128, 128)) if par == 1
               else np.full((128, 128), -30000.0)).astype(BF)
        in_maps.append(dict(shared, x_l=xl, oth_d=asc(ot), othn_d=asc(otn)))
    return in_maps


def kernel(**inputs):
    if "nc" not in _CACHE:
        _CACHE["nc"] = build_program()[0]
    nc = _CACHE["nc"]
    in_maps = _prep(**inputs)
    res = run_bass_kernel_spmd(nc, in_maps, core_ids=list(range(8)))
    y = np.empty((BATCH, SEQ, D), np.float32)
    for c in range(8):
        b, par = c // 2, c % 2
        out = np.asarray(res.results[c]["y_d"])
        yv = y[b].reshape(STN, 128, D)
        for j in range(NTO):
            yv[2 * j + par] = out[j]
    return y


if __name__ == "__main__":
    nc, counts = build_program()
    print("instructions:", nc.n_instructions(), counts)
